# version 02
# baseline (speedup 1.0000x reference)
import numpy as np
import ml_dtypes
import concourse.bass as bass
import concourse.mybir as mybir
from concourse.bass_utils import run_bass_kernel_spmd
from contextlib import ExitStack

F32 = mybir.dt.float32
BF16 = mybir.dt.bfloat16
I32 = mybir.dt.int32
AF = mybir.ActivationFunctionType
ALU = mybir.AluOpType
AX = mybir.AxisListType

E = 2048
T = 2048
TH = 1024
NEC = 16
NH = 32
CH = 256
IMOE = 1408
NKD = 11
NFB = 11
NEXP = 12
IATT = 4096
CD = 6144
LNEPS = 1e-5
RMSEPS = 1e-6
NEGC = -1e30

ENGINES = ("tensor", "vector", "scalar", "gpsimd", "sync")
MK_TRUNC = "full"
MK_P2A = {"tail", "conv", "silu", "store", "dt"}


class KB:
    def __init__(self, nc, es):
        self.nc = nc
        self.es = es
        self.pend = []
        self.cnt = {}
        self.sems = {}
        self._waited = {}
        self.dma_chans = {}

    def ch(self, name):
        if name not in self.sems:
            self.sems[name] = self.es.enter_context(self.nc.semaphore(name))
            self.cnt[name] = 0
        return name

    def val(self, name):
        self.ch(name)
        return self.cnt[name]

    def op(self, eng, fn, waits=(), inc=None, amt=1, incs=()):
        allincs = []
        after = None
        if inc is not None:
            self.ch(inc)
            self.cnt[inc] += amt
            after = self.cnt[inc]
            allincs.append((inc, amt))
        extra = []
        for (c, a) in incs:
            self.ch(c)
            self.cnt[c] += a
            if not allincs:
                allincs.append((c, a))
                after = self.cnt[c]
            else:
                extra.append((c, a, self.cnt[c]))
        ws = []
        for (c, v) in waits:
            self.ch(c)
            if v > 0:
                ws.append((c, v))
        self.pend.append((eng, fn, tuple(ws), tuple(allincs[:1])))
        if extra:
            prim = allincs[0][0]
            pv = self.cnt[prim]
            for (c, a, _v) in extra:
                self.pend.append((eng, (lambda e: None), ((prim, pv),), ((c, a),)))
        return after

    def dma(self, eng, out, in_, inc, waits=(), incs=(), **kw):
        self.ch(inc)
        self.dma_chans[inc] = eng
        return self.op(
            eng,
            (lambda e, out=out, in_=in_, kw=kw: e.dma_start(out=out, in_=in_, **kw)),
            waits, inc, 16, incs)

    def raw(self, eng, fn, waits=(), bump=()):
        for (c, a) in bump:
            self.ch(c)
            self.cnt[c] += a
        self.pend.append((eng, (lambda e, fn=fn: fn(e, self.sems)), tuple(waits), ()))
        return {c: self.cnt[c] for (c, _a) in bump}

    def noop(self, eng, waits=(), inc=None):
        return self.op(eng, (lambda e: None), waits, inc, 1)

    def barrier(self):
        for c, eng in list(self.dma_chans.items()):
            self.noop(eng, waits=[(c, self.cnt[c])])
        self.ch("bar")
        for eng in ENGINES:
            self.noop(eng, inc="bar")
        tot = self.cnt["bar"]
        for eng in ENGINES:
            self.noop(eng, waits=[("bar", tot)])

    def flush(self, block):
        groups = {}
        order = []
        for item in self.pend:
            if item[0] not in groups:
                groups[item[0]] = []
                order.append(item[0])
            groups[item[0]].append(item)
        for eng in order:
            items = groups[eng]

            def body(engine, items=items, eng=eng):
                for (_, fn, waits, incs) in items:
                    for (c, v) in waits:
                        key = (eng, c)
                        if self._waited.get(key, -1) >= v:
                            continue
                        engine.wait_ge(self.sems[c], v)
                        self._waited[key] = v
                    ins = fn(engine)
                    for (c, a) in incs:
                        if ins is None:
                            engine.sem_inc(self.sems[c], a)
                        else:
                            ins = ins.then_inc(self.sems[c], a)
            getattr(block, eng)(body)
        self.pend = []


def _prep_core_inputs(inputs):
    f = np.float32
    ac = np.ascontiguousarray
    W = inputs
    ip = np.asarray(W["L0_attn_in_proj_w"], f)
    ip_re = np.concatenate([ip[4096:10240], ip[10240:10272], ip[0:4096]], axis=0)
    shared = {
        "w_ip": ac(ip_re.T),
        "w_op": ac(np.asarray(W["L0_attn_out_proj_w"], f).T),
        "convw": ac(np.asarray(W["L0_attn_conv_w"], f)[:, 0, :].reshape(48, 128, 4).transpose(1, 0, 2)),
        "convb": ac(np.asarray(W["L0_attn_conv_b"], f).reshape(48, 128).T),
        "dtbias": ac(np.asarray(W["L0_attn_dt_bias"], f).reshape(32, 1)),
        "alog": ac(np.asarray(W["L0_attn_A_log"], f).reshape(32, 1)),
        "dvec": ac(np.asarray(W["L0_attn_D"], f).reshape(1, 32)),
        "nw": ac(np.asarray(W["L0_attn_norm_w"], f).reshape(32, 128).T),
        "wrt9": ac(np.concatenate([np.asarray(W["L0_ffn_router_weight"], f),
                                   np.asarray(W["L0_ffn_shared_gate_w"], f)], axis=0).T),
        "wgu": ac(np.asarray(W["L0_ffn_gate_up_proj"], f).transpose(0, 2, 1)),
        "wdn": ac(np.asarray(W["L0_ffn_down_proj"], f).transpose(0, 2, 1)),
        "wshg": ac(np.asarray(W["L0_ffn_sh_gate_w"], f).T),
        "wshu": ac(np.asarray(W["L0_ffn_sh_up_w"], f).T),
        "wshd": ac(np.asarray(W["L0_ffn_sh_down_w"], f).T),
    }
    ident = np.eye(128, dtype=f)
    sel9 = np.zeros((9, 9 * 128), f)
    for e in range(9):
        sel9[e, e * 128:(e + 1) * 128] = 1.0
    sel32 = np.zeros((32, 32 * 128), f)
    for h in range(32):
        sel32[h, h * 128:(h + 1) * 128] = 1.0
    maskadd = np.zeros((2, 128, 256), f)
    for k in range(2):
        for p in range(128):
            maskadd[k, p, : 128 * k + p] = NEGC
    shared.update(ident=ident, sel9=sel9, sel32=sel32, maskadd=maskadd)

    x = np.asarray(W["L0_attn_x"], f)
    token = np.int32(np.random.randint(1, 2 ** 30))
    maps = []
    for c in range(8):
        bidx, half = c // 2, c % 2
        m = dict(shared)
        m["x_fm"] = ac(x[bidx, half * T:(half + 1) * T, :].T)
        m["parity"] = np.full((1, 1), half, np.int32)
        m["token"] = np.full((1, 1), token, np.int32)
        m["maskcol"] = np.full((128, 1), float(half), f)
        maps.append(m)
    return maps


class Prog:
    def __init__(self, nc, es, stages=("ffn1", "mamba", "ffn2")):
        self.nc = nc
        self.es = es
        self.stages = stages
        self.b = KB(nc, es)
        self.t = {}
        self.uid = 0
        self.block = None

    def u(self, base):
        self.uid += 1
        return f"{base}_{self.uid}"

    def uc(self, base):
        return base

    def declare_dram(self):
        nc = self.nc
        D = {}

        def din(name, shape, dt=F32):
            D[name] = nc.dram_tensor(name, shape, dt, kind="ExternalInput").ap()
        din("x_fm", [E, T]); din("w_ip", [E, 10272]); din("w_op", [IATT, E])
        din("convw", [128, 48, 4]); din("convb", [128, 48]); din("dtbias", [32, 1])
        din("alog", [32, 1]); din("dvec", [1, 32]); din("nw", [128, 32])
        din("wrt9", [E, 9]); din("wgu", [8, E, 2816]); din("wdn", [8, IMOE, E])
        din("wshg", [E, 5632]); din("wshu", [E, 5632]); din("wshd", [5632, E])
        din("ident", [128, 128]); din("sel9", [9, 9 * 128]); din("sel32", [32, 32 * 128])
        din("maskadd", [2, 128, 256]); din("parity", [1, 1], I32); din("token", [1, 1], I32)
        din("maskcol", [128, 1])
        D["out_fm"] = nc.dram_tensor("out_fm", [E, T], F32, kind="ExternalOutput").ap()
        import os as _os
        _dbg = "ExternalOutput" if _os.environ.get("MK_DBG") else "Internal"
        D["r1_d"] = nc.dram_tensor("r1_d", [E, T], F32, kind=_dbg).ap()
        D["r2_d"] = nc.dram_tensor("r2_d", [E, T], F32, kind="Internal").ap()
        D["hbc_d"] = nc.dram_tensor("hbc_d", [CD, T], BF16, kind=_dbg).ap()
        D["y_d"] = nc.dram_tensor("y_d", [IATT, T], BF16, kind=_dbg).ap()
        D["shst"] = nc.dram_tensor("shst", [2, 128, 32, 128], F32, kind="Internal",
                                   addr_space="Shared").ap()
        D["shr1t"] = nc.dram_tensor("shr1t", [2, 128, 16, 3], F32, kind="Internal",
                                    addr_space="Shared").ap()
        D["shflag"] = nc.dram_tensor("shflag", [2, 2, 1, 16], I32, kind="Internal",
                                     addr_space="Shared").ap()
        self.d = D

    def alloc_global(self):
        nc, es, t = self.nc, self.es, self.t

        def sb(n, s, dt):
            t[n] = es.enter_context(nc.sbuf_tensor(n, s, dt))
        sb("identf", [128, 128], F32)
        sb("sel9b", [9, 9 * 128], BF16)
        sb("wrt9s", [128, 16, 9], BF16)
        sb("onesb", [128, 1], BF16)
        sb("ones1f", [1, 128], F32)
        sb("paritys", [1, 1], I32)
        sb("tokens", [1, 1], I32)
        sb("maskcols", [128, 1], F32)
        sb("flagsb", [1, 16], I32)
        sb("tailacc", [128, 16, 3], F32)
        for i in range(8):
            t[f"pb{i}"] = es.enter_context(nc.psum_tensor(f"pb{i}", [128, 512], F32))

    def flush(self):
        self.b.flush(self.block)

    def emit_preamble(self):
        b, d, t = self.b, self.d, self.t
        b.dma("gpsimd", t["sel9b"][:, :], d["sel9"], "cl")
        b.dma("gpsimd", t["wrt9s"][:, :, :], d["wrt9"].rearrange("(c p) n -> p c n", p=128), "cl")
        b.dma("sync", t["identf"][:, :], d["ident"], "cf")
        b.dma("sync", t["paritys"][:, :], d["parity"], "cf")
        b.dma("sync", t["tokens"][:, :], d["token"], "cf")
        b.dma("sync", t["maskcols"][:, :], d["maskcol"], "cf")
        wf = b.val("cf")
        b.op("vector", lambda e: e.memset(t["onesb"][:, :], 1.0), waits=[("cf", wf), ("cl", b.val("cl"))], inc="pre")
        b.op("vector", lambda e: e.memset(t["ones1f"][:, :], 1.0), inc="pre")
        b.op("vector", lambda e: e.memset(t["flagsb"][:, :], -1), inc="pre")
        b.barrier()
        self.flush()

    def chain(self, eng, fn, ch, extra=(), incs=()):
        b = self.b
        return b.op(eng, fn, waits=[(ch, b.val(ch))] + list(extra), inc=ch, incs=incs)

    def poll_flag(self, stage):
        b, t, d = self.b, self.t, self.d
        nc = self.nc
        pol = b.ch(f"pol{stage}")
        pdn = b.ch("poldone")
        fsb = t["flagsb"]

        def fn(e, sems):
            with e.register(self.u("pr")) as pr, e.register(self.u("df")) as df, \
                    e.register(self.u("cn")) as cn, e.register(self.u("tk")) as tk:
                e.reg_load(pr, t["paritys"][0:1, 0:1])
                e.reg_load(tk, t["tokens"][0:1, 0:1])
                e.reg_mov(df, 1)
                e.reg_mov(cn, 0)
                with e.While(df):
                    e.reg_add(cn, cn, 16)
                    for i in nc.Switch(engines=[e], index=[e.compute_val(pr)], n=2):
                        e.dma_start(out=fsb[:, :], in_=d["shflag"][1 - i, stage]).then_inc(sems[pol], 16)
                    e.wait_ge(sems[pol], cn)
                    e.reg_load(df, fsb[0:1, 0:1])
                    e.reg_sub(df, df, tk)
            e.sem_inc(sems[pdn], 1)
        self.b.raw("sync", fn, waits=(), bump=((pdn, 1),))
        return pdn, b.val(pdn)

    def slot_write(self, dst_slot_fn, src_ap, inc, waits):
        b, t = self.b, self.t
        nc = self.nc
        b.ch(inc)
        b.dma_chans[inc] = "sync"

        def fn(e, sems):
            with e.register(self.u("pr")) as pr:
                e.reg_load(pr, t["paritys"][0:1, 0:1])
                for i in nc.Switch(engines=[e], index=[e.compute_val(pr)], n=2):
                    e.dma_start(out=dst_slot_fn(i), in_=src_ap).then_inc(sems[inc], 16)
        self.b.raw("sync", fn, waits=waits, bump=((inc, 16),))
        return b.val(inc)

    def slot_read(self, eng, dst_ap, src_slot_fn, inc, waits):
        b, t = self.b, self.t
        nc = self.nc
        b.ch(inc)
        b.dma_chans[inc] = eng

        def fn(e, sems):
            with e.register(self.u("pr")) as pr:
                e.reg_load(pr, t["paritys"][0:1, 0:1])
                for i in nc.Switch(engines=[e], index=[e.compute_val(pr)], n=2):
                    e.dma_start(out=dst_ap, in_=src_slot_fn(1 - i)).then_inc(sems[inc], 16)
        self.b.raw(eng, fn, waits=waits, bump=((inc, 16),))
        return b.val(inc)

    def emit_ln(self, src_d, half, hbuf, ph, stats_save=None):
        b, t = self.b, self.t
        rt = ph["rtile"]
        sq = ph["sqt"]
        c_r = self.uc("lnr"); c_rf = self.uc("lnrf")
        c_c = self.uc("lnc"); c_sq = self.uc("lnq"); c_qf = self.uc("lnqf")
        c_mm = self.uc("lnm"); c_row = self.uc("lnw"); c_bc = self.uc("lnb"); c_be = self.uc("lne")
        c_h = self.uc("lnh")
        t0 = half * TH
        lastcast = 0
        for ec in range(NEC):
            w = [(c_rf, b.val(c_rf) - 1)] if ec >= 2 else []
            lv = b.dma("sync", rt[ec % 2][:, :], src_d[ec * 128:(ec + 1) * 128, t0:t0 + TH], c_r, waits=w)
            cv = b.op("vector", lambda e, ec=ec: e.tensor_copy(hbuf[:, ec, :], rt[ec % 2][:, :]),
                      waits=[(c_r, lv)], inc=c_c, incs=((c_rf, 1),))
            lastcast = cv
            wq = [(c_qf, b.val(c_qf) - 1)] if ec >= 2 else []
            sv = b.op("scalar", lambda e, ec=ec: e.activation(sq[ec % 2][:, :], hbuf[:, ec, :], AF.Square),
                      waits=[(c_c, cv)] + wq, inc=c_sq)
            for tc in range(2):
                b.op("tensor",
                     lambda e, ec=ec, tc=tc: e.matmul(t[f"pb{tc}"][0:1, 0:512], t["onesb"][:, :],
                                                      hbuf[:, ec, tc * 512:(tc + 1) * 512],
                                                      start=(ec == 0), stop=(ec == NEC - 1)),
                     waits=[(c_c, cv)], inc=(c_mm if ec == NEC - 1 else None))
                b.op("tensor",
                     lambda e, ec=ec, tc=tc: e.matmul(t[f"pb{2 + tc}"][0:1, 0:512], t["onesb"][:, :],
                                                      sq[ec % 2][:, tc * 512:(tc + 1) * 512],
                                                      start=(ec == 0), stop=(ec == NEC - 1)),
                     waits=[(c_sq, sv)], inc=(c_mm if ec == NEC - 1 else None),
                     incs=(((c_qf, 1),) if tc == 1 else ()))
        mmv = b.val(c_mm)
        mu, msq, var = ph["rowa"], ph["rowb"], ph["rowc"]
        for tc in range(2):
            s = slice(tc * 512, (tc + 1) * 512)
            b.op("vector", lambda e, tc=tc, s=s: e.tensor_scalar(mu[0:1, s], t[f"pb{tc}"][0:1, 0:512],
                                                                 1.0 / E, None, ALU.mult),
                 waits=[(c_mm, mmv)], inc=c_row)
            b.op("vector", lambda e, tc=tc, s=s: e.tensor_scalar(msq[0:1, s], t[f"pb{2 + tc}"][0:1, 0:512],
                                                                 1.0 / E, None, ALU.mult), inc=c_row)
        self.chain("vector", lambda e: e.tensor_mul(var[0:1, :], mu[0:1, :], mu[0:1, :]), c_row)
        self.chain("vector", lambda e: e.tensor_sub(var[0:1, :], msq[0:1, :], var[0:1, :]), c_row)
        self.chain("vector", lambda e: e.tensor_scalar(var[0:1, :], var[0:1, :], LNEPS, None, ALU.add), c_row)
        vv = b.val(c_row)
        b.op("scalar", lambda e: e.activation(var[0:1, :], var[0:1, :], AF.Sqrt),
             waits=[(c_row, vv)], inc=c_row)
        self.chain("vector", lambda e: e.reciprocal(var[0:1, :], var[0:1, :]), c_row)
        self.chain("vector", lambda e: e.tensor_mul(mu[0:1, :], mu[0:1, :], var[0:1, :]), c_row)
        if stats_save is not None:
            mus_s, rstd_s = stats_save
            self.chain("vector", lambda e: e.tensor_copy(mus_s, mu[0:1, :]), c_row)
            self.chain("vector", lambda e: e.tensor_copy(rstd_s, var[0:1, :]), c_row)
        rowv = b.val(c_row)
        for tc in range(2):
            s = slice(tc * 512, (tc + 1) * 512)
            b.op("tensor", lambda e, tc=tc, s=s: e.matmul(t[f"pb{4 + tc}"][:, 0:512], t["ones1f"][:, :],
                                                          var[0:1, s], start=True, stop=True),
                 waits=[(c_row, rowv)], inc=c_bc)
            b.op("tensor", lambda e, tc=tc, s=s: e.matmul(t[f"pb{6 + tc}"][:, 0:512], t["ones1f"][:, :],
                                                          mu[0:1, s], start=True, stop=True),
                 waits=[(c_row, rowv)], inc=c_bc)
        bcv = b.val(c_bc)
        rbc, mbc = ph["rstdbc"], ph["musbc"]
        for tc in range(2):
            s = slice(tc * 512, (tc + 1) * 512)
            b.op("vector", lambda e, tc=tc, s=s: e.tensor_copy(rbc[:, s], t[f"pb{4 + tc}"][:, 0:512]),
                 waits=[(c_bc, bcv)], inc=c_be)
            b.op("vector", lambda e, tc=tc, s=s: e.tensor_copy(mbc[:, s], t[f"pb{6 + tc}"][:, 0:512]), inc=c_be)
        bev = b.val(c_be)
        tt = ph["lntmp"]
        for ec in range(NEC):
            b.op("vector", lambda e, ec=ec: e.tensor_mul(tt[:, :], hbuf[:, ec, :], rbc[:, :]),
                 waits=[(c_be, bev), (c_c, lastcast)], inc=c_h)
            b.op("vector", lambda e, ec=ec: e.tensor_sub(hbuf[:, ec, :], tt[:, :], mbc[:, :]), inc=c_h)
        return c_h, b.val(c_h)

    def emit_ffn(self, src_d, dst_d, capture_tail=False):
        nc, b, t, d = self.nc, self.b, self.t, self.d
        pes = ExitStack()

        def sb(n, s, dt):
            return pes.enter_context(nc.sbuf_tensor(self.u(n), s, dt))
        ph = {
            "rtile": [sb("fr", [128, TH], F32) for _ in range(2)],
            "sqt": [sb("fq", [128, TH], BF16) for _ in range(2)],
            "rowa": sb("fra", [1, TH], F32),
            "rowb": sb("frb", [1, TH], F32),
            "rowc": sb("frc", [1, TH], F32),
            "rstdbc": sb("frs", [128, TH], BF16),
            "musbc": sb("fmu", [128, TH], BF16),
            "lntmp": sb("flt", [128, TH], BF16),
        }
        hbuf = sb("fh", [128, NEC, TH], BF16)
        acc = sb("fac", [128, 16, TH], F32)
        gu2 = sb("fg2", [128, NKD, TH], BF16)
        w1 = [sb("fw1", [128, 16, 256], BF16) for _ in range(2)]
        w2 = [sb("fw2", [128, NKD, 128], BF16) for _ in range(2)]
        wbc = sb("fwb", [128, TH], BF16)
        gtmp = [sb("fgt", [128, 512], BF16) for _ in range(2)]
        gut = [sb("fgu", [128, 512], BF16) for _ in range(2)]
        wrowb = sb("fwo", [9, TH], BF16)
        ltok = sb("flk", [128, 9], F32)
        m1c = sb("fm1", [128, 1], F32)
        m2c = sb("fm2", [128, 1], F32)
        dcol = sb("fdc", [128, 1], F32)
        c2c = sb("fc2", [128, 1], F32)
        eq1 = sb("fe1", [128, 9], F32)
        eq2 = sb("fe2", [128, 9], F32)
        mskt = sb("fmk", [128, 9], F32)
        wtok = sb("fwt", [128, 9], F32)

        c_w1 = self.uc("w1"); c_w1f = self.uc("w1f")
        c_w2 = self.uc("w2"); c_w2f = self.uc("w2f")
        c_mm = self.uc("fmm")
        c_ep = self.uc("fep")
        c_ev = self.uc("fev")
        c_gu2 = self.uc("fgu2")
        c_fin = self.uc("ffin")
        c_sto = self.uc("fsto")
        c_rt = self.uc("frt")
        c_tp = self.uc("ftp")
        nw1 = 0
        nw2 = 0
        base1 = b.val(c_w1f)
        base2 = b.val(c_w2f)
        for half in range(2):
            t0 = half * TH
            c_h, hv = self.emit_ln(src_d, half, hbuf, ph)
            zv = b.op("vector", lambda e: e.memset(acc[:, :, :], 0.0), waits=[(c_h, hv)], inc=c_ev)

            tpv = 0
            for tc8 in range(8):
                for k in range(NEC):
                    b.op("tensor",
                         lambda e, tc8=tc8, k=k: e.matmul(
                             t["pb6"][:, 0:9], hbuf[:, k, tc8 * 128:(tc8 + 1) * 128],
                             t["wrt9s"][:, k, :], start=(k == 0), stop=(k == NEC - 1)),
                         waits=([(c_h, hv)] if (tc8 == 0 and k == 0) else []) +
                               ([(c_rt, b.val(c_rt))] if k == 0 else []),
                         inc=(c_mm if k == NEC - 1 else None))
                mv = b.val(c_mm)
                st = self.uc("t2")
                b.op("vector", lambda e: e.tensor_copy(ltok[:, :], t["pb6"][:, 0:9]),
                     waits=[(c_mm, mv)], inc=st)
                b.noop("vector", waits=[(st, b.val(st))], inc=c_rt)
                self.chain("vector", lambda e: e.reduce_max(m1c[:, :], ltok[:, 0:8], axis=AX.X), st)
                self.chain("vector", lambda e: e.tensor_scalar(eq1[:, 0:8], ltok[:, 0:8], m1c[:, 0:1], None, ALU.is_equal), st)
                self.chain("vector", lambda e: e.scalar_tensor_tensor(mskt[:, 0:8], eq1[:, 0:8], NEGC, ltok[:, 0:8], ALU.mult, ALU.add), st)
                self.chain("vector", lambda e: e.reduce_max(m2c[:, :], mskt[:, 0:8], axis=AX.X), st)
                self.chain("vector", lambda e: e.tensor_sub(dcol[:, :], m1c[:, :], m2c[:, :]), st)
                sv = b.op("scalar", lambda e: e.activation(dcol[:, :], dcol[:, :], AF.Sigmoid),
                          waits=[(st, b.val(st))], inc=st)
                b.op("scalar", lambda e: e.activation(wtok[:, 8:9], ltok[:, 8:9], AF.Sigmoid), inc=st)
                self.chain("vector", lambda e: e.tensor_scalar(c2c[:, :], dcol[:, :], -1.0, 1.0, ALU.mult, ALU.add), st)
                self.chain("vector", lambda e: e.tensor_scalar(eq1[:, 0:8], eq1[:, 0:8], dcol[:, 0:1], None, ALU.mult), st)
                self.chain("vector", lambda e: e.tensor_scalar(eq2[:, 0:8], mskt[:, 0:8], m2c[:, 0:1], None, ALU.is_equal), st)
                self.chain("vector", lambda e: e.tensor_scalar(eq2[:, 0:8], eq2[:, 0:8], c2c[:, 0:1], None, ALU.mult), st)
                self.chain("vector", lambda e: e.tensor_add(wtok[:, 0:8], eq1[:, 0:8], eq2[:, 0:8]), st)
                wv = b.val(st)
                tv = b.op("tensor", lambda e: e.transpose(t["pb7"][0:9, 0:128], wtok[:, :], t["identf"][:, :]),
                          waits=[(st, wv), (c_tp, tpv)], inc=c_mm)
                tpv = b.op("vector",
                           lambda e, tc8=tc8: e.tensor_copy(wrowb[:, tc8 * 128:(tc8 + 1) * 128], t["pb7"][0:9, 0:128]),
                           waits=[(c_mm, tv)], inc=c_tp)
            wrv = tpv

            dn_rel = [zv, zv]
            g_rel = [0, 0]
            u_rel = [0, 0]
            gt_rel = [0, 0]
            wb_rel = 0
            for ex in range(NEXP):
                e9 = ex if ex < 8 else 8
                wbv = []
                for tc in range(2):
                    v = b.op("tensor",
                             lambda e, tc=tc, e9=e9: e.matmul(
                                 t["pb6"][:, 0:512], t["sel9b"][:, e9 * 128:(e9 + 1) * 128],
                                 wrowb[:, tc * 512:(tc + 1) * 512], start=True, stop=True),
                             waits=[(c_tp, wrv), (c_ev, wb_rel), (c_rt, b.val(c_rt))], inc=c_mm)
                    wb_rel = b.op("vector",
                                  lambda e, tc=tc: e.tensor_copy(wbc[:, tc * 512:(tc + 1) * 512], t["pb6"][:, 0:512]),
                                  waits=[(c_mm, v)], inc=c_ev)
                    wbv.append(wb_rel)

                if ex < 8:
                    gsrc = lambda fb, ex=ex: d["wgu"][ex][:, fb * 128:(fb + 1) * 128]
                    usrc = lambda fb, ex=ex: d["wgu"][ex][:, 1408 + fb * 128:1408 + (fb + 1) * 128]
                else:
                    q = ex - 8
                    gsrc = lambda fb, q=q: d["wshg"][:, q * 1408 + fb * 128:q * 1408 + (fb + 1) * 128]
                    usrc = lambda fb, q=q: d["wshu"][:, q * 1408 + fb * 128:q * 1408 + (fb + 1) * 128]
                guv = 0
                for fb in range(NFB):
                    s = nw1 % 2
                    wfree = [(c_w1f, base1 + nw1 - 1)] if nw1 >= 2 else []
                    lv1 = b.dma("gpsimd", w1[s][:, :, 0:128],
                                gsrc(fb).rearrange("(c p) f -> p c f", p=128), c_w1, waits=wfree)
                    lv2 = b.dma("gpsimd", w1[s][:, :, 128:256],
                                usrc(fb).rearrange("(c p) f -> p c f", p=128), c_w1)
                    nw1 += 1
                    mmg = [0, 0]
                    mmu = [0, 0]
                    for k in range(NEC):
                        for tc in range(2):
                            mmg[tc] = b.op("tensor",
                                 lambda e, s=s, k=k, tc=tc: e.matmul(
                                     t[f"pb{tc}"][:, 0:512], w1[s][:, k, 0:128],
                                     hbuf[:, k, tc * 512:(tc + 1) * 512],
                                     start=(k == 0), stop=(k == NEC - 1)),
                                 waits=([(c_w1, lv2), (c_ep, g_rel[tc]), (c_h, hv)] if k == 0 else []),
                                 inc=(c_mm if k == NEC - 1 else None))
                    for k in range(NEC):
                        for tc in range(2):
                            mmu[tc] = b.op("tensor",
                                 lambda e, s=s, k=k, tc=tc: e.matmul(
                                     t[f"pb{2 + tc}"][:, 0:512], w1[s][:, k, 128:256],
                                     hbuf[:, k, tc * 512:(tc + 1) * 512],
                                     start=(k == 0), stop=(k == NEC - 1)),
                                 waits=([(c_ev, u_rel[tc])] if k == 0 else []),
                                 inc=(c_mm if k == NEC - 1 else None),
                                 incs=(((c_w1f, 1),) if (k == NEC - 1 and tc == 1) else ()))
                    for tc in range(2):
                        g_rel[tc] = b.op("scalar",
                             lambda e, tc=tc: e.activation(gtmp[tc][:, :], t[f"pb{tc}"][:, 0:512], AF.Silu),
                             waits=[(c_mm, mmg[tc]), (c_ev, gt_rel[tc])], inc=c_ep)
                        u_rel[tc] = b.op("vector",
                             lambda e, tc=tc: e.tensor_mul(gut[tc][:, :], gtmp[tc][:, :], t[f"pb{2 + tc}"][:, 0:512]),
                             waits=[(c_ep, g_rel[tc]), (c_mm, mmu[tc])], inc=c_ev)
                        gt_rel[tc] = u_rel[tc]
                        guv = b.op("vector",
                             lambda e, fb=fb, tc=tc: e.tensor_mul(gu2[:, fb, tc * 512:(tc + 1) * 512],
                                                                  gut[tc][:, :], wbc[:, tc * 512:(tc + 1) * 512]),
                             waits=[(c_ev, wbv[tc])], inc=c_gu2)

                if ex < 8:
                    dsrc = lambda hb, ex=ex: d["wdn"][ex][:, hb * 128:(hb + 1) * 128]
                else:
                    q = ex - 8
                    dsrc = lambda hb, q=q: d["wshd"][q * 1408:(q + 1) * 1408, hb * 128:(hb + 1) * 128]
                for hb in range(16):
                    s = nw2 % 2
                    wfree = [(c_w2f, base2 + nw2 - 1)] if nw2 >= 2 else []
                    lv = b.dma("gpsimd", w2[s][:, :, :],
                               dsrc(hb).rearrange("(c p) f -> p c f", p=128), c_w2, waits=wfree)
                    nw2 += 1
                    for tc in range(2):
                        mmd = 0
                        for k in range(NKD):
                            mmd = b.op("tensor",
                                 lambda e, s=s, k=k, tc=tc: e.matmul(
                                     t[f"pb{4 + tc}"][:, 0:512], w2[s][:, k, :],
                                     gu2[:, k, tc * 512:(tc + 1) * 512],
                                     start=(k == 0), stop=(k == NKD - 1)),
                                 waits=([(c_w2, lv), (c_gu2, guv), (c_ev, dn_rel[tc])] if k == 0 else []),
                                 inc=(c_mm if k == NKD - 1 else None),
                                 incs=(((c_w2f, 1),) if (k == NKD - 1 and tc == 1) else ()))
                        dn_rel[tc] = b.op("vector",
                             lambda e, hb=hb, tc=tc: e.tensor_add(acc[:, hb, tc * 512:(tc + 1) * 512],
                                                                  acc[:, hb, tc * 512:(tc + 1) * 512],
                                                                  t[f"pb{4 + tc}"][:, 0:512]),
                             waits=[(c_mm, mmd)], inc=c_ev)

            c_fr = self.uc("ffr")
            c_frf = self.uc("ffrf")
            for hb in range(16):
                w = [(c_frf, b.val(c_frf) - 1)] if hb >= 2 else []
                w += [(c_sto, b.val(c_sto) - 16)] if hb >= 2 else []
                lv = b.dma("sync", ph["rtile"][hb % 2][:, :],
                           src_d[hb * 128:(hb + 1) * 128, t0:t0 + TH], c_fr, waits=w)
                fv = b.op("vector",
                          lambda e, hb=hb: e.scalar_tensor_tensor(ph["rtile"][hb % 2][:, :], acc[:, hb, :], 0.5,
                                                                  ph["rtile"][hb % 2][:, :], ALU.mult, ALU.add),
                          waits=[(c_fr, lv), (c_ev, dn_rel[1])], inc=c_fin,
                          incs=((c_frf, 1),))
                if capture_tail and half == 1:
                    fv = b.op("vector",
                              lambda e, hb=hb: e.tensor_copy(t["tailacc"][:, hb, :], ph["rtile"][hb % 2][:, 1021:1024]),
                              waits=[(c_fin, fv)], inc=c_fin)
                b.dma("sync", dst_d[hb * 128:(hb + 1) * 128, t0:t0 + TH], ph["rtile"][hb % 2][:, :], c_sto,
                      waits=[(c_fin, fv)])
        if capture_tail:
            stv = self.slot_write(lambda i: d["shr1t"][i], t["tailacc"][:, :, :], "sht",
                                  waits=[(c_fin, b.val(c_fin))])
            self.slot_write(lambda i: d["shflag"][i, 0][0:1, 0:1], t["tokens"][0:1, 0:1], "shf0",
                            waits=[("sht", stv)])
        b.noop("sync", waits=[(c_sto, b.val(c_sto))])
        self.b.barrier()
        self.flush()
        pes.close()



    def emit_mamba(self, src_d, dst_d):
        nc, b, t, d = self.nc, self.b, self.t, self.d
        pm = ExitStack()

        def sm(n, s_, dt):
            return pm.enter_context(nc.sbuf_tensor(self.u(n), s_, dt))
        rep32b = sm("rep32b", [32, 32 * 128], BF16)
        convws = sm("convws", [128, 48, 4], F32)
        convbs = sm("convbs", [128, 48], F32)
        dtbiass = sm("dtbiass", [32, 1], F32)
        acol = sm("acol", [32, 1], F32)
        dbc = sm("dbc", [128, 32], F32)
        nws = sm("nws", [128, 32], F32)
        h3s = sm("h3s", [128, 16, 3], BF16)
        r1ts = sm("r1ts", [128, 16, 3], F32)
        sq3 = sm("sq3", [128, 3], BF16)
        dts = sm("dts", [32, 8, 256], F32)
        eacumb = sm("eacumb", [32, 8, 256], BF16)
        lambc = sm("lambc", [128, 32, 8], F32)
        sinb = sm("sinb", [128, 32, 128], BF16)
        row3 = sm("row3", [1, 12], F32)

        b.dma("gpsimd", rep32b[:, :], d["sel32"], "mcl")
        b.dma("sync", convws[:, :, :], d["convw"], "mcf")
        b.dma("sync", convbs[:, :], d["convb"], "mcf")
        b.dma("sync", dtbiass[:, :], d["dtbias"], "mcf")
        b.dma("sync", acol[:, :], d["alog"], "mcf")
        b.dma("sync", nws[:, :], d["nw"], "mcf")
        b.dma("sync", dbc[0:1, :], d["dvec"], "mcf")
        wf = b.val("mcf")
        b.op("scalar", lambda e: e.activation(acol[:, :], acol[:, :], AF.Exp),
             waits=[("mcf", wf)], inc="mpre")
        b.op("vector", lambda e: e.tensor_scalar(acol[:, :], acol[:, :], -1.0, None, ALU.mult),
             waits=[("mpre", b.val("mpre"))], inc="mpre")
        b.op("tensor",
             lambda e: e.matmul(t["pb7"][:, 0:32], t["ones1f"][:, :], dbc[0:1, :], start=True, stop=True),
             waits=[("mcf", wf), ("mpre", b.val("mpre"))], inc="mpre")
        b.op("vector", lambda e: e.tensor_copy(dbc[:, :], t["pb7"][:, 0:32]),
             waits=[("mpre", b.val("mpre"))], inc="mpre")
        b.barrier()

        if MK_TRUNC == "consts":
            self.flush(); pm.close(); return
        pdn, pdv = self.poll_flag(0)
        rv = self.slot_read("sync", r1ts[:, :, :], lambda i: d["shr1t"][i], "mr1t",
                            waits=[(pdn, pdv)])
        c3 = self.uc("ln3")
        cast3 = b.op("vector", lambda e: e.tensor_copy(h3s[:, :, :], r1ts[:, :, :]),
                     waits=[("mr1t", rv)], inc=c3)
        for k in range(NEC):
            sv = b.op("scalar", lambda e, k=k: e.activation(sq3[:, :], h3s[:, k, :], AF.Square),
                      waits=[(c3, b.val(c3))], inc=c3)
            b.op("tensor", lambda e, k=k: e.matmul(t["pb0"][0:1, 0:3], t["onesb"][:, :], h3s[:, k, :],
                                                   start=(k == 0), stop=(k == NEC - 1)),
                 waits=[(c3, cast3)], inc=c3)
            b.op("tensor", lambda e, k=k: e.matmul(t["pb1"][0:1, 0:3], t["onesb"][:, :], sq3[:, :],
                                                   start=(k == 0), stop=(k == NEC - 1)),
                 waits=[(c3, sv)], inc=c3)
        self.chain("vector", lambda e: e.tensor_scalar(row3[0:1, 0:3], t["pb0"][0:1, 0:3], 1.0 / E, None, ALU.mult), c3)
        self.chain("vector", lambda e: e.tensor_scalar(row3[0:1, 4:7], t["pb1"][0:1, 0:3], 1.0 / E, None, ALU.mult), c3)
        self.chain("vector", lambda e: e.tensor_mul(row3[0:1, 8:11], row3[0:1, 0:3], row3[0:1, 0:3]), c3)
        self.chain("vector", lambda e: e.tensor_sub(row3[0:1, 8:11], row3[0:1, 4:7], row3[0:1, 8:11]), c3)
        self.chain("vector", lambda e: e.tensor_scalar(row3[0:1, 8:11], row3[0:1, 8:11], LNEPS, None, ALU.add), c3)
        sv = b.op("scalar", lambda e: e.activation(row3[0:1, 8:11], row3[0:1, 8:11], AF.Sqrt),
                  waits=[(c3, b.val(c3))], inc=c3)
        self.chain("vector", lambda e: e.reciprocal(row3[0:1, 8:11], row3[0:1, 8:11]), c3)
        self.chain("vector", lambda e: e.tensor_mul(row3[0:1, 0:3], row3[0:1, 0:3], row3[0:1, 8:11]), c3)
        rw3 = b.val(c3)
        b.op("tensor", lambda e: e.matmul(t["pb2"][:, 0:3], t["ones1f"][:, :], row3[0:1, 8:11],
                                          start=True, stop=True), waits=[(c3, rw3)], inc=c3)
        b.op("tensor", lambda e: e.matmul(t["pb3"][:, 0:3], t["ones1f"][:, :], row3[0:1, 0:3],
                                          start=True, stop=True), waits=[(c3, rw3)], inc=c3)
        bc3 = b.val(c3)
        for k in range(NEC):
            b.op("vector", lambda e, k=k: e.tensor_mul(h3s[:, k, :], h3s[:, k, :], t["pb2"][:, 0:3]),
                 waits=[(c3, bc3)], inc=c3)
            b.op("vector", lambda e, k=k: e.tensor_sub(h3s[:, k, :], h3s[:, k, :], t["pb3"][:, 0:3]),
                 inc=c3)
        h3v = b.val(c3)
        if MK_TRUNC == "ln3":
            b.barrier(); self.flush(); pm.close(); return

        pa = ExitStack()

        def sa(n, s_, dt):
            return pa.enter_context(nc.sbuf_tensor(self.u(n), s_, dt))
        ph = {
            "rtile": [sa("mr", [128, TH], F32) for _ in range(2)],
            "sqt": [sa("mq", [128, TH], BF16) for _ in range(2)],
            "rowa": sa("mra", [1, TH], F32),
            "rowb": sa("mrb", [1, TH], F32),
            "rowc": sa("mrc", [1, TH], F32),
            "rstdbc": sa("mrs", [128, TH], BF16),
            "musbc": sa("mmu", [128, TH], BF16),
            "lntmp": sa("mlt", [128, TH], BF16),
        }
        hbuf = sa("mh", [128, NEC, TH], BF16)
        w1 = [sa("mw1", [128, 16, 256], BF16) for _ in range(2)]
        wdt = sa("mwdt", [128, 16, 32], BF16)
        pre = [sa("mpre", [128, 3 + TH], BF16) for _ in range(2)]
        cacc = [sa("mca", [128, TH], F32) for _ in range(2)]
        post = [sa("mpo", [128, TH], BF16) for _ in range(2)]
        tailb = sa("mtb", [128, 48, 3], BF16)
        dtv1 = sa("mdv", [32, 512], F32)
        dta = sa("mda", [32, 512], F32)

        c_w1 = self.uc("mi1"); c_w1f = self.uc("mif")
        c_ev = self.uc("mie")
        c_mm = self.uc("mim")
        c_sil = self.uc("mis")
        c_cv = self.uc("micv")
        c_st = self.uc("mist")
        c_tev = self.uc("mtev")
        c_dt = self.uc("mdt")
        ldv = b.dma("gpsimd", wdt[:, :, :], d["w_ip"][:, 6144:6176].rearrange("(c p) f -> p c f", p=128), c_w1)
        nip = 0
        prel = {0: 0, 1: 0, 2: 0, 3: 0}
        t4rel = 0
        silrel = [0, 0]
        strel = [0, 0]
        for half in range(2):
            t0 = half * TH
            c_h, hv = self.emit_ln(src_d, half, hbuf, ph)
            for fb in range(24):
                s = nip % 2
                wfree = [(c_w1f, nip - 1)] if nip >= 2 else []
                lv = b.dma("gpsimd", w1[s][:, :, :],
                           d["w_ip"][:, fb * 256:(fb + 1) * 256].rearrange("(c p) f -> p c f", p=128),
                           c_w1, waits=wfree)
                nip += 1
                for fc2 in range(2):
                    fc = fb * 2 + fc2
                    mmv = [0, 0]
                    for k in range(NEC):
                        for tc in range(2):
                            bank = fc2 * 2 + tc
                            mmv[tc] = b.op("tensor",
                                 lambda e, s=s, k=k, tc=tc, fc2=fc2, bank=bank: e.matmul(
                                     t[f"pb{bank}"][:, 0:512], w1[s][:, k, fc2 * 128:(fc2 + 1) * 128],
                                     hbuf[:, k, tc * 512:(tc + 1) * 512],
                                     start=(k == 0), stop=(k == NEC - 1)),
                                 waits=([(c_w1, lv), (c_h, hv), (c_ev, prel[bank])] if k == 0 else []),
                                 inc=(c_mm if k == NEC - 1 else None),
                                 incs=(((c_w1f, 1),) if (k == NEC - 1 and tc == 1 and fc2 == 1) else ()))
                    if half == 0 and "tail" in MK_P2A:
                        tmv = 0
                        for k in range(NEC):
                            tmv = b.op("tensor",
                                 lambda e, s=s, k=k, fc2=fc2: e.matmul(
                                     t["pb4"][:, 0:3], w1[s][:, k, fc2 * 128:(fc2 + 1) * 128],
                                     h3s[:, k, :], start=(k == 0), stop=(k == NEC - 1)),
                                 waits=([(c3, h3v), (c_tev, t4rel)] if k == 0 else []),
                                 inc=(c_mm if k == NEC - 1 else None))
                        t4rel = b.op("vector",
                             lambda e, s=s: e.tensor_scalar(pre[s][:, 0:3], t["pb4"][:, 0:3],
                                                            t["maskcols"][:, 0:1], None, ALU.mult),
                             waits=[(c_mm, tmv)], inc=c_tev)
                    elif "tail" in MK_P2A:
                        b.op("vector", lambda e, s=s, fc=fc: e.tensor_copy(pre[s][:, 0:3], tailb[:, fc, :]),
                             inc=c_tev)
                    else:
                        b.op("vector", lambda e, s=s: e.memset(pre[s][:, 0:3], 0.0), inc=c_tev)
                    for tc in range(2):
                        bank = fc2 * 2 + tc
                        prel[bank] = b.op("vector",
                             lambda e, s=s, tc=tc, bank=bank: e.tensor_copy(
                                 pre[s][:, 3 + tc * 512: 3 + (tc + 1) * 512], t[f"pb{bank}"][:, 0:512]),
                             waits=[(c_mm, mmv[tc])], inc=c_ev)
                    ca = fc % 2
                    if "conv" in MK_P2A:
                        b.op("vector", lambda e, s=s, ca=ca, fc=fc: e.tensor_scalar(
                                 cacc[ca][:, :], pre[s][:, 0:TH], convws[:, fc, 0:1], None, ALU.mult),
                             waits=[(c_sil, silrel[ca])], inc=c_cv)
                        for j in (1, 2, 3):
                            b.op("vector", lambda e, s=s, ca=ca, fc=fc, j=j: e.scalar_tensor_tensor(
                                     cacc[ca][:, :], pre[s][:, j:j + TH], convws[:, fc, j:j + 1],
                                     cacc[ca][:, :], ALU.mult, ALU.add), inc=c_cv)
                        if half == 0:
                            b.op("vector", lambda e, s=s, fc=fc: e.tensor_copy(tailb[:, fc, :], pre[s][:, TH:TH + 3]),
                                 inc=c_cv)
                    cvv = b.val(c_cv)
                    if "silu" in MK_P2A:
                        silrel[ca] = b.op("scalar",
                             lambda e, ca=ca, fc=fc: e.activation(post[ca][:, :], cacc[ca][:, :], AF.Silu,
                                                                  bias=convbs[:, fc:fc + 1]),
                             waits=[(c_cv, cvv), (c_st, strel[ca])], inc=c_sil)
                    if "store" in MK_P2A:
                        strel[ca] = b.dma("sync", d["hbc_d"][fc * 128:(fc + 1) * 128, t0:t0 + TH], post[ca][:, :],
                                          c_st, waits=[(c_sil, silrel[ca])])
            dtrel = 0
            for tc in (range(2) if ("dt" in MK_P2A or "dtmm" in MK_P2A) else []):
                mmv = 0
                for k in range(NEC):
                    mmv = b.op("tensor",
                         lambda e, k=k, tc=tc: e.matmul(t["pb5"][0:32, 0:512], wdt[:, k, :],
                                                        hbuf[:, k, tc * 512:(tc + 1) * 512],
                                                        start=(k == 0), stop=(k == NEC - 1)),
                         waits=([(c_w1, ldv), (c_h, hv), (c_dt, dtrel)] if k == 0 else []),
                         inc=(c_mm if k == NEC - 1 else None))
                if "dt" not in MK_P2A:
                    dtrel = b.op("vector", lambda e: e.tensor_copy(dtv1[:, :], t["pb5"][0:32, 0:512]),
                                 waits=[(c_mm, mmv)], inc=c_dt)
                    continue
                import os as _os
                lvl = int(_os.environ.get("MK_DTLVL", "9"))
                tsv = b.op("vector", lambda e: e.tensor_scalar(dtv1[:, :], t["pb5"][0:32, 0:512],
                                                               dtbiass[:, 0:1], 0.0, ALU.add, ALU.max),
                           waits=[(c_mm, mmv)], inc=c_dt)
                av = b.op("scalar", lambda e: e.activation(dta[:, :], t["pb5"][0:32, 0:512], AF.Abs,
                                                           bias=dtbiass[:, 0:1]),
                          waits=[(c_dt, tsv)], inc=c_dt)
                av = b.op("scalar", lambda e: e.activation(dta[:, :], dta[:, :], AF.Exp, scale=-1.0),
                          waits=[(c_dt, av)], inc=c_dt)
                av = b.op("scalar", lambda e: e.activation(dta[:, :], dta[:, :], AF.Ln, bias=1.0),
                          waits=[(c_dt, av)], inc=c_dt)
                for j in range(2):
                    dtrel = b.op("vector",
                         lambda e, half=half, tc=tc, j=j: e.tensor_add(
                             dts[:, half * 4 + tc * 2 + j, :],
                             dtv1[:, j * 256:(j + 1) * 256], dta[:, j * 256:(j + 1) * 256]),
                         waits=[(c_dt, b.val(c_dt))], inc=c_dt)
        b.barrier()
        self.flush()
        pa.close()
        if MK_TRUNC == "p2a":
            pm.close(); return

        pc = ExitStack()

        def sc(n, s_, dt):
            return pc.enter_context(nc.sbuf_tensor(self.u(n), s_, dt))
        sel32f = sc("sel32f", [32, 32 * 128], F32)
        maskadds = sc("maskadds", [128, 2, 256], F32)
        arow = sc("arow", [32, 8, 256], F32)
        acum = sc("acum", [32, 8, 256], F32)
        nacum = sc("nacum", [32, 8, 256], F32)
        dtb = sc("dtb", [32, 8, 256], BF16)
        alast = sc("alast", [32, 8], F32)
        ealast = sc("ealast", [32, 8], F32)
        lamx = sc("lamx", [32, 8], F32)
        acumT = sc("acumT", [128, 16, 32], F32)
        nacumT = sc("nacumT", [128, 16, 32], F32)
        ealastbc = sc("ealastbc", [128, 32, 8], F32)
        b.dma("sync", sel32f[:, :], d["sel32"], "mcf")
        b.dma("sync", maskadds[:, :, :], d["maskadd"].rearrange("k p l -> p k l"), "mcf")
        cB = self.uc("p2b")
        wcf = b.val("mcf")
        b.op("vector", lambda e: e.tensor_scalar(arow[:, :, :], dts[:, :, :], acol[:, 0:1], None, ALU.mult),
             waits=[(cB, 0)], inc=cB)
        b.op("vector", lambda e: e.tensor_copy(dtb[:, :, :], dts[:, :, :]), inc=cB)
        for c in range(8):
            self.chain("vector", lambda e, c=c: e.tensor_tensor_scan(
                acum[:, c, :], arow[:, c, :], arow[:, c, :], 0.0, ALU.add, ALU.bypass), cB)
        self.chain("vector", lambda e: e.tensor_scalar(nacum[:, :, :], acum[:, :, :], -1.0, None, ALU.mult), cB)
        eav = b.op("scalar", lambda e: e.activation(eacumb[:, :, :], acum[:, :, :], AF.Exp),
                   waits=[(cB, b.val(cB))], inc=cB)
        self.chain("vector", lambda e: e.tensor_copy(alast[:, :], acum[:, :, 255]), cB)
        alv = b.op("scalar", lambda e: e.activation(ealast[:, :], alast[:, :], AF.Exp),
                   waits=[(cB, b.val(cB))], inc=cB)
        self.chain("vector", lambda e: e.tensor_tensor_scan(lamx[:, :], alast[:, :], alast[:, :],
                                                            0.0, ALU.add, ALU.bypass), cB)
        self.chain("vector", lambda e: e.tensor_sub(lamx[:, :], lamx[:, :], alast[:, :]), cB)
        lmv = b.op("scalar", lambda e: e.activation(lamx[:, :], lamx[:, :], AF.Exp),
                   waits=[(cB, b.val(cB))], inc=cB)
        for i in range(16):
            c = i // 2
            s0 = (i % 2) * 128
            tv = b.op("tensor", lambda e, c=c, s0=s0: e.transpose(
                     t["pb6"][:, 0:32], acum[:, c, s0:s0 + 128],
                     t["identf"][0:32, 0:32]),
                 waits=[(cB, b.val(cB))], inc=cB)
            self.chain("vector", lambda e, i=i: e.tensor_copy(acumT[:, i, :], t["pb6"][:, 0:32]), cB)
        self.chain("vector", lambda e: e.tensor_scalar(nacumT[:, :, :], acumT[:, :, :], -1.0, None, ALU.mult), cB)
        for h in range(NH):
            tv = b.op("tensor", lambda e, h=h: e.matmul(t["pb0"][:, 0:8], sel32f[:, h * 128:(h + 1) * 128],
                                                        ealast[:, :], start=True, stop=True),
                      waits=[(cB, b.val(cB)), ("mcf", wcf), (cB, alv)], inc=cB)
            self.chain("vector", lambda e, h=h: e.tensor_copy(ealastbc[:, h, :], t["pb0"][:, 0:8]), cB)
            tv = b.op("tensor", lambda e, h=h: e.matmul(t["pb1"][:, 0:8], sel32f[:, h * 128:(h + 1) * 128],
                                                        lamx[:, :], start=True, stop=True),
                      waits=[(cB, b.val(cB)), (cB, lmv)], inc=cB)
            self.chain("vector", lambda e, h=h: e.tensor_copy(lambc[:, h, :], t["pb1"][:, 0:8]), cB)
        b.barrier()
        if MK_TRUNC == "p2b":
            self.flush(); pc.close(); pm.close(); return

        XT = [sc("XT", [128, 2, 4096], BF16) for _ in range(2)]
        XDD = sc("XDD", [128, 2, 4096], BF16)
        BT = [sc("BT", [128, 2, 1024], BF16) for _ in range(2)]
        BF_ = [sc("BFm", [128, 8, 256], BF16) for _ in range(2)]
        CF = [sc("CFm", [128, 8, 256], BF16) for _ in range(2)]
        Ssb = sc("Ssb", [128, 32, 128], F32)
        SPb = sc("SPb", [128, 128], BF16)
        LTE = sc("LTE", [128, 2, 256], F32)
        MTb = sc("MTb", [128, 2, 256], BF16)
        Cscb = sc("Cscb", [128, 256], BF16)
        decf = sc("decf", [32, 256], F32)
        decb = sc("decb", [32, 256], BF16)
        yev = [sc("yev", [128, 256], BF16) for _ in range(2)]

        c_xl = self.uc("sxl")
        c_smm = self.uc("smm")
        c_sdv = self.uc("sdv")
        c_sac = self.uc("sac")
        c_yst = self.uc("syst")
        zmv = b.op("vector", lambda e: e.memset(Ssb[:, :, :], 0.0), inc=c_sdv)
        xf = [0, 0]
        bcf = [0, 0]
        loads = {}
        for c in range(9):
            if c < 8:
                sl = c % 2
                cs = c * CH
                w = [(c_smm, xf[sl]), (c_sdv, bcf[sl])]
                l1 = b.dma("scalar", XT[sl][:, 0, :], d["hbc_d"][0:IATT, cs:cs + 128], c_xl,
                           waits=w, transpose=True)
                l2 = b.dma("scalar", XT[sl][:, 1, :], d["hbc_d"][0:IATT, cs + 128:cs + 256], c_xl,
                           transpose=True)
                l3 = b.dma("scalar", BT[sl][:, 0, :], d["hbc_d"][IATT:IATT + 1024, cs:cs + 128], c_xl,
                           transpose=True)
                l4 = b.dma("scalar", BT[sl][:, 1, :], d["hbc_d"][IATT:IATT + 1024, cs + 128:cs + 256], c_xl,
                           transpose=True)
                l5 = b.dma("scalar", BF_[sl][:, :, :],
                           d["hbc_d"][IATT:IATT + 1024, cs:cs + 256].rearrange("(g p) s -> p g s", p=128), c_xl)
                l6 = b.dma("scalar", CF[sl][:, :, :],
                           d["hbc_d"][IATT + 1024:CD, cs:cs + 256].rearrange("(g p) s -> p g s", p=128), c_xl)
                loads[c] = b.val(c_xl)
            if c == 0:
                continue
            cc = c - 1
            sl2 = cc % 2
            ldv2 = loads[cc]
            dxv = 0
            for s2 in range(2):
                for n8 in range(8):
                    bank = "pb0" if (n8 % 2 == 0) else "pb7"
                    mv = b.op("tensor", lambda e, cc=cc, s2=s2, n8=n8, bank=bank: e.matmul(
                             t[bank][:, 0:512], dtb[:, cc, s2 * 128:(s2 + 1) * 128],
                             rep32b[:, n8 * 512:(n8 + 1) * 512], start=True, stop=True),
                         waits=[(c_xl, ldv2), (c_sdv, b.val(c_sdv) - 1 if (s2 * 8 + n8) >= 2 else 0)],
                         inc=c_smm)
                    dxv = b.op("vector", lambda e, sl2=sl2, s2=s2, n8=n8, bank=bank: e.tensor_mul(
                             XT[sl2][:, s2, n8 * 512:(n8 + 1) * 512],
                             XT[sl2][:, s2, n8 * 512:(n8 + 1) * 512], t[bank][:, 0:512]),
                         waits=[(c_smm, mv)], inc=c_sdv)
            dv = self.chain("vector", lambda e, cc=cc: e.tensor_scalar(
                     decf[:, :], nacum[:, cc, :], alast[:, cc:cc + 1], None, ALU.add), c_sdv)
            dv = b.op("scalar", lambda e: e.activation(decb[:, :], decf[:, :], AF.Exp),
                      waits=[(c_sdv, dv), (c_smm, b.val(c_smm))], inc=c_sac)
            ddv = 0
            xddgate = b.val(c_smm)
            for s2 in range(2):
                for n8 in range(8):
                    bank = "pb0" if (n8 % 2 == 0) else "pb7"
                    mv = b.op("tensor", lambda e, s2=s2, n8=n8, bank=bank: e.matmul(
                             t[bank][:, 0:512], decb[:, s2 * 128:(s2 + 1) * 128],
                             rep32b[:, n8 * 512:(n8 + 1) * 512], start=True, stop=True),
                         waits=[(c_sac, dv), (c_sdv, b.val(c_sdv) - 1 if (s2 * 8 + n8) >= 2 else 0)],
                         inc=c_smm)
                    ddv = b.op("vector", lambda e, sl2=sl2, s2=s2, n8=n8, bank=bank: e.tensor_mul(
                             XDD[:, s2, n8 * 512:(n8 + 1) * 512],
                             XT[sl2][:, s2, n8 * 512:(n8 + 1) * 512], t[bank][:, 0:512]),
                         waits=[(c_smm, mv), (c_smm, xddgate)], inc=c_sdv)
            rel = {"pb1": 0, "pb2": 0, "pb3": 0, "pb4": 0, "pb5": 0, "pb6": 0}
            supd = {}
            lastymm = 0
            for h in range(NH):
                g = h % 8
                rb = b.op("tensor", lambda e, h=h, cc=cc: e.matmul(
                         t["pb5"][:, 0:256], sel32f[:, h * 128:(h + 1) * 128], acum[:, cc, :],
                         start=True, stop=True),
                     waits=[(c_sdv, rel["pb5"])], inc=c_smm)
                m1 = [0, 0]
                for s2 in range(2):
                    m1[s2] = b.op("tensor", lambda e, h=h, g=g, s2=s2, sl2=sl2: e.matmul(
                             t[f"pb{1 + s2}"][:, 0:256], BF_[sl2][:, g, s2 * 128:(s2 + 1) * 128],
                             CF[sl2][:, g, :], start=True, stop=True),
                         waits=[(c_sdv, rel[f"pb{1 + s2}"]), (c_xl, ldv2)], inc=c_smm)
                cb = b.op("tensor", lambda e, h=h, cc=cc: e.matmul(
                         t["pb6"][:, 0:256], rep32b[:, h * 128:(h + 1) * 128], eacumb[:, cc, :],
                         start=True, stop=True),
                     waits=[(c_sdv, rel["pb6"])], inc=c_smm)
                sm = 0
                for s2 in range(2):
                    sm = b.op("tensor", lambda e, h=h, g=g, s2=s2, sl2=sl2: e.matmul(
                             t["pb4"][:, 0:128], BT[sl2][:, s2, g * 128:(g + 1) * 128],
                             XDD[:, s2, h * 128:(h + 1) * 128],
                             start=(s2 == 0), stop=(s2 == 1)),
                         waits=([(c_sdv, rel["pb4"]), (c_sdv, ddv)] if s2 == 0 else []), inc=(c_smm if s2 == 1 else None))
                mt = 0
                for s2 in range(2):
                    lt = b.op("vector", lambda e, h=h, cc=cc, s2=s2: e.scalar_tensor_tensor(
                             LTE[:, s2, :], t["pb5"][:, 0:256], nacumT[:, cc * 2 + s2, h:h + 1],
                             maskadds[:, s2, :], ALU.add, ALU.add),
                         waits=[(c_smm, rb)], inc=c_sdv)
                    if s2 == 1:
                        rel["pb5"] = lt
                    ex = b.op("scalar", lambda e, s2=s2: e.activation(LTE[:, s2, :], LTE[:, s2, :], AF.Exp),
                              waits=[(c_sdv, lt)], inc=c_sac)
                    mt = b.op("vector", lambda e, s2=s2: e.tensor_mul(MTb[:, s2, :], LTE[:, s2, :],
                                                                     t[f"pb{1 + s2}"][:, 0:256]),
                         waits=[(c_sac, ex), (c_smm, m1[s2]), (c_smm, lastymm)], inc=c_sdv)
                    rel[f"pb{1 + s2}"] = mt
                csc = b.op("vector", lambda e, g=g, sl2=sl2: e.tensor_mul(Cscb[:, :], CF[sl2][:, g, :],
                                                                          t["pb6"][:, 0:256]),
                     waits=[(c_smm, cb), (c_smm, lastymm)], inc=c_sdv)
                rel["pb6"] = csc
                spv = b.op("vector", lambda e, h=h: e.tensor_copy(SPb[:, :], Ssb[:, h, :]),
                     waits=[(c_sdv, supd.get(h, zmv)), (c_smm, lastymm)], inc=c_sdv)
                ym = 0
                for s2 in range(2):
                    ym = b.op("tensor", lambda e, h=h, s2=s2, sl2=sl2: e.matmul(
                             t["pb3"][:, 0:256], XT[sl2][:, s2, h * 128:(h + 1) * 128], MTb[:, s2, :],
                             start=(s2 == 0), stop=False),
                         waits=([(c_sac, rel["pb3"]), (c_sdv, mt), (c_sdv, dxv)] if s2 == 0 else []),
                         inc=None)
                ym = b.op("tensor", lambda e, h=h: e.matmul(
                         t["pb3"][:, 0:256], SPb[:, :], Cscb[:, :], start=False, stop=True),
                     waits=[(c_sdv, csc), (c_sdv, spv)], inc=c_smm)
                lastymm = ym
                if h == NH - 1:
                    xf[sl2] = ym
                sup = b.op("vector", lambda e, h=h, cc=cc: e.scalar_tensor_tensor(
                         Ssb[:, h, :], Ssb[:, h, :], ealastbc[:, h, cc:cc + 1], t["pb4"][:, 0:128],
                         ALU.mult, ALU.add),
                     waits=[(c_smm, sm), (c_sdv, spv)], inc=c_sdv)
                supd[h] = sup
                rel["pb4"] = sup
                if h == NH - 1:
                    bcf[sl2] = sup
                ya = b.op("scalar", lambda e, h=h: e.activation(yev[h % 2][:, :], t["pb3"][:, 0:256], AF.Copy),
                     waits=[(c_smm, ym), (c_yst, b.val(c_yst) - 16 if h >= 2 else 0)], inc=c_sac)
                rel["pb3"] = ya
                b.dma("sync", d["y_d"][h * 128:(h + 1) * 128, cc * CH:(cc + 1) * CH], yev[h % 2][:, :],
                      c_yst, waits=[(c_sac, ya)])
        sv = self.slot_write(lambda i: d["shst"][i], Ssb[:, :, :], "mshs",
                             waits=[(c_sdv, b.val(c_sdv))])
        self.slot_write(lambda i: d["shflag"][i, 1][0:1, 0:1], t["tokens"][0:1, 0:1], "mshf",
                        waits=[("mshs", sv)])
        if MK_TRUNC == "p2c":
            b.barrier(); self.flush(); pc.close(); pm.close(); return
        pdn2, pdv2 = self.poll_flag(1)
        rv2 = self.slot_read("gpsimd", sinb[:, :, :], lambda i: d["shst"][i], "msin",
                             waits=[(pdn2, pdv2)])
        c_si = self.uc("msk")
        b.op("vector", lambda e: e.tensor_scalar(sinb[:, :, :], sinb[:, :, :], t["maskcols"][:, 0:1],
                                                 None, ALU.mult),
             waits=[("msin", rv2)], inc=c_si)
        b.barrier()
        self.flush()
        pc.close()
        if MK_TRUNC == "p2d":
            pm.close(); return

        p3 = ExitStack()

        def s3(n, s_, dt):
            return p3.enter_context(nc.sbuf_tensor(self.u(n), s_, dt))
        ph3 = {
            "rtile": [s3("3r", [128, TH], F32) for _ in range(2)],
            "sqt": [s3("3q", [128, TH], BF16) for _ in range(2)],
            "rowa": s3("3ra", [1, TH], F32),
            "rowb": s3("3rb", [1, TH], F32),
            "rowc": s3("3rc", [1, TH], F32),
            "rstdbc": s3("3rs", [128, TH], BF16),
            "musbc": s3("3mu", [128, TH], BF16),
            "lntmp": s3("3lt", [128, TH], BF16),
        }
        hbuf3 = s3("3h", [128, NEC, TH], BF16)
        gbuf = s3("3g", [128, 32, TH], BF16)
        zw = [s3("3zw", [128, 16, 128], BF16) for _ in range(2)]
        wo = [s3("3wo", [128, 32, 128], BF16) for _ in range(2)]
        hs2 = [s3("3hs", [128, TH], BF16) for _ in range(2)]
        ytl = [s3("3yt", [128, TH], BF16) for _ in range(2)]
        ctl = [s3("3ca", [128, TH], BF16) for _ in range(2)]
        silz = s3("3sz", [128, 256], BF16)
        sqg = [s3("3sq", [128, 512], BF16) for _ in range(2)]
        yt2 = s3("3y2", [128, 256], F32)
        SPc = s3("3sp", [128, 128], BF16)
        Cs3 = s3("3cs", [128, 256], BF16)
        rrow = s3("3rr", [1, 512], F32)
        rbc3 = [s3("3rb2", [128, 512], F32) for _ in range(2)]

        c_zl = self.uc("z3l"); c_zlf = self.uc("z3f")
        c_hl = self.uc("h3l")
        c_yl = self.uc("y3l")
        c_3m = self.uc("m3m")
        c_3v = self.uc("m3v")
        c_3a = self.uc("m3a")
        c_gn = self.uc("m3g")
        c_ol = self.uc("o3l"); c_olf = self.uc("o3f")
        c_or = self.uc("o3r")
        c_ost = self.uc("o3s")
        nzl = 0
        nol = 0
        for half in range(2):
            t0 = half * TH
            c_h3, h3vv = self.emit_ln(src_d, half, hbuf3, ph3)

            relz = 0
            relc = 0
            rel1 = 0
            rstat = 0
            rbcr = [0, 0]
            for grp in range(8):
                for hh in range(4):
                    h = grp * 4 + hh
                    g = h % 8
                    s = nzl % 2
                    wz = [(c_zlf, nzl - 1)] if nzl >= 2 else []
                    zlv = b.dma("gpsimd", zw[s][:, :, :],
                                d["w_ip"][:, 6176 + h * 128:6176 + (h + 1) * 128].rearrange("(c p) f -> p c f", p=128),
                                c_zl, waits=wz)
                    nzl += 1
                    hlv = b.dma("sync", hs2[s][:, :], d["hbc_d"][h * 128:(h + 1) * 128, t0:t0 + TH], c_hl,
                                waits=([(c_3v, b.val(c_3v) - 2)] if nzl >= 3 else []))
                    calv = b.dma("sync", ctl[s][:, :],
                                 d["hbc_d"][IATT + 1024 + g * 128:IATT + 1024 + (g + 1) * 128, t0:t0 + TH],
                                 c_hl, waits=([(c_3v, b.val(c_3v) - 2)] if nzl >= 3 else []))
                    ylv = b.dma("sync", ytl[s][:, :], d["y_d"][h * 128:(h + 1) * 128, t0:t0 + TH], c_yl,
                                waits=([(c_3v, b.val(c_3v) - 2)] if nzl >= 3 else []))
                    for c2 in range(4):
                        cglob = half * 4 + c2
                        zm = 0
                        for k in range(NEC):
                            zm = b.op("tensor", lambda e, s=s, k=k, c2=c2: e.matmul(
                                     t["pb0"][:, 0:256], zw[s][:, k, :],
                                     hbuf3[:, k, c2 * 256:(c2 + 1) * 256],
                                     start=(k == 0), stop=(k == NEC - 1)),
                                 waits=([(c_zl, zlv), (c_h3, h3vv), (c_3a, relz)] if k == 0 else []),
                                 inc=(c_3m if k == NEC - 1 else None),
                                 incs=(((c_zlf, 1),) if (k == NEC - 1 and c2 == 3) else ()))
                        ebm = b.op("tensor", lambda e, h=h, cglob=cglob: e.matmul(
                                 t["pb6"][:, 0:256], rep32b[:, h * 128:(h + 1) * 128], eacumb[:, cglob, :],
                                 start=True, stop=True),
                             waits=[(c_3v, relc)], inc=c_3m)
                        cs3 = b.op("vector", lambda e, s=s, c2=c2: e.tensor_mul(
                                 Cs3[:, :], ctl[s][:, c2 * 256:(c2 + 1) * 256], t["pb6"][:, 0:256]),
                             waits=[(c_3m, ebm), (c_hl, calv)], inc=c_3v)
                        relc = cs3
                        spv = b.op("vector", lambda e, h=h, cglob=cglob: e.tensor_scalar(
                                 SPc[:, :], sinb[:, h, :], lambc[:, h, cglob:cglob + 1], None, ALU.mult),
                             waits=[(c_si, 1)], inc=c_3v)
                        ym = b.op("tensor", lambda e: e.matmul(t["pb1"][:, 0:256], SPc[:, :], Cs3[:, :],
                                                               start=True, stop=True),
                             waits=[(c_3v, spv), (c_3v, cs3), (c_3v, rel1)],
                             inc=c_3m)
                        sz = b.op("scalar", lambda e: e.activation(silz[:, :], t["pb0"][:, 0:256], AF.Silu),
                             waits=[(c_3m, zm)], inc=c_3a)
                        relz = sz
                        rel1 = b.op("vector", lambda e, c2=c2, s=s: e.tensor_add(yt2[:, :],
                                                                          ytl[s][:, c2 * 256:(c2 + 1) * 256],
                                                                          t["pb1"][:, 0:256]),
                             waits=[(c_3m, ym), (c_yl, ylv)], inc=c_3v)
                        y2 = b.op("vector", lambda e, h=h, c2=c2, s=s: e.scalar_tensor_tensor(
                                 yt2[:, :], hs2[s][:, c2 * 256:(c2 + 1) * 256], dbc[:, h:h + 1],
                                 yt2[:, :], ALU.mult, ALU.add),
                             waits=[(c_hl, hlv)], inc=c_3v)
                        gv = b.op("vector", lambda e, h=h, c2=c2: e.tensor_mul(
                                 gbuf[:, h, c2 * 256:(c2 + 1) * 256], yt2[:, :], silz[:, :]),
                             waits=[(c_3a, sz)], inc=c_3v)
                for tc in range(2):
                    smv = 0
                    for hh in range(4):
                        h = grp * 4 + hh
                        qv = b.op("scalar", lambda e, h=h, tc=tc, hh=hh: e.activation(
                                 sqg[hh % 2][:, :], gbuf[:, h, tc * 512:(tc + 1) * 512], AF.Square),
                             waits=[(c_3v, b.val(c_3v))], inc=c_3a)
                        smv = b.op("tensor", lambda e, hh=hh: e.matmul(
                                 t["pb2"][0:1, 0:512], t["onesb"][:, :], sqg[hh % 2][:, :],
                                 start=(hh == 0), stop=(hh == 3)),
                             waits=[(c_3a, qv), (c_3v, rstat)], inc=(c_3m if hh == 3 else None))
                    rv3 = b.op("vector", lambda e: e.tensor_scalar(rrow[0:1, :], t["pb2"][0:1, 0:512],
                                                                   1.0 / 512, RMSEPS, ALU.mult, ALU.add),
                         waits=[(c_3m, smv)], inc=c_3v)
                    rstat = rv3
                    rv3 = b.op("scalar", lambda e: e.activation(rrow[0:1, :], rrow[0:1, :], AF.Sqrt),
                         waits=[(c_3v, rv3)], inc=c_3a)
                    rv3 = b.op("vector", lambda e: e.reciprocal(rrow[0:1, :], rrow[0:1, :]),
                         waits=[(c_3a, rv3)], inc=c_3v)
                    bm = b.op("tensor", lambda e: e.matmul(t["pb3"][:, 0:512], t["ones1f"][:, :], rrow[0:1, :],
                                                           start=True, stop=True),
                         waits=[(c_3v, rv3), (c_3v, rbcr[tc])], inc=c_3m)
                    rbv = b.op("vector", lambda e, tc=tc: e.tensor_copy(rbc3[tc][:, :], t["pb3"][:, 0:512]),
                         waits=[(c_3m, bm)], inc=c_3v)
                    for hh in range(4):
                        h = grp * 4 + hh
                        rbcr[tc] = b.op("vector", lambda e, h=h, tc=tc: e.scalar_tensor_tensor(
                                 gbuf[:, h, tc * 512:(tc + 1) * 512], gbuf[:, h, tc * 512:(tc + 1) * 512],
                                 nws[:, h:h + 1], rbc3[tc][:, :], ALU.mult, ALU.mult),
                             waits=[(c_3v, rbv)], inc=c_3v, incs=(((c_gn, 1),) if hh == 3 and tc == 1 else ()))
            gnv = b.val(c_gn)
            for fb in range(16):
                s = nol % 2
                wz = [(c_olf, nol - 1)] if nol >= 2 else []
                olv = b.dma("gpsimd", wo[s][:, :, :],
                            d["w_op"][:, fb * 128:(fb + 1) * 128].rearrange("(c p) f -> p c f", p=128),
                            c_ol, waits=wz)
                nol += 1
                rlv = b.dma("sync", ph3["rtile"][s][:, :], src_d[fb * 128:(fb + 1) * 128, t0:t0 + TH],
                            c_hl, waits=([(c_ost, b.val(c_ost) - 16)] if fb >= 2 else []))
                adv = 0
                for tc in range(2):
                    om = 0
                    for k in range(32):
                        om = b.op("tensor", lambda e, s=s, k=k, tc=tc: e.matmul(
                                 t[f"pb{4 + tc}"][:, 0:512], wo[s][:, k, :],
                                 gbuf[:, k, tc * 512:(tc + 1) * 512],
                                 start=(k == 0), stop=(k == 31)),
                             waits=([(c_ol, olv), (c_gn, gnv), (c_or, b.val(c_or) - 1 if fb >= 1 else 0)] if k == 0 else []),
                             inc=(c_3m if k == 31 else None),
                             incs=(((c_olf, 1),) if (k == 31 and tc == 1) else ()))
                    adv = b.op("vector", lambda e, s=s, tc=tc: e.tensor_add(
                             ph3["rtile"][s][:, tc * 512:(tc + 1) * 512],
                             ph3["rtile"][s][:, tc * 512:(tc + 1) * 512], t[f"pb{4 + tc}"][:, 0:512]),
                         waits=[(c_3m, om), (c_hl, rlv)], inc=c_or)
                b.dma("sync", dst_d[fb * 128:(fb + 1) * 128, t0:t0 + TH], ph3["rtile"][s][:, :], c_ost,
                      waits=[(c_or, adv)])
        b.noop("sync", waits=[(c_ost, b.val(c_ost))])
        b.barrier()
        self.flush()
        p3.close()
        pm.close()



def build_nc(stages=("ffn1", "mamba", "ffn2")):
    nc = bass.Bass("TRN2", target_bir_lowering=False, debug=False)
    with ExitStack() as es:
        p = Prog(nc, es, stages)
        p.declare_dram()
        p.alloc_global()
        p.block = es.enter_context(nc.Block())
        p.emit_preamble()
        if "ffn1" in stages:
            dst = p.d["r1_d"] if ("mamba" in stages or "ffn2" in stages) else p.d["out_fm"]
            p.emit_ffn(p.d["x_fm"], dst, capture_tail=("mamba" in stages))
        if "mamba" in stages:
            dst = p.d["r2_d"] if "ffn2" in stages else p.d["out_fm"]
            p.emit_mamba(p.d["r1_d"], dst)
        if "ffn2" in stages:
            src = p.d["r2_d"] if "mamba" in stages else (p.d["r1_d"] if "ffn1" in stages else p.d["x_fm"])
            p.emit_ffn(src, p.d["out_fm"], capture_tail=False)
    return nc


def kernel(**inputs):
    maps = _prep_core_inputs(inputs)
    nc = build_nc()
    res = run_bass_kernel_spmd(nc, maps, core_ids=list(range(8)))
    out = np.empty((4, 4096, 2048), np.float32)
    for c in range(8):
        bidx, half = c // 2, c % 2
        out[bidx, half * T:(half + 1) * T, :] = res.results[c]["out_fm"].T
    return out


# version 03
# speedup vs baseline: 1.0717x; 1.0717x over previous
import numpy as np
import ml_dtypes
import concourse.bass as bass
import concourse.mybir as mybir
import concourse.bass_utils as _BU
from concourse.bass_utils import run_bass_kernel_spmd

if not getattr(_BU, "_mk_ldwopt_patched", False):
    _BU._mk_ldwopt_patched = True
    _orig_run_command = _BU.run_command

    def _mk_run_command(cmd, *a, **kw):
        cmd = ["--enable-ldw-opt=true" if c == "--enable-ldw-opt=false" else c for c in cmd]
        return _orig_run_command(cmd, *a, **kw)
    _BU.run_command = _mk_run_command
from contextlib import ExitStack

F32 = mybir.dt.float32
BF16 = mybir.dt.bfloat16
I32 = mybir.dt.int32
AF = mybir.ActivationFunctionType
ALU = mybir.AluOpType
AX = mybir.AxisListType

E = 2048
T = 2048
TH = 1024
NEC = 16
NH = 32
CH = 256
IMOE = 1408
NKD = 11
NFB = 11
NEXP = 12
IATT = 4096
CD = 6144
LNEPS = 1e-5
RMSEPS = 1e-6
NEGC = -1e30

ENGINES = ("tensor", "vector", "scalar", "gpsimd", "sync")
MK_TRUNC = "full"
MK_P2A = {"tail", "conv", "silu", "store", "dt"}


class KB:
    def __init__(self, nc, es):
        self.nc = nc
        self.es = es
        self.pend = []
        self.cnt = {}
        self.sems = {}
        self._waited = {}
        self.dma_chans = {}

    def ch(self, name):
        if name not in self.sems:
            self.sems[name] = self.es.enter_context(self.nc.semaphore(name))
            self.cnt[name] = 0
        return name

    def val(self, name):
        self.ch(name)
        return self.cnt[name]

    def op(self, eng, fn, waits=(), inc=None, amt=1, incs=()):
        allincs = []
        after = None
        if inc is not None:
            self.ch(inc)
            self.cnt[inc] += amt
            after = self.cnt[inc]
            allincs.append((inc, amt))
        extra = []
        for (c, a) in incs:
            self.ch(c)
            self.cnt[c] += a
            if not allincs:
                allincs.append((c, a))
                after = self.cnt[c]
            else:
                extra.append((c, a, self.cnt[c]))
        ws = []
        for (c, v) in waits:
            self.ch(c)
            if v > 0:
                ws.append((c, v))
        self.pend.append((eng, fn, tuple(ws), tuple(allincs[:1])))
        if extra:
            prim = allincs[0][0]
            pv = self.cnt[prim]
            for (c, a, _v) in extra:
                self.pend.append((eng, (lambda e: None), ((prim, pv),), ((c, a),)))
        return after

    def dma(self, eng, out, in_, inc, waits=(), incs=(), **kw):
        self.ch(inc)
        self.dma_chans[inc] = eng
        return self.op(
            eng,
            (lambda e, out=out, in_=in_, kw=kw: e.dma_start(out=out, in_=in_, **kw)),
            waits, inc, 16, incs)

    def raw(self, eng, fn, waits=(), bump=()):
        for (c, a) in bump:
            self.ch(c)
            self.cnt[c] += a
        self.pend.append((eng, (lambda e, fn=fn: fn(e, self.sems)), tuple(waits), ()))
        return {c: self.cnt[c] for (c, _a) in bump}

    def noop(self, eng, waits=(), inc=None):
        return self.op(eng, (lambda e: None), waits, inc, 1)

    def barrier(self):
        for c, eng in list(self.dma_chans.items()):
            self.noop(eng, waits=[(c, self.cnt[c])])
        self.ch("bar")
        for eng in ENGINES:
            self.noop(eng, inc="bar")
        tot = self.cnt["bar"]
        for eng in ENGINES:
            self.noop(eng, waits=[("bar", tot)])

    def flush(self, block):
        groups = {}
        order = []
        for item in self.pend:
            if item[0] not in groups:
                groups[item[0]] = []
                order.append(item[0])
            groups[item[0]].append(item)
        for eng in order:
            items = groups[eng]

            def body(engine, items=items, eng=eng):
                for (_, fn, waits, incs) in items:
                    for (c, v) in waits:
                        key = (eng, c)
                        if self._waited.get(key, -1) >= v:
                            continue
                        engine.wait_ge(self.sems[c], v)
                        self._waited[key] = v
                    ins = fn(engine)
                    for (c, a) in incs:
                        if ins is None:
                            engine.sem_inc(self.sems[c], a)
                        else:
                            ins = ins.then_inc(self.sems[c], a)
            getattr(block, eng)(body)
        self.pend = []


def _prep_core_inputs(inputs):
    f = np.float32
    ac = np.ascontiguousarray
    W = inputs
    ip = np.asarray(W["L0_attn_in_proj_w"], f)
    ip_re = np.concatenate([ip[4096:10240], ip[10240:10272], ip[0:4096]], axis=0)
    shared = {
        "w_ip": ac(ip_re.T),
        "w_op": ac(np.asarray(W["L0_attn_out_proj_w"], f).T),
        "convw": ac(np.asarray(W["L0_attn_conv_w"], f)[:, 0, :].reshape(48, 128, 4).transpose(1, 0, 2)),
        "convb": ac(np.asarray(W["L0_attn_conv_b"], f).reshape(48, 128).T),
        "dtbias": ac(np.asarray(W["L0_attn_dt_bias"], f).reshape(32, 1)),
        "alog": ac(np.asarray(W["L0_attn_A_log"], f).reshape(32, 1)),
        "dvec": ac(np.asarray(W["L0_attn_D"], f).reshape(1, 32)),
        "nw": ac(np.asarray(W["L0_attn_norm_w"], f).reshape(32, 128).T),
        "wrt9": ac(np.concatenate([np.asarray(W["L0_ffn_router_weight"], f),
                                   np.asarray(W["L0_ffn_shared_gate_w"], f)], axis=0).T),
        "wgu": ac(np.asarray(W["L0_ffn_gate_up_proj"], f).transpose(0, 2, 1)),
        "wdn": ac(np.asarray(W["L0_ffn_down_proj"], f).transpose(0, 2, 1)),
        "wshg": ac(np.asarray(W["L0_ffn_sh_gate_w"], f).T),
        "wshu": ac(np.asarray(W["L0_ffn_sh_up_w"], f).T),
        "wshd": ac(np.asarray(W["L0_ffn_sh_down_w"], f).T),
    }
    ident = np.eye(128, dtype=f)
    sel9 = np.zeros((9, 9 * 128), f)
    for e in range(9):
        sel9[e, e * 128:(e + 1) * 128] = 1.0
    sel32 = np.zeros((32, 32 * 128), f)
    for h in range(32):
        sel32[h, h * 128:(h + 1) * 128] = 1.0
    maskadd = np.zeros((2, 128, 256), f)
    for k in range(2):
        for p in range(128):
            maskadd[k, p, : 128 * k + p] = NEGC
    shared.update(ident=ident, sel9=sel9, sel32=sel32, maskadd=maskadd)

    x = np.asarray(W["L0_attn_x"], f)
    token = np.int32(np.random.randint(1, 2 ** 30))
    maps = []
    for c in range(8):
        bidx, half = c // 2, c % 2
        m = dict(shared)
        m["x_fm"] = ac(x[bidx, half * T:(half + 1) * T, :].T)
        m["parity"] = np.full((1, 1), half, np.int32)
        m["token"] = np.full((1, 1), token, np.int32)
        m["maskcol"] = np.full((128, 1), float(half), f)
        maps.append(m)
    return maps


class Prog:
    def __init__(self, nc, es, stages=("ffn1", "mamba", "ffn2")):
        self.nc = nc
        self.es = es
        self.stages = stages
        self.b = KB(nc, es)
        self.t = {}
        self.uid = 0
        self.block = None

    def u(self, base):
        self.uid += 1
        return f"{base}_{self.uid}"

    def uc(self, base):
        return base

    def declare_dram(self):
        nc = self.nc
        D = {}

        def din(name, shape, dt=F32):
            D[name] = nc.dram_tensor(name, shape, dt, kind="ExternalInput").ap()
        din("x_fm", [E, T]); din("w_ip", [E, 10272]); din("w_op", [IATT, E])
        din("convw", [128, 48, 4]); din("convb", [128, 48]); din("dtbias", [32, 1])
        din("alog", [32, 1]); din("dvec", [1, 32]); din("nw", [128, 32])
        din("wrt9", [E, 9]); din("wgu", [8, E, 2816]); din("wdn", [8, IMOE, E])
        din("wshg", [E, 5632]); din("wshu", [E, 5632]); din("wshd", [5632, E])
        din("ident", [128, 128]); din("sel9", [9, 9 * 128]); din("sel32", [32, 32 * 128])
        din("maskadd", [2, 128, 256]); din("parity", [1, 1], I32); din("token", [1, 1], I32)
        din("maskcol", [128, 1])
        D["out_fm"] = nc.dram_tensor("out_fm", [E, T], F32, kind="ExternalOutput").ap()
        import os as _os
        _dbg = "ExternalOutput" if _os.environ.get("MK_DBG") else "Internal"
        D["r1_d"] = nc.dram_tensor("r1_d", [E, T], F32, kind=_dbg).ap()
        D["r2_d"] = nc.dram_tensor("r2_d", [E, T], F32, kind="Internal").ap()
        D["hbc_d"] = nc.dram_tensor("hbc_d", [CD, T], BF16, kind=_dbg).ap()
        D["y_d"] = nc.dram_tensor("y_d", [IATT, T], BF16, kind=_dbg).ap()
        D["shst"] = nc.dram_tensor("shst", [2, 128, 32, 128], F32, kind="Internal",
                                   addr_space="Shared").ap()
        D["shr1t"] = nc.dram_tensor("shr1t", [2, 128, 16, 3], F32, kind="Internal",
                                    addr_space="Shared").ap()
        D["shflag"] = nc.dram_tensor("shflag", [2, 2, 1, 16], I32, kind="Internal",
                                     addr_space="Shared").ap()
        self.d = D

    def alloc_global(self):
        nc, es, t = self.nc, self.es, self.t

        def sb(n, s, dt):
            t[n] = es.enter_context(nc.sbuf_tensor(n, s, dt))
        sb("identf", [128, 128], F32)
        sb("sel9b", [9, 9 * 128], BF16)
        sb("wrt9s", [128, 16, 9], BF16)
        sb("onesb", [128, 1], BF16)
        sb("ones1f", [1, 128], F32)
        sb("paritys", [1, 1], I32)
        sb("tokens", [1, 1], I32)
        sb("maskcols", [128, 1], F32)
        sb("flagsb", [1, 16], I32)
        sb("tailacc", [128, 16, 3], F32)
        for i in range(8):
            t[f"pb{i}"] = es.enter_context(nc.psum_tensor(f"pb{i}", [128, 512], F32))

    def flush(self):
        self.b.flush(self.block)

    def emit_preamble(self):
        b, d, t = self.b, self.d, self.t
        b.dma("gpsimd", t["sel9b"][:, :], d["sel9"], "cl")
        b.dma("gpsimd", t["wrt9s"][:, :, :], d["wrt9"].rearrange("(c p) n -> p c n", p=128), "cl")
        b.dma("sync", t["identf"][:, :], d["ident"], "cf")
        b.dma("sync", t["paritys"][:, :], d["parity"], "cf")
        b.dma("sync", t["tokens"][:, :], d["token"], "cf")
        b.dma("sync", t["maskcols"][:, :], d["maskcol"], "cf")
        wf = b.val("cf")
        b.op("vector", lambda e: e.memset(t["onesb"][:, :], 1.0), waits=[("cf", wf), ("cl", b.val("cl"))], inc="pre")
        b.op("vector", lambda e: e.memset(t["ones1f"][:, :], 1.0), inc="pre")
        b.op("vector", lambda e: e.memset(t["flagsb"][:, :], -1), inc="pre")
        b.barrier()
        self.flush()

    def chain(self, eng, fn, ch, extra=(), incs=()):
        b = self.b
        return b.op(eng, fn, waits=[(ch, b.val(ch))] + list(extra), inc=ch, incs=incs)

    def poll_flag(self, stage):
        b, t, d = self.b, self.t, self.d
        nc = self.nc
        pol = b.ch(f"pol{stage}")
        pdn = b.ch("poldone")
        fsb = t["flagsb"]

        def fn(e, sems):
            with e.register(self.u("pr")) as pr, e.register(self.u("df")) as df, \
                    e.register(self.u("cn")) as cn, e.register(self.u("tk")) as tk:
                e.reg_load(pr, t["paritys"][0:1, 0:1])
                e.reg_load(tk, t["tokens"][0:1, 0:1])
                e.reg_mov(df, 1)
                e.reg_mov(cn, 0)
                with e.While(df):
                    e.reg_add(cn, cn, 16)
                    for i in nc.Switch(engines=[e], index=[e.compute_val(pr)], n=2):
                        e.dma_start(out=fsb[:, :], in_=d["shflag"][1 - i, stage]).then_inc(sems[pol], 16)
                    e.wait_ge(sems[pol], cn)
                    e.reg_load(df, fsb[0:1, 0:1])
                    e.reg_sub(df, df, tk)
            e.sem_inc(sems[pdn], 1)
        self.b.raw("sync", fn, waits=(), bump=((pdn, 1),))
        return pdn, b.val(pdn)

    def slot_write(self, dst_slot_fn, src_ap, inc, waits):
        b, t = self.b, self.t
        nc = self.nc
        b.ch(inc)
        b.dma_chans[inc] = "sync"

        def fn(e, sems):
            with e.register(self.u("pr")) as pr:
                e.reg_load(pr, t["paritys"][0:1, 0:1])
                for i in nc.Switch(engines=[e], index=[e.compute_val(pr)], n=2):
                    e.dma_start(out=dst_slot_fn(i), in_=src_ap).then_inc(sems[inc], 16)
        self.b.raw("sync", fn, waits=waits, bump=((inc, 16),))
        return b.val(inc)

    def slot_read(self, eng, dst_ap, src_slot_fn, inc, waits):
        b, t = self.b, self.t
        nc = self.nc
        b.ch(inc)
        b.dma_chans[inc] = eng

        def fn(e, sems):
            with e.register(self.u("pr")) as pr:
                e.reg_load(pr, t["paritys"][0:1, 0:1])
                for i in nc.Switch(engines=[e], index=[e.compute_val(pr)], n=2):
                    e.dma_start(out=dst_ap, in_=src_slot_fn(1 - i)).then_inc(sems[inc], 16)
        self.b.raw(eng, fn, waits=waits, bump=((inc, 16),))
        return b.val(inc)

    def emit_ln(self, src_d, half, hbuf, ph, stats_save=None):
        b, t = self.b, self.t
        rt = ph["rtile"]
        sq = ph["sqt"]
        c_r = self.uc("lnr"); c_rf = self.uc("lnrf")
        c_c = self.uc("lnc"); c_sq = self.uc("lnq"); c_qf = self.uc("lnqf")
        c_mm = self.uc("lnm"); c_row = self.uc("lnw"); c_bc = self.uc("lnb"); c_be = self.uc("lne")
        c_h = self.uc("lnh")
        t0 = half * TH
        lastcast = 0
        for ec in range(NEC):
            w = [(c_rf, b.val(c_rf) - 1)] if ec >= 2 else []
            lv = b.dma("sync", rt[ec % 2][:, :], src_d[ec * 128:(ec + 1) * 128, t0:t0 + TH], c_r, waits=w)
            cv = b.op("vector", lambda e, ec=ec: e.tensor_copy(hbuf[:, ec, :], rt[ec % 2][:, :]),
                      waits=[(c_r, lv)], inc=c_c, incs=((c_rf, 1),))
            lastcast = cv
            wq = [(c_qf, b.val(c_qf) - 1)] if ec >= 2 else []
            sv = b.op("scalar", lambda e, ec=ec: e.activation(sq[ec % 2][:, :], hbuf[:, ec, :], AF.Square),
                      waits=[(c_c, cv)] + wq, inc=c_sq)
            for tc in range(2):
                b.op("tensor",
                     lambda e, ec=ec, tc=tc: e.matmul(t[f"pb{tc}"][0:1, 0:512], t["onesb"][:, :],
                                                      hbuf[:, ec, tc * 512:(tc + 1) * 512],
                                                      start=(ec == 0), stop=(ec == NEC - 1)),
                     waits=[(c_c, cv)], inc=(c_mm if ec == NEC - 1 else None))
                b.op("tensor",
                     lambda e, ec=ec, tc=tc: e.matmul(t[f"pb{2 + tc}"][0:1, 0:512], t["onesb"][:, :],
                                                      sq[ec % 2][:, tc * 512:(tc + 1) * 512],
                                                      start=(ec == 0), stop=(ec == NEC - 1)),
                     waits=[(c_sq, sv)], inc=(c_mm if ec == NEC - 1 else None),
                     incs=(((c_qf, 1),) if tc == 1 else ()))
        mmv = b.val(c_mm)
        mu, msq, var = ph["rowa"], ph["rowb"], ph["rowc"]
        for tc in range(2):
            s = slice(tc * 512, (tc + 1) * 512)
            b.op("vector", lambda e, tc=tc, s=s: e.tensor_scalar(mu[0:1, s], t[f"pb{tc}"][0:1, 0:512],
                                                                 1.0 / E, None, ALU.mult),
                 waits=[(c_mm, mmv)], inc=c_row)
            b.op("vector", lambda e, tc=tc, s=s: e.tensor_scalar(msq[0:1, s], t[f"pb{2 + tc}"][0:1, 0:512],
                                                                 1.0 / E, None, ALU.mult), inc=c_row)
        self.chain("vector", lambda e: e.tensor_mul(var[0:1, :], mu[0:1, :], mu[0:1, :]), c_row)
        self.chain("vector", lambda e: e.tensor_sub(var[0:1, :], msq[0:1, :], var[0:1, :]), c_row)
        self.chain("vector", lambda e: e.tensor_scalar(var[0:1, :], var[0:1, :], LNEPS, None, ALU.add), c_row)
        vv = b.val(c_row)
        b.op("scalar", lambda e: e.activation(var[0:1, :], var[0:1, :], AF.Sqrt),
             waits=[(c_row, vv)], inc=c_row)
        self.chain("vector", lambda e: e.reciprocal(var[0:1, :], var[0:1, :]), c_row)
        self.chain("vector", lambda e: e.tensor_mul(mu[0:1, :], mu[0:1, :], var[0:1, :]), c_row)
        if stats_save is not None:
            mus_s, rstd_s = stats_save
            self.chain("vector", lambda e: e.tensor_copy(mus_s, mu[0:1, :]), c_row)
            self.chain("vector", lambda e: e.tensor_copy(rstd_s, var[0:1, :]), c_row)
        rowv = b.val(c_row)
        for tc in range(2):
            s = slice(tc * 512, (tc + 1) * 512)
            b.op("tensor", lambda e, tc=tc, s=s: e.matmul(t[f"pb{4 + tc}"][:, 0:512], t["ones1f"][:, :],
                                                          var[0:1, s], start=True, stop=True),
                 waits=[(c_row, rowv)], inc=c_bc)
            b.op("tensor", lambda e, tc=tc, s=s: e.matmul(t[f"pb{6 + tc}"][:, 0:512], t["ones1f"][:, :],
                                                          mu[0:1, s], start=True, stop=True),
                 waits=[(c_row, rowv)], inc=c_bc)
        bcv = b.val(c_bc)
        rbc, mbc = ph["rstdbc"], ph["musbc"]
        for tc in range(2):
            s = slice(tc * 512, (tc + 1) * 512)
            b.op("vector", lambda e, tc=tc, s=s: e.tensor_copy(rbc[:, s], t[f"pb{4 + tc}"][:, 0:512]),
                 waits=[(c_bc, bcv)], inc=c_be)
            b.op("vector", lambda e, tc=tc, s=s: e.tensor_copy(mbc[:, s], t[f"pb{6 + tc}"][:, 0:512]), inc=c_be)
        bev = b.val(c_be)
        tt = ph["lntmp"]
        for ec in range(NEC):
            b.op("vector", lambda e, ec=ec: e.tensor_mul(tt[:, :], hbuf[:, ec, :], rbc[:, :]),
                 waits=[(c_be, bev), (c_c, lastcast)], inc=c_h)
            b.op("vector", lambda e, ec=ec: e.tensor_sub(hbuf[:, ec, :], tt[:, :], mbc[:, :]), inc=c_h)
        return c_h, b.val(c_h)

    def emit_ffn(self, src_d, dst_d, capture_tail=False):
        nc, b, t, d = self.nc, self.b, self.t, self.d
        pes = ExitStack()

        def sb(n, s, dt):
            return pes.enter_context(nc.sbuf_tensor(self.u(n), s, dt))
        ph = {
            "rtile": [sb("fr", [128, TH], F32) for _ in range(2)],
            "sqt": [sb("fq", [128, TH], BF16) for _ in range(2)],
            "rowa": sb("fra", [1, TH], F32),
            "rowb": sb("frb", [1, TH], F32),
            "rowc": sb("frc", [1, TH], F32),
            "rstdbc": sb("frs", [128, TH], BF16),
            "musbc": sb("fmu", [128, TH], BF16),
            "lntmp": sb("flt", [128, TH], BF16),
        }
        hbuf = sb("fh", [128, NEC, TH], BF16)
        acc = sb("fac", [128, 16, TH], F32)
        gu2 = sb("fg2", [128, NKD, TH], BF16)
        w1 = [sb("fw1", [128, 16, 256], BF16) for _ in range(2)]
        w2 = [sb("fw2", [128, NKD, 128], BF16) for _ in range(2)]
        wbc = sb("fwb", [128, TH], BF16)
        gtmp = [sb("fgt", [128, 512], BF16) for _ in range(2)]
        gut = [sb("fgu", [128, 512], BF16) for _ in range(2)]
        wrowb = sb("fwo", [9, TH], BF16)
        ltok = sb("flk", [128, 9], F32)
        m1c = sb("fm1", [128, 1], F32)
        m2c = sb("fm2", [128, 1], F32)
        dcol = sb("fdc", [128, 1], F32)
        c2c = sb("fc2", [128, 1], F32)
        eq1 = sb("fe1", [128, 9], F32)
        eq2 = sb("fe2", [128, 9], F32)
        mskt = sb("fmk", [128, 9], F32)
        wtok = sb("fwt", [128, 9], F32)

        c_w1 = self.uc("w1"); c_w1f = self.uc("w1f")
        c_w2 = self.uc("w2"); c_w2f = self.uc("w2f")
        c_mm = self.uc("fmm")
        c_ep = self.uc("fep")
        c_ev = self.uc("fev")
        c_gu2 = self.uc("fgu2")
        c_fin = self.uc("ffin")
        c_sto = self.uc("fsto")
        c_rt = self.uc("frt")
        c_tp = self.uc("ftp")
        nw1 = 0
        nw2 = 0
        base1 = b.val(c_w1f)
        base2 = b.val(c_w2f)
        for half in range(2):
            t0 = half * TH
            c_h, hv = self.emit_ln(src_d, half, hbuf, ph)
            zv = b.op("vector", lambda e: e.memset(acc[:, :, :], 0.0), waits=[(c_h, hv)], inc=c_ev)

            tpv = 0
            for tc8 in range(8):
                for k in range(NEC):
                    b.op("tensor",
                         lambda e, tc8=tc8, k=k: e.matmul(
                             t["pb6"][:, 0:9], hbuf[:, k, tc8 * 128:(tc8 + 1) * 128],
                             t["wrt9s"][:, k, :], start=(k == 0), stop=(k == NEC - 1)),
                         waits=([(c_h, hv)] if (tc8 == 0 and k == 0) else []) +
                               ([(c_rt, b.val(c_rt))] if k == 0 else []),
                         inc=(c_mm if k == NEC - 1 else None))
                mv = b.val(c_mm)
                st = self.uc("t2")
                b.op("vector", lambda e: e.tensor_copy(ltok[:, :], t["pb6"][:, 0:9]),
                     waits=[(c_mm, mv)], inc=st)
                b.noop("vector", waits=[(st, b.val(st))], inc=c_rt)
                self.chain("vector", lambda e: e.reduce_max(m1c[:, :], ltok[:, 0:8], axis=AX.X), st)
                self.chain("vector", lambda e: e.tensor_scalar(eq1[:, 0:8], ltok[:, 0:8], m1c[:, 0:1], None, ALU.is_equal), st)
                self.chain("vector", lambda e: e.scalar_tensor_tensor(mskt[:, 0:8], eq1[:, 0:8], NEGC, ltok[:, 0:8], ALU.mult, ALU.add), st)
                self.chain("vector", lambda e: e.reduce_max(m2c[:, :], mskt[:, 0:8], axis=AX.X), st)
                self.chain("vector", lambda e: e.tensor_sub(dcol[:, :], m1c[:, :], m2c[:, :]), st)
                sv = b.op("scalar", lambda e: e.activation(dcol[:, :], dcol[:, :], AF.Sigmoid),
                          waits=[(st, b.val(st))], inc=st)
                b.op("scalar", lambda e: e.activation(wtok[:, 8:9], ltok[:, 8:9], AF.Sigmoid), inc=st)
                self.chain("vector", lambda e: e.tensor_scalar(c2c[:, :], dcol[:, :], -1.0, 1.0, ALU.mult, ALU.add), st)
                self.chain("vector", lambda e: e.tensor_scalar(eq1[:, 0:8], eq1[:, 0:8], dcol[:, 0:1], None, ALU.mult), st)
                self.chain("vector", lambda e: e.tensor_scalar(eq2[:, 0:8], mskt[:, 0:8], m2c[:, 0:1], None, ALU.is_equal), st)
                self.chain("vector", lambda e: e.tensor_scalar(eq2[:, 0:8], eq2[:, 0:8], c2c[:, 0:1], None, ALU.mult), st)
                self.chain("vector", lambda e: e.tensor_add(wtok[:, 0:8], eq1[:, 0:8], eq2[:, 0:8]), st)
                wv = b.val(st)
                tv = b.op("tensor", lambda e: e.transpose(t["pb7"][0:9, 0:128], wtok[:, :], t["identf"][:, :]),
                          waits=[(st, wv), (c_tp, tpv)], inc=c_mm)
                tpv = b.op("vector",
                           lambda e, tc8=tc8: e.tensor_copy(wrowb[:, tc8 * 128:(tc8 + 1) * 128], t["pb7"][0:9, 0:128]),
                           waits=[(c_mm, tv)], inc=c_tp)
            wrv = tpv

            dn_rel = [zv, zv]
            g_rel = [0, 0]
            u_rel = [0, 0]
            gt_rel = [0, 0]
            wb_rel = 0
            for ex in range(NEXP):
                e9 = ex if ex < 8 else 8
                wbv = []
                for tc in range(2):
                    v = b.op("tensor",
                             lambda e, tc=tc, e9=e9: e.matmul(
                                 t["pb6"][:, 0:512], t["sel9b"][:, e9 * 128:(e9 + 1) * 128],
                                 wrowb[:, tc * 512:(tc + 1) * 512], start=True, stop=True),
                             waits=[(c_tp, wrv), (c_ev, wb_rel), (c_rt, b.val(c_rt))], inc=c_mm)
                    wb_rel = b.op("vector",
                                  lambda e, tc=tc: e.tensor_copy(wbc[:, tc * 512:(tc + 1) * 512], t["pb6"][:, 0:512]),
                                  waits=[(c_mm, v)], inc=c_ev)
                    wbv.append(wb_rel)

                if ex < 8:
                    gsrc = lambda fb, ex=ex: d["wgu"][ex][:, fb * 128:(fb + 1) * 128]
                    usrc = lambda fb, ex=ex: d["wgu"][ex][:, 1408 + fb * 128:1408 + (fb + 1) * 128]
                else:
                    q = ex - 8
                    gsrc = lambda fb, q=q: d["wshg"][:, q * 1408 + fb * 128:q * 1408 + (fb + 1) * 128]
                    usrc = lambda fb, q=q: d["wshu"][:, q * 1408 + fb * 128:q * 1408 + (fb + 1) * 128]
                guv = 0
                for fb in range(NFB):
                    s = nw1 % 2
                    wfree = [(c_w1f, base1 + nw1 - 1)] if nw1 >= 2 else []
                    lv1 = b.dma("gpsimd", w1[s][:, :, 0:128],
                                gsrc(fb).rearrange("(c p) f -> p c f", p=128), c_w1, waits=wfree)
                    lv2 = b.dma("gpsimd", w1[s][:, :, 128:256],
                                usrc(fb).rearrange("(c p) f -> p c f", p=128), c_w1)
                    nw1 += 1
                    mmg = [0, 0]
                    mmu = [0, 0]
                    for k in range(NEC):
                        for tc in range(2):
                            mmg[tc] = b.op("tensor",
                                 lambda e, s=s, k=k, tc=tc: e.matmul(
                                     t[f"pb{tc}"][:, 0:512], w1[s][:, k, 0:128],
                                     hbuf[:, k, tc * 512:(tc + 1) * 512],
                                     start=(k == 0), stop=(k == NEC - 1)),
                                 waits=([(c_w1, lv2), (c_ep, g_rel[tc]), (c_h, hv)] if k == 0 else []),
                                 inc=(c_mm if k == NEC - 1 else None))
                    for k in range(NEC):
                        for tc in range(2):
                            mmu[tc] = b.op("tensor",
                                 lambda e, s=s, k=k, tc=tc: e.matmul(
                                     t[f"pb{2 + tc}"][:, 0:512], w1[s][:, k, 128:256],
                                     hbuf[:, k, tc * 512:(tc + 1) * 512],
                                     start=(k == 0), stop=(k == NEC - 1)),
                                 waits=([(c_ev, u_rel[tc])] if k == 0 else []),
                                 inc=(c_mm if k == NEC - 1 else None),
                                 incs=(((c_w1f, 1),) if (k == NEC - 1 and tc == 1) else ()))
                    for tc in range(2):
                        g_rel[tc] = b.op("scalar",
                             lambda e, tc=tc: e.activation(gtmp[tc][:, :], t[f"pb{tc}"][:, 0:512], AF.Silu),
                             waits=[(c_mm, mmg[tc]), (c_ev, gt_rel[tc])], inc=c_ep)
                        u_rel[tc] = b.op("vector",
                             lambda e, tc=tc: e.tensor_mul(gut[tc][:, :], gtmp[tc][:, :], t[f"pb{2 + tc}"][:, 0:512]),
                             waits=[(c_ep, g_rel[tc]), (c_mm, mmu[tc])], inc=c_ev)
                        gt_rel[tc] = u_rel[tc]
                        guv = b.op("vector",
                             lambda e, fb=fb, tc=tc: e.tensor_mul(gu2[:, fb, tc * 512:(tc + 1) * 512],
                                                                  gut[tc][:, :], wbc[:, tc * 512:(tc + 1) * 512]),
                             waits=[(c_ev, wbv[tc])], inc=c_gu2)

                if ex < 8:
                    dsrc = lambda hb, ex=ex: d["wdn"][ex][:, hb * 128:(hb + 1) * 128]
                else:
                    q = ex - 8
                    dsrc = lambda hb, q=q: d["wshd"][q * 1408:(q + 1) * 1408, hb * 128:(hb + 1) * 128]
                for hb in range(16):
                    s = nw2 % 2
                    wfree = [(c_w2f, base2 + nw2 - 1)] if nw2 >= 2 else []
                    lv = b.dma("gpsimd", w2[s][:, :, :],
                               dsrc(hb).rearrange("(c p) f -> p c f", p=128), c_w2, waits=wfree)
                    nw2 += 1
                    for tc in range(2):
                        mmd = 0
                        for k in range(NKD):
                            mmd = b.op("tensor",
                                 lambda e, s=s, k=k, tc=tc: e.matmul(
                                     t[f"pb{4 + tc}"][:, 0:512], w2[s][:, k, :],
                                     gu2[:, k, tc * 512:(tc + 1) * 512],
                                     start=(k == 0), stop=(k == NKD - 1)),
                                 waits=([(c_w2, lv), (c_gu2, guv), (c_ev, dn_rel[tc])] if k == 0 else []),
                                 inc=(c_mm if k == NKD - 1 else None),
                                 incs=(((c_w2f, 1),) if (k == NKD - 1 and tc == 1) else ()))
                        dn_rel[tc] = b.op("vector",
                             lambda e, hb=hb, tc=tc: e.tensor_add(acc[:, hb, tc * 512:(tc + 1) * 512],
                                                                  acc[:, hb, tc * 512:(tc + 1) * 512],
                                                                  t[f"pb{4 + tc}"][:, 0:512]),
                             waits=[(c_mm, mmd)], inc=c_ev)

            c_fr = self.uc("ffr")
            c_frf = self.uc("ffrf")
            for hb in range(16):
                w = [(c_frf, b.val(c_frf) - 1)] if hb >= 2 else []
                w += [(c_sto, b.val(c_sto) - 16)] if hb >= 2 else []
                lv = b.dma("sync", ph["rtile"][hb % 2][:, :],
                           src_d[hb * 128:(hb + 1) * 128, t0:t0 + TH], c_fr, waits=w)
                fv = b.op("vector",
                          lambda e, hb=hb: e.scalar_tensor_tensor(ph["rtile"][hb % 2][:, :], acc[:, hb, :], 0.5,
                                                                  ph["rtile"][hb % 2][:, :], ALU.mult, ALU.add),
                          waits=[(c_fr, lv), (c_ev, dn_rel[1])], inc=c_fin,
                          incs=((c_frf, 1),))
                if capture_tail and half == 1:
                    fv = b.op("vector",
                              lambda e, hb=hb: e.tensor_copy(t["tailacc"][:, hb, :], ph["rtile"][hb % 2][:, 1021:1024]),
                              waits=[(c_fin, fv)], inc=c_fin)
                b.dma("sync", dst_d[hb * 128:(hb + 1) * 128, t0:t0 + TH], ph["rtile"][hb % 2][:, :], c_sto,
                      waits=[(c_fin, fv)])
        if capture_tail:
            stv = self.slot_write(lambda i: d["shr1t"][i], t["tailacc"][:, :, :], "sht",
                                  waits=[(c_fin, b.val(c_fin))])
            self.slot_write(lambda i: d["shflag"][i, 0][0:1, 0:1], t["tokens"][0:1, 0:1], "shf0",
                            waits=[("sht", stv)])
        b.noop("sync", waits=[(c_sto, b.val(c_sto))])
        self.b.barrier()
        self.flush()
        pes.close()



    def emit_mamba(self, src_d, dst_d):
        nc, b, t, d = self.nc, self.b, self.t, self.d
        pm = ExitStack()

        def sm(n, s_, dt):
            return pm.enter_context(nc.sbuf_tensor(self.u(n), s_, dt))
        rep32b = sm("rep32b", [32, 32 * 128], BF16)
        convws = sm("convws", [128, 48, 4], F32)
        convbs = sm("convbs", [128, 48], F32)
        dtbiass = sm("dtbiass", [32, 1], F32)
        acol = sm("acol", [32, 1], F32)
        dbc = sm("dbc", [128, 32], F32)
        nws = sm("nws", [128, 32], F32)
        h3s = sm("h3s", [128, 16, 3], BF16)
        r1ts = sm("r1ts", [128, 16, 3], F32)
        sq3 = sm("sq3", [128, 3], BF16)
        dts = sm("dts", [32, 8, 256], F32)
        eacumb = sm("eacumb", [32, 8, 256], BF16)
        lambc = sm("lambc", [128, 32, 8], F32)
        sinb = sm("sinb", [128, 32, 128], BF16)
        row3 = sm("row3", [1, 12], F32)

        b.dma("gpsimd", rep32b[:, :], d["sel32"], "mcl")
        b.dma("sync", convws[:, :, :], d["convw"], "mcf")
        b.dma("sync", convbs[:, :], d["convb"], "mcf")
        b.dma("sync", dtbiass[:, :], d["dtbias"], "mcf")
        b.dma("sync", acol[:, :], d["alog"], "mcf")
        b.dma("sync", nws[:, :], d["nw"], "mcf")
        b.dma("sync", dbc[0:1, :], d["dvec"], "mcf")
        wf = b.val("mcf")
        b.op("scalar", lambda e: e.activation(acol[:, :], acol[:, :], AF.Exp),
             waits=[("mcf", wf)], inc="mpre")
        b.op("vector", lambda e: e.tensor_scalar(acol[:, :], acol[:, :], -1.0, None, ALU.mult),
             waits=[("mpre", b.val("mpre"))], inc="mpre")
        b.op("tensor",
             lambda e: e.matmul(t["pb7"][:, 0:32], t["ones1f"][:, :], dbc[0:1, :], start=True, stop=True),
             waits=[("mcf", wf), ("mpre", b.val("mpre"))], inc="mpre")
        b.op("vector", lambda e: e.tensor_copy(dbc[:, :], t["pb7"][:, 0:32]),
             waits=[("mpre", b.val("mpre"))], inc="mpre")
        b.barrier()

        if MK_TRUNC == "consts":
            self.flush(); pm.close(); return
        pdn, pdv = self.poll_flag(0)
        rv = self.slot_read("sync", r1ts[:, :, :], lambda i: d["shr1t"][i], "mr1t",
                            waits=[(pdn, pdv)])
        c3 = self.uc("ln3")
        cast3 = b.op("vector", lambda e: e.tensor_copy(h3s[:, :, :], r1ts[:, :, :]),
                     waits=[("mr1t", rv)], inc=c3)
        for k in range(NEC):
            sv = b.op("scalar", lambda e, k=k: e.activation(sq3[:, :], h3s[:, k, :], AF.Square),
                      waits=[(c3, b.val(c3))], inc=c3)
            b.op("tensor", lambda e, k=k: e.matmul(t["pb0"][0:1, 0:3], t["onesb"][:, :], h3s[:, k, :],
                                                   start=(k == 0), stop=(k == NEC - 1)),
                 waits=[(c3, cast3)], inc=c3)
            b.op("tensor", lambda e, k=k: e.matmul(t["pb1"][0:1, 0:3], t["onesb"][:, :], sq3[:, :],
                                                   start=(k == 0), stop=(k == NEC - 1)),
                 waits=[(c3, sv)], inc=c3)
        self.chain("vector", lambda e: e.tensor_scalar(row3[0:1, 0:3], t["pb0"][0:1, 0:3], 1.0 / E, None, ALU.mult), c3)
        self.chain("vector", lambda e: e.tensor_scalar(row3[0:1, 4:7], t["pb1"][0:1, 0:3], 1.0 / E, None, ALU.mult), c3)
        self.chain("vector", lambda e: e.tensor_mul(row3[0:1, 8:11], row3[0:1, 0:3], row3[0:1, 0:3]), c3)
        self.chain("vector", lambda e: e.tensor_sub(row3[0:1, 8:11], row3[0:1, 4:7], row3[0:1, 8:11]), c3)
        self.chain("vector", lambda e: e.tensor_scalar(row3[0:1, 8:11], row3[0:1, 8:11], LNEPS, None, ALU.add), c3)
        sv = b.op("scalar", lambda e: e.activation(row3[0:1, 8:11], row3[0:1, 8:11], AF.Sqrt),
                  waits=[(c3, b.val(c3))], inc=c3)
        self.chain("vector", lambda e: e.reciprocal(row3[0:1, 8:11], row3[0:1, 8:11]), c3)
        self.chain("vector", lambda e: e.tensor_mul(row3[0:1, 0:3], row3[0:1, 0:3], row3[0:1, 8:11]), c3)
        rw3 = b.val(c3)
        b.op("tensor", lambda e: e.matmul(t["pb2"][:, 0:3], t["ones1f"][:, :], row3[0:1, 8:11],
                                          start=True, stop=True), waits=[(c3, rw3)], inc=c3)
        b.op("tensor", lambda e: e.matmul(t["pb3"][:, 0:3], t["ones1f"][:, :], row3[0:1, 0:3],
                                          start=True, stop=True), waits=[(c3, rw3)], inc=c3)
        bc3 = b.val(c3)
        for k in range(NEC):
            b.op("vector", lambda e, k=k: e.tensor_mul(h3s[:, k, :], h3s[:, k, :], t["pb2"][:, 0:3]),
                 waits=[(c3, bc3)], inc=c3)
            b.op("vector", lambda e, k=k: e.tensor_sub(h3s[:, k, :], h3s[:, k, :], t["pb3"][:, 0:3]),
                 inc=c3)
        h3v = b.val(c3)
        if MK_TRUNC == "ln3":
            b.barrier(); self.flush(); pm.close(); return

        pa = ExitStack()

        def sa(n, s_, dt):
            return pa.enter_context(nc.sbuf_tensor(self.u(n), s_, dt))
        ph = {
            "rtile": [sa("mr", [128, TH], F32) for _ in range(2)],
            "sqt": [sa("mq", [128, TH], BF16) for _ in range(2)],
            "rowa": sa("mra", [1, TH], F32),
            "rowb": sa("mrb", [1, TH], F32),
            "rowc": sa("mrc", [1, TH], F32),
            "rstdbc": sa("mrs", [128, TH], BF16),
            "musbc": sa("mmu", [128, TH], BF16),
            "lntmp": sa("mlt", [128, TH], BF16),
        }
        hbuf = sa("mh", [128, NEC, TH], BF16)
        w1 = [sa("mw1", [128, 16, 256], BF16) for _ in range(2)]
        wdt = sa("mwdt", [128, 16, 32], BF16)
        pre = [sa("mpre", [128, 3 + TH], BF16) for _ in range(2)]
        cacc = [sa("mca", [128, TH], F32) for _ in range(2)]
        post = [sa("mpo", [128, TH], BF16) for _ in range(2)]
        tailb = sa("mtb", [128, 48, 3], BF16)
        dtv1 = sa("mdv", [32, 512], F32)
        dta = sa("mda", [32, 512], F32)

        c_w1 = self.uc("mi1"); c_w1f = self.uc("mif")
        c_ev = self.uc("mie")
        c_mm = self.uc("mim")
        c_sil = self.uc("mis")
        c_cv = self.uc("micv")
        c_st = self.uc("mist")
        c_tev = self.uc("mtev")
        c_dt = self.uc("mdt")
        ldv = b.dma("gpsimd", wdt[:, :, :], d["w_ip"][:, 6144:6176].rearrange("(c p) f -> p c f", p=128), c_w1)
        nip = 0
        prel = {0: 0, 1: 0, 2: 0, 3: 0}
        t4rel = 0
        silrel = [0, 0]
        strel = [0, 0]
        for half in range(2):
            t0 = half * TH
            c_h, hv = self.emit_ln(src_d, half, hbuf, ph)
            for fb in range(24):
                s = nip % 2
                wfree = [(c_w1f, nip - 1)] if nip >= 2 else []
                lv = b.dma("gpsimd", w1[s][:, :, :],
                           d["w_ip"][:, fb * 256:(fb + 1) * 256].rearrange("(c p) f -> p c f", p=128),
                           c_w1, waits=wfree)
                nip += 1
                for fc2 in range(2):
                    fc = fb * 2 + fc2
                    mmv = [0, 0]
                    for k in range(NEC):
                        for tc in range(2):
                            bank = fc2 * 2 + tc
                            mmv[tc] = b.op("tensor",
                                 lambda e, s=s, k=k, tc=tc, fc2=fc2, bank=bank: e.matmul(
                                     t[f"pb{bank}"][:, 0:512], w1[s][:, k, fc2 * 128:(fc2 + 1) * 128],
                                     hbuf[:, k, tc * 512:(tc + 1) * 512],
                                     start=(k == 0), stop=(k == NEC - 1)),
                                 waits=([(c_w1, lv), (c_h, hv), (c_ev, prel[bank])] if k == 0 else []),
                                 inc=(c_mm if k == NEC - 1 else None),
                                 incs=(((c_w1f, 1),) if (k == NEC - 1 and tc == 1 and fc2 == 1) else ()))
                    if half == 0 and "tail" in MK_P2A:
                        tmv = 0
                        for k in range(NEC):
                            tmv = b.op("tensor",
                                 lambda e, s=s, k=k, fc2=fc2: e.matmul(
                                     t["pb4"][:, 0:3], w1[s][:, k, fc2 * 128:(fc2 + 1) * 128],
                                     h3s[:, k, :], start=(k == 0), stop=(k == NEC - 1)),
                                 waits=([(c3, h3v), (c_tev, t4rel)] if k == 0 else []),
                                 inc=(c_mm if k == NEC - 1 else None))
                        t4rel = b.op("vector",
                             lambda e, s=s: e.tensor_scalar(pre[s][:, 0:3], t["pb4"][:, 0:3],
                                                            t["maskcols"][:, 0:1], None, ALU.mult),
                             waits=[(c_mm, tmv)], inc=c_tev)
                    elif "tail" in MK_P2A:
                        b.op("vector", lambda e, s=s, fc=fc: e.tensor_copy(pre[s][:, 0:3], tailb[:, fc, :]),
                             inc=c_tev)
                    else:
                        b.op("vector", lambda e, s=s: e.memset(pre[s][:, 0:3], 0.0), inc=c_tev)
                    for tc in range(2):
                        bank = fc2 * 2 + tc
                        prel[bank] = b.op("vector",
                             lambda e, s=s, tc=tc, bank=bank: e.tensor_copy(
                                 pre[s][:, 3 + tc * 512: 3 + (tc + 1) * 512], t[f"pb{bank}"][:, 0:512]),
                             waits=[(c_mm, mmv[tc])], inc=c_ev)
                    ca = fc % 2
                    if "conv" in MK_P2A:
                        b.op("vector", lambda e, s=s, ca=ca, fc=fc: e.tensor_scalar(
                                 cacc[ca][:, :], pre[s][:, 0:TH], convws[:, fc, 0:1], None, ALU.mult),
                             waits=[(c_sil, silrel[ca])], inc=c_cv)
                        for j in (1, 2, 3):
                            b.op("vector", lambda e, s=s, ca=ca, fc=fc, j=j: e.scalar_tensor_tensor(
                                     cacc[ca][:, :], pre[s][:, j:j + TH], convws[:, fc, j:j + 1],
                                     cacc[ca][:, :], ALU.mult, ALU.add), inc=c_cv)
                        if half == 0:
                            b.op("vector", lambda e, s=s, fc=fc: e.tensor_copy(tailb[:, fc, :], pre[s][:, TH:TH + 3]),
                                 inc=c_cv)
                    cvv = b.val(c_cv)
                    if "silu" in MK_P2A:
                        silrel[ca] = b.op("scalar",
                             lambda e, ca=ca, fc=fc: e.activation(post[ca][:, :], cacc[ca][:, :], AF.Silu,
                                                                  bias=convbs[:, fc:fc + 1]),
                             waits=[(c_cv, cvv), (c_st, strel[ca])], inc=c_sil)
                    if "store" in MK_P2A:
                        strel[ca] = b.dma("sync", d["hbc_d"][fc * 128:(fc + 1) * 128, t0:t0 + TH], post[ca][:, :],
                                          c_st, waits=[(c_sil, silrel[ca])])
            dtrel = 0
            for tc in (range(2) if ("dt" in MK_P2A or "dtmm" in MK_P2A) else []):
                mmv = 0
                for k in range(NEC):
                    mmv = b.op("tensor",
                         lambda e, k=k, tc=tc: e.matmul(t["pb5"][0:32, 0:512], wdt[:, k, :],
                                                        hbuf[:, k, tc * 512:(tc + 1) * 512],
                                                        start=(k == 0), stop=(k == NEC - 1)),
                         waits=([(c_w1, ldv), (c_h, hv), (c_dt, dtrel)] if k == 0 else []),
                         inc=(c_mm if k == NEC - 1 else None))
                if "dt" not in MK_P2A:
                    dtrel = b.op("vector", lambda e: e.tensor_copy(dtv1[:, :], t["pb5"][0:32, 0:512]),
                                 waits=[(c_mm, mmv)], inc=c_dt)
                    continue
                import os as _os
                lvl = int(_os.environ.get("MK_DTLVL", "9"))
                tsv = b.op("vector", lambda e: e.tensor_scalar(dtv1[:, :], t["pb5"][0:32, 0:512],
                                                               dtbiass[:, 0:1], 0.0, ALU.add, ALU.max),
                           waits=[(c_mm, mmv)], inc=c_dt)
                av = b.op("scalar", lambda e: e.activation(dta[:, :], t["pb5"][0:32, 0:512], AF.Abs,
                                                           bias=dtbiass[:, 0:1]),
                          waits=[(c_dt, tsv)], inc=c_dt)
                av = b.op("scalar", lambda e: e.activation(dta[:, :], dta[:, :], AF.Exp, scale=-1.0),
                          waits=[(c_dt, av)], inc=c_dt)
                av = b.op("scalar", lambda e: e.activation(dta[:, :], dta[:, :], AF.Ln, bias=1.0),
                          waits=[(c_dt, av)], inc=c_dt)
                for j in range(2):
                    dtrel = b.op("vector",
                         lambda e, half=half, tc=tc, j=j: e.tensor_add(
                             dts[:, half * 4 + tc * 2 + j, :],
                             dtv1[:, j * 256:(j + 1) * 256], dta[:, j * 256:(j + 1) * 256]),
                         waits=[(c_dt, b.val(c_dt))], inc=c_dt)
        b.barrier()
        self.flush()
        pa.close()
        if MK_TRUNC == "p2a":
            pm.close(); return

        pc = ExitStack()

        def sc(n, s_, dt):
            return pc.enter_context(nc.sbuf_tensor(self.u(n), s_, dt))
        sel32f = sc("sel32f", [32, 32 * 128], F32)
        maskadds = sc("maskadds", [128, 2, 256], F32)
        arow = sc("arow", [32, 8, 256], F32)
        acum = sc("acum", [32, 8, 256], F32)
        nacum = sc("nacum", [32, 8, 256], F32)
        dtb = sc("dtb", [32, 8, 256], BF16)
        alast = sc("alast", [32, 8], F32)
        ealast = sc("ealast", [32, 8], F32)
        lamx = sc("lamx", [32, 8], F32)
        acumT = sc("acumT", [128, 16, 32], F32)
        nacumT = sc("nacumT", [128, 16, 32], F32)
        ealastbc = sc("ealastbc", [128, 32, 8], F32)
        b.dma("sync", sel32f[:, :], d["sel32"], "mcf")
        b.dma("sync", maskadds[:, :, :], d["maskadd"].rearrange("k p l -> p k l"), "mcf")
        cB = self.uc("p2b")
        wcf = b.val("mcf")
        b.op("vector", lambda e: e.tensor_scalar(arow[:, :, :], dts[:, :, :], acol[:, 0:1], None, ALU.mult),
             waits=[(cB, 0)], inc=cB)
        b.op("vector", lambda e: e.tensor_copy(dtb[:, :, :], dts[:, :, :]), inc=cB)
        for c in range(8):
            self.chain("vector", lambda e, c=c: e.tensor_tensor_scan(
                acum[:, c, :], arow[:, c, :], arow[:, c, :], 0.0, ALU.add, ALU.bypass), cB)
        self.chain("vector", lambda e: e.tensor_scalar(nacum[:, :, :], acum[:, :, :], -1.0, None, ALU.mult), cB)
        eav = b.op("scalar", lambda e: e.activation(eacumb[:, :, :], acum[:, :, :], AF.Exp),
                   waits=[(cB, b.val(cB))], inc=cB)
        self.chain("vector", lambda e: e.tensor_copy(alast[:, :], acum[:, :, 255]), cB)
        alv = b.op("scalar", lambda e: e.activation(ealast[:, :], alast[:, :], AF.Exp),
                   waits=[(cB, b.val(cB))], inc=cB)
        self.chain("vector", lambda e: e.tensor_tensor_scan(lamx[:, :], alast[:, :], alast[:, :],
                                                            0.0, ALU.add, ALU.bypass), cB)
        self.chain("vector", lambda e: e.tensor_sub(lamx[:, :], lamx[:, :], alast[:, :]), cB)
        lmv = b.op("scalar", lambda e: e.activation(lamx[:, :], lamx[:, :], AF.Exp),
                   waits=[(cB, b.val(cB))], inc=cB)
        for i in range(16):
            c = i // 2
            s0 = (i % 2) * 128
            tv = b.op("tensor", lambda e, c=c, s0=s0: e.transpose(
                     t["pb6"][:, 0:32], acum[:, c, s0:s0 + 128],
                     t["identf"][0:32, 0:32]),
                 waits=[(cB, b.val(cB))], inc=cB)
            self.chain("vector", lambda e, i=i: e.tensor_copy(acumT[:, i, :], t["pb6"][:, 0:32]), cB)
        self.chain("vector", lambda e: e.tensor_scalar(nacumT[:, :, :], acumT[:, :, :], -1.0, None, ALU.mult), cB)
        for h in range(NH):
            tv = b.op("tensor", lambda e, h=h: e.matmul(t["pb0"][:, 0:8], sel32f[:, h * 128:(h + 1) * 128],
                                                        ealast[:, :], start=True, stop=True),
                      waits=[(cB, b.val(cB)), ("mcf", wcf), (cB, alv)], inc=cB)
            self.chain("vector", lambda e, h=h: e.tensor_copy(ealastbc[:, h, :], t["pb0"][:, 0:8]), cB)
            tv = b.op("tensor", lambda e, h=h: e.matmul(t["pb1"][:, 0:8], sel32f[:, h * 128:(h + 1) * 128],
                                                        lamx[:, :], start=True, stop=True),
                      waits=[(cB, b.val(cB)), (cB, lmv)], inc=cB)
            self.chain("vector", lambda e, h=h: e.tensor_copy(lambc[:, h, :], t["pb1"][:, 0:8]), cB)
        b.barrier()
        if MK_TRUNC == "p2b":
            self.flush(); pc.close(); pm.close(); return

        XT = [sc("XT", [128, 2, 4096], BF16) for _ in range(2)]
        XDD = sc("XDD", [128, 2, 4096], BF16)
        BT = [sc("BT", [128, 2, 1024], BF16) for _ in range(2)]
        BF_ = [sc("BFm", [128, 8, 256], BF16) for _ in range(2)]
        CF = [sc("CFm", [128, 8, 256], BF16) for _ in range(2)]
        Ssb = sc("Ssb", [128, 32, 128], F32)
        SPb = sc("SPb", [128, 128], BF16)
        LTE = sc("LTE", [128, 2, 256], F32)
        MTb = sc("MTb", [128, 2, 256], BF16)
        Cscb = sc("Cscb", [128, 256], BF16)
        decf = sc("decf", [32, 256], F32)
        decb = sc("decb", [32, 256], BF16)
        yev = [sc("yev", [128, 256], BF16) for _ in range(2)]

        c_xl = self.uc("sxl")
        c_smm = self.uc("smm")
        c_sdv = self.uc("sdv")
        c_sac = self.uc("sac")
        c_yst = self.uc("syst")
        zmv = b.op("vector", lambda e: e.memset(Ssb[:, :, :], 0.0), inc=c_sdv)
        xf = [0, 0]
        bcf = [0, 0]
        loads = {}
        for c in range(9):
            if c < 8:
                sl = c % 2
                cs = c * CH
                w = [(c_smm, xf[sl]), (c_sdv, bcf[sl])]
                l1 = b.dma("scalar", XT[sl][:, 0, :], d["hbc_d"][0:IATT, cs:cs + 128], c_xl,
                           waits=w, transpose=True)
                l2 = b.dma("scalar", XT[sl][:, 1, :], d["hbc_d"][0:IATT, cs + 128:cs + 256], c_xl,
                           transpose=True)
                l3 = b.dma("scalar", BT[sl][:, 0, :], d["hbc_d"][IATT:IATT + 1024, cs:cs + 128], c_xl,
                           transpose=True)
                l4 = b.dma("scalar", BT[sl][:, 1, :], d["hbc_d"][IATT:IATT + 1024, cs + 128:cs + 256], c_xl,
                           transpose=True)
                l5 = b.dma("scalar", BF_[sl][:, :, :],
                           d["hbc_d"][IATT:IATT + 1024, cs:cs + 256].rearrange("(g p) s -> p g s", p=128), c_xl)
                l6 = b.dma("scalar", CF[sl][:, :, :],
                           d["hbc_d"][IATT + 1024:CD, cs:cs + 256].rearrange("(g p) s -> p g s", p=128), c_xl)
                loads[c] = b.val(c_xl)
            if c == 0:
                continue
            cc = c - 1
            sl2 = cc % 2
            ldv2 = loads[cc]
            dxv = 0
            for s2 in range(2):
                for n8 in range(8):
                    bank = "pb0" if (n8 % 2 == 0) else "pb7"
                    mv = b.op("tensor", lambda e, cc=cc, s2=s2, n8=n8, bank=bank: e.matmul(
                             t[bank][:, 0:512], dtb[:, cc, s2 * 128:(s2 + 1) * 128],
                             rep32b[:, n8 * 512:(n8 + 1) * 512], start=True, stop=True),
                         waits=[(c_xl, ldv2), (c_sdv, b.val(c_sdv) - 1 if (s2 * 8 + n8) >= 2 else 0)],
                         inc=c_smm)
                    dxv = b.op("vector", lambda e, sl2=sl2, s2=s2, n8=n8, bank=bank: e.tensor_mul(
                             XT[sl2][:, s2, n8 * 512:(n8 + 1) * 512],
                             XT[sl2][:, s2, n8 * 512:(n8 + 1) * 512], t[bank][:, 0:512]),
                         waits=[(c_smm, mv)], inc=c_sdv)
            dv = self.chain("vector", lambda e, cc=cc: e.tensor_scalar(
                     decf[:, :], nacum[:, cc, :], alast[:, cc:cc + 1], None, ALU.add), c_sdv)
            dv = b.op("scalar", lambda e: e.activation(decb[:, :], decf[:, :], AF.Exp),
                      waits=[(c_sdv, dv), (c_smm, b.val(c_smm))], inc=c_sac)
            ddv = 0
            xddgate = b.val(c_smm)
            for s2 in range(2):
                for n8 in range(8):
                    bank = "pb0" if (n8 % 2 == 0) else "pb7"
                    mv = b.op("tensor", lambda e, s2=s2, n8=n8, bank=bank: e.matmul(
                             t[bank][:, 0:512], decb[:, s2 * 128:(s2 + 1) * 128],
                             rep32b[:, n8 * 512:(n8 + 1) * 512], start=True, stop=True),
                         waits=[(c_sac, dv), (c_sdv, b.val(c_sdv) - 1 if (s2 * 8 + n8) >= 2 else 0)],
                         inc=c_smm)
                    ddv = b.op("vector", lambda e, sl2=sl2, s2=s2, n8=n8, bank=bank: e.tensor_mul(
                             XDD[:, s2, n8 * 512:(n8 + 1) * 512],
                             XT[sl2][:, s2, n8 * 512:(n8 + 1) * 512], t[bank][:, 0:512]),
                         waits=[(c_smm, mv), (c_smm, xddgate)], inc=c_sdv)
            rel = {"pb1": 0, "pb2": 0, "pb3": 0, "pb4": 0, "pb5": 0, "pb6": 0}
            supd = {}
            lastymm = 0
            for h in range(NH):
                g = h % 8
                rb = b.op("tensor", lambda e, h=h, cc=cc: e.matmul(
                         t["pb5"][:, 0:256], sel32f[:, h * 128:(h + 1) * 128], acum[:, cc, :],
                         start=True, stop=True),
                     waits=[(c_sdv, rel["pb5"])], inc=c_smm)
                m1 = [0, 0]
                for s2 in range(2):
                    m1[s2] = b.op("tensor", lambda e, h=h, g=g, s2=s2, sl2=sl2: e.matmul(
                             t[f"pb{1 + s2}"][:, 0:256], BF_[sl2][:, g, s2 * 128:(s2 + 1) * 128],
                             CF[sl2][:, g, :], start=True, stop=True),
                         waits=[(c_sdv, rel[f"pb{1 + s2}"]), (c_xl, ldv2)], inc=c_smm)
                cb = b.op("tensor", lambda e, h=h, cc=cc: e.matmul(
                         t["pb6"][:, 0:256], rep32b[:, h * 128:(h + 1) * 128], eacumb[:, cc, :],
                         start=True, stop=True),
                     waits=[(c_sdv, rel["pb6"])], inc=c_smm)
                sm = 0
                for s2 in range(2):
                    sm = b.op("tensor", lambda e, h=h, g=g, s2=s2, sl2=sl2: e.matmul(
                             t["pb4"][:, 0:128], BT[sl2][:, s2, g * 128:(g + 1) * 128],
                             XDD[:, s2, h * 128:(h + 1) * 128],
                             start=(s2 == 0), stop=(s2 == 1)),
                         waits=([(c_sdv, rel["pb4"]), (c_sdv, ddv)] if s2 == 0 else []), inc=(c_smm if s2 == 1 else None))
                mt = 0
                for s2 in range(2):
                    lt = b.op("vector", lambda e, h=h, cc=cc, s2=s2: e.scalar_tensor_tensor(
                             LTE[:, s2, :], t["pb5"][:, 0:256], nacumT[:, cc * 2 + s2, h:h + 1],
                             maskadds[:, s2, :], ALU.add, ALU.add),
                         waits=[(c_smm, rb)], inc=c_sdv)
                    if s2 == 1:
                        rel["pb5"] = lt
                    ex = b.op("scalar", lambda e, s2=s2: e.activation(LTE[:, s2, :], LTE[:, s2, :], AF.Exp),
                              waits=[(c_sdv, lt)], inc=c_sac)
                    mt = b.op("vector", lambda e, s2=s2: e.tensor_mul(MTb[:, s2, :], LTE[:, s2, :],
                                                                     t[f"pb{1 + s2}"][:, 0:256]),
                         waits=[(c_sac, ex), (c_smm, m1[s2]), (c_smm, lastymm)], inc=c_sdv)
                    rel[f"pb{1 + s2}"] = mt
                csc = b.op("vector", lambda e, g=g, sl2=sl2: e.tensor_mul(Cscb[:, :], CF[sl2][:, g, :],
                                                                          t["pb6"][:, 0:256]),
                     waits=[(c_smm, cb), (c_smm, lastymm)], inc=c_sdv)
                rel["pb6"] = csc
                spv = b.op("vector", lambda e, h=h: e.tensor_copy(SPb[:, :], Ssb[:, h, :]),
                     waits=[(c_sdv, supd.get(h, zmv)), (c_smm, lastymm)], inc=c_sdv)
                ym = 0
                for s2 in range(2):
                    ym = b.op("tensor", lambda e, h=h, s2=s2, sl2=sl2: e.matmul(
                             t["pb3"][:, 0:256], XT[sl2][:, s2, h * 128:(h + 1) * 128], MTb[:, s2, :],
                             start=(s2 == 0), stop=False),
                         waits=([(c_sac, rel["pb3"]), (c_sdv, mt), (c_sdv, dxv)] if s2 == 0 else []),
                         inc=None)
                ym = b.op("tensor", lambda e, h=h: e.matmul(
                         t["pb3"][:, 0:256], SPb[:, :], Cscb[:, :], start=False, stop=True),
                     waits=[(c_sdv, csc), (c_sdv, spv)], inc=c_smm)
                lastymm = ym
                if h == NH - 1:
                    xf[sl2] = ym
                sup = b.op("vector", lambda e, h=h, cc=cc: e.scalar_tensor_tensor(
                         Ssb[:, h, :], Ssb[:, h, :], ealastbc[:, h, cc:cc + 1], t["pb4"][:, 0:128],
                         ALU.mult, ALU.add),
                     waits=[(c_smm, sm), (c_sdv, spv)], inc=c_sdv)
                supd[h] = sup
                rel["pb4"] = sup
                if h == NH - 1:
                    bcf[sl2] = sup
                ya = b.op("scalar", lambda e, h=h: e.activation(yev[h % 2][:, :], t["pb3"][:, 0:256], AF.Copy),
                     waits=[(c_smm, ym), (c_yst, b.val(c_yst) - 16 if h >= 2 else 0)], inc=c_sac)
                rel["pb3"] = ya
                b.dma("sync", d["y_d"][h * 128:(h + 1) * 128, cc * CH:(cc + 1) * CH], yev[h % 2][:, :],
                      c_yst, waits=[(c_sac, ya)])
        sv = self.slot_write(lambda i: d["shst"][i], Ssb[:, :, :], "mshs",
                             waits=[(c_sdv, b.val(c_sdv))])
        self.slot_write(lambda i: d["shflag"][i, 1][0:1, 0:1], t["tokens"][0:1, 0:1], "mshf",
                        waits=[("mshs", sv)])
        if MK_TRUNC == "p2c":
            b.barrier(); self.flush(); pc.close(); pm.close(); return
        pdn2, pdv2 = self.poll_flag(1)
        rv2 = self.slot_read("gpsimd", sinb[:, :, :], lambda i: d["shst"][i], "msin",
                             waits=[(pdn2, pdv2)])
        c_si = self.uc("msk")
        b.op("vector", lambda e: e.tensor_scalar(sinb[:, :, :], sinb[:, :, :], t["maskcols"][:, 0:1],
                                                 None, ALU.mult),
             waits=[("msin", rv2)], inc=c_si)
        b.barrier()
        self.flush()
        pc.close()
        if MK_TRUNC == "p2d":
            pm.close(); return

        p3 = ExitStack()

        def s3(n, s_, dt):
            return p3.enter_context(nc.sbuf_tensor(self.u(n), s_, dt))
        ph3 = {
            "rtile": [s3("3r", [128, TH], F32) for _ in range(2)],
            "sqt": [s3("3q", [128, TH], BF16) for _ in range(2)],
            "rowa": s3("3ra", [1, TH], F32),
            "rowb": s3("3rb", [1, TH], F32),
            "rowc": s3("3rc", [1, TH], F32),
            "rstdbc": s3("3rs", [128, TH], BF16),
            "musbc": s3("3mu", [128, TH], BF16),
            "lntmp": s3("3lt", [128, TH], BF16),
        }
        hbuf3 = s3("3h", [128, NEC, TH], BF16)
        gbuf = s3("3g", [128, 32, TH], BF16)
        zw = [s3("3zw", [128, 16, 128], BF16) for _ in range(2)]
        wo = [s3("3wo", [128, 32, 128], BF16) for _ in range(2)]
        hs2 = [s3("3hs", [128, TH], BF16) for _ in range(2)]
        ytl = [s3("3yt", [128, TH], BF16) for _ in range(2)]
        ctl = [s3("3ca", [128, TH], BF16) for _ in range(2)]
        silz = s3("3sz", [128, 256], BF16)
        sqg = [s3("3sq", [128, 512], BF16) for _ in range(2)]
        yt2 = s3("3y2", [128, 256], F32)
        SPc = s3("3sp", [128, 128], BF16)
        Cs3 = s3("3cs", [128, 256], BF16)
        rrow = s3("3rr", [1, 512], F32)
        rbc3 = [s3("3rb2", [128, 512], F32) for _ in range(2)]

        c_zl = self.uc("z3l"); c_zlf = self.uc("z3f")
        c_hl = self.uc("h3l")
        c_yl = self.uc("y3l")
        c_3m = self.uc("m3m")
        c_3v = self.uc("m3v")
        c_3a = self.uc("m3a")
        c_gn = self.uc("m3g")
        c_ol = self.uc("o3l"); c_olf = self.uc("o3f")
        c_or = self.uc("o3r")
        c_ost = self.uc("o3s")
        nzl = 0
        nol = 0
        for half in range(2):
            t0 = half * TH
            c_h3, h3vv = self.emit_ln(src_d, half, hbuf3, ph3)

            relz = 0
            relc = 0
            rel1 = 0
            rstat = 0
            rbcr = [0, 0]
            for grp in range(8):
                for hh in range(4):
                    h = grp * 4 + hh
                    g = h % 8
                    s = nzl % 2
                    wz = [(c_zlf, nzl - 1)] if nzl >= 2 else []
                    zlv = b.dma("gpsimd", zw[s][:, :, :],
                                d["w_ip"][:, 6176 + h * 128:6176 + (h + 1) * 128].rearrange("(c p) f -> p c f", p=128),
                                c_zl, waits=wz)
                    nzl += 1
                    hlv = b.dma("sync", hs2[s][:, :], d["hbc_d"][h * 128:(h + 1) * 128, t0:t0 + TH], c_hl,
                                waits=([(c_3v, b.val(c_3v) - 2)] if nzl >= 3 else []))
                    calv = b.dma("sync", ctl[s][:, :],
                                 d["hbc_d"][IATT + 1024 + g * 128:IATT + 1024 + (g + 1) * 128, t0:t0 + TH],
                                 c_hl, waits=([(c_3v, b.val(c_3v) - 2)] if nzl >= 3 else []))
                    ylv = b.dma("sync", ytl[s][:, :], d["y_d"][h * 128:(h + 1) * 128, t0:t0 + TH], c_yl,
                                waits=([(c_3v, b.val(c_3v) - 2)] if nzl >= 3 else []))
                    for c2 in range(4):
                        cglob = half * 4 + c2
                        zm = 0
                        for k in range(NEC):
                            zm = b.op("tensor", lambda e, s=s, k=k, c2=c2: e.matmul(
                                     t["pb0"][:, 0:256], zw[s][:, k, :],
                                     hbuf3[:, k, c2 * 256:(c2 + 1) * 256],
                                     start=(k == 0), stop=(k == NEC - 1)),
                                 waits=([(c_zl, zlv), (c_h3, h3vv), (c_3a, relz)] if k == 0 else []),
                                 inc=(c_3m if k == NEC - 1 else None),
                                 incs=(((c_zlf, 1),) if (k == NEC - 1 and c2 == 3) else ()))
                        ebm = b.op("tensor", lambda e, h=h, cglob=cglob: e.matmul(
                                 t["pb6"][:, 0:256], rep32b[:, h * 128:(h + 1) * 128], eacumb[:, cglob, :],
                                 start=True, stop=True),
                             waits=[(c_3v, relc)], inc=c_3m)
                        cs3 = b.op("vector", lambda e, s=s, c2=c2: e.tensor_mul(
                                 Cs3[:, :], ctl[s][:, c2 * 256:(c2 + 1) * 256], t["pb6"][:, 0:256]),
                             waits=[(c_3m, ebm), (c_hl, calv)], inc=c_3v)
                        relc = cs3
                        spv = b.op("vector", lambda e, h=h, cglob=cglob: e.tensor_scalar(
                                 SPc[:, :], sinb[:, h, :], lambc[:, h, cglob:cglob + 1], None, ALU.mult),
                             waits=[(c_si, 1)], inc=c_3v)
                        ym = b.op("tensor", lambda e: e.matmul(t["pb1"][:, 0:256], SPc[:, :], Cs3[:, :],
                                                               start=True, stop=True),
                             waits=[(c_3v, spv), (c_3v, cs3), (c_3v, rel1)],
                             inc=c_3m)
                        sz = b.op("scalar", lambda e: e.activation(silz[:, :], t["pb0"][:, 0:256], AF.Silu),
                             waits=[(c_3m, zm)], inc=c_3a)
                        relz = sz
                        rel1 = b.op("vector", lambda e, c2=c2, s=s: e.tensor_add(yt2[:, :],
                                                                          ytl[s][:, c2 * 256:(c2 + 1) * 256],
                                                                          t["pb1"][:, 0:256]),
                             waits=[(c_3m, ym), (c_yl, ylv)], inc=c_3v)
                        y2 = b.op("vector", lambda e, h=h, c2=c2, s=s: e.scalar_tensor_tensor(
                                 yt2[:, :], hs2[s][:, c2 * 256:(c2 + 1) * 256], dbc[:, h:h + 1],
                                 yt2[:, :], ALU.mult, ALU.add),
                             waits=[(c_hl, hlv)], inc=c_3v)
                        gv = b.op("vector", lambda e, h=h, c2=c2: e.tensor_mul(
                                 gbuf[:, h, c2 * 256:(c2 + 1) * 256], yt2[:, :], silz[:, :]),
                             waits=[(c_3a, sz)], inc=c_3v)
                for tc in range(2):
                    smv = 0
                    for hh in range(4):
                        h = grp * 4 + hh
                        qv = b.op("scalar", lambda e, h=h, tc=tc, hh=hh: e.activation(
                                 sqg[hh % 2][:, :], gbuf[:, h, tc * 512:(tc + 1) * 512], AF.Square),
                             waits=[(c_3v, b.val(c_3v))], inc=c_3a)
                        smv = b.op("tensor", lambda e, hh=hh: e.matmul(
                                 t["pb2"][0:1, 0:512], t["onesb"][:, :], sqg[hh % 2][:, :],
                                 start=(hh == 0), stop=(hh == 3)),
                             waits=[(c_3a, qv), (c_3v, rstat)], inc=(c_3m if hh == 3 else None))
                    rv3 = b.op("vector", lambda e: e.tensor_scalar(rrow[0:1, :], t["pb2"][0:1, 0:512],
                                                                   1.0 / 512, RMSEPS, ALU.mult, ALU.add),
                         waits=[(c_3m, smv)], inc=c_3v)
                    rstat = rv3
                    rv3 = b.op("scalar", lambda e: e.activation(rrow[0:1, :], rrow[0:1, :], AF.Sqrt),
                         waits=[(c_3v, rv3)], inc=c_3a)
                    rv3 = b.op("vector", lambda e: e.reciprocal(rrow[0:1, :], rrow[0:1, :]),
                         waits=[(c_3a, rv3)], inc=c_3v)
                    bm = b.op("tensor", lambda e: e.matmul(t["pb3"][:, 0:512], t["ones1f"][:, :], rrow[0:1, :],
                                                           start=True, stop=True),
                         waits=[(c_3v, rv3), (c_3v, rbcr[tc])], inc=c_3m)
                    rbv = b.op("vector", lambda e, tc=tc: e.tensor_copy(rbc3[tc][:, :], t["pb3"][:, 0:512]),
                         waits=[(c_3m, bm)], inc=c_3v)
                    for hh in range(4):
                        h = grp * 4 + hh
                        rbcr[tc] = b.op("vector", lambda e, h=h, tc=tc: e.scalar_tensor_tensor(
                                 gbuf[:, h, tc * 512:(tc + 1) * 512], gbuf[:, h, tc * 512:(tc + 1) * 512],
                                 nws[:, h:h + 1], rbc3[tc][:, :], ALU.mult, ALU.mult),
                             waits=[(c_3v, rbv)], inc=c_3v, incs=(((c_gn, 1),) if hh == 3 and tc == 1 else ()))
            gnv = b.val(c_gn)
            for fb in range(16):
                s = nol % 2
                wz = [(c_olf, nol - 1)] if nol >= 2 else []
                olv = b.dma("gpsimd", wo[s][:, :, :],
                            d["w_op"][:, fb * 128:(fb + 1) * 128].rearrange("(c p) f -> p c f", p=128),
                            c_ol, waits=wz)
                nol += 1
                rlv = b.dma("sync", ph3["rtile"][s][:, :], src_d[fb * 128:(fb + 1) * 128, t0:t0 + TH],
                            c_hl, waits=([(c_ost, b.val(c_ost) - 16)] if fb >= 2 else []))
                adv = 0
                for tc in range(2):
                    om = 0
                    for k in range(32):
                        om = b.op("tensor", lambda e, s=s, k=k, tc=tc: e.matmul(
                                 t[f"pb{4 + tc}"][:, 0:512], wo[s][:, k, :],
                                 gbuf[:, k, tc * 512:(tc + 1) * 512],
                                 start=(k == 0), stop=(k == 31)),
                             waits=([(c_ol, olv), (c_gn, gnv), (c_or, b.val(c_or) - 1 if fb >= 1 else 0)] if k == 0 else []),
                             inc=(c_3m if k == 31 else None),
                             incs=(((c_olf, 1),) if (k == 31 and tc == 1) else ()))
                    adv = b.op("vector", lambda e, s=s, tc=tc: e.tensor_add(
                             ph3["rtile"][s][:, tc * 512:(tc + 1) * 512],
                             ph3["rtile"][s][:, tc * 512:(tc + 1) * 512], t[f"pb{4 + tc}"][:, 0:512]),
                         waits=[(c_3m, om), (c_hl, rlv)], inc=c_or)
                b.dma("sync", dst_d[fb * 128:(fb + 1) * 128, t0:t0 + TH], ph3["rtile"][s][:, :], c_ost,
                      waits=[(c_or, adv)])
        b.noop("sync", waits=[(c_ost, b.val(c_ost))])
        b.barrier()
        self.flush()
        p3.close()
        pm.close()



def build_nc(stages=("ffn1", "mamba", "ffn2")):
    nc = bass.Bass("TRN2", target_bir_lowering=False, debug=False)
    with ExitStack() as es:
        p = Prog(nc, es, stages)
        p.declare_dram()
        p.alloc_global()
        p.block = es.enter_context(nc.Block())
        p.emit_preamble()
        if "ffn1" in stages:
            dst = p.d["r1_d"] if ("mamba" in stages or "ffn2" in stages) else p.d["out_fm"]
            p.emit_ffn(p.d["x_fm"], dst, capture_tail=("mamba" in stages))
        if "mamba" in stages:
            dst = p.d["r2_d"] if "ffn2" in stages else p.d["out_fm"]
            p.emit_mamba(p.d["r1_d"], dst)
        if "ffn2" in stages:
            src = p.d["r2_d"] if "mamba" in stages else (p.d["r1_d"] if "ffn1" in stages else p.d["x_fm"])
            p.emit_ffn(src, p.d["out_fm"], capture_tail=False)
    return nc


def kernel(**inputs):
    maps = _prep_core_inputs(inputs)
    nc = build_nc()
    res = run_bass_kernel_spmd(nc, maps, core_ids=list(range(8)))
    out = np.empty((4, 4096, 2048), np.float32)
    for c in range(8):
        bidx, half = c // 2, c % 2
        out[bidx, half * T:(half + 1) * T, :] = res.results[c]["out_fm"].T
    return out


# version 04
# speedup vs baseline: 1.1168x; 1.0421x over previous
import numpy as np
import ml_dtypes
import concourse.bass as bass
import concourse.mybir as mybir
from concourse.bass_utils import run_bass_kernel_spmd
from contextlib import ExitStack

F32 = mybir.dt.float32
BF16 = mybir.dt.bfloat16
I32 = mybir.dt.int32
AF = mybir.ActivationFunctionType
ALU = mybir.AluOpType
AX = mybir.AxisListType

E = 2048
T = 2048
TH = 1024
NEC = 16
NH = 32
CH = 256
IMOE = 1408
NKD = 11
NFB = 11
NEXP = 12
IATT = 4096
CD = 6144
LNEPS = 1e-5
RMSEPS = 1e-6
NEGC = -1e30

ENGINES = ("tensor", "vector", "scalar", "gpsimd", "sync")
MK_TRUNC = "full"
MK_P2A = {"tail", "conv", "silu", "store", "dt"}


class KB:
    def __init__(self, nc, es):
        self.nc = nc
        self.es = es
        self.pend = []
        self.cnt = {}
        self.sems = {}
        self._waited = {}
        self.dma_chans = {}

    def ch(self, name):
        if name not in self.sems:
            self.sems[name] = self.es.enter_context(self.nc.semaphore(name))
            self.cnt[name] = 0
        return name

    def val(self, name):
        self.ch(name)
        return self.cnt[name]

    def op(self, eng, fn, waits=(), inc=None, amt=1, incs=()):
        allincs = []
        after = None
        if inc is not None:
            self.ch(inc)
            self.cnt[inc] += amt
            after = self.cnt[inc]
            allincs.append((inc, amt))
        extra = []
        for (c, a) in incs:
            self.ch(c)
            self.cnt[c] += a
            if not allincs:
                allincs.append((c, a))
                after = self.cnt[c]
            else:
                extra.append((c, a, self.cnt[c]))
        ws = []
        for (c, v) in waits:
            self.ch(c)
            if v > 0:
                ws.append((c, v))
        self.pend.append((eng, fn, tuple(ws), tuple(allincs[:1])))
        if extra:
            prim = allincs[0][0]
            pv = self.cnt[prim]
            for (c, a, _v) in extra:
                self.pend.append((eng, (lambda e: None), ((prim, pv),), ((c, a),)))
        return after

    def dma(self, eng, out, in_, inc, waits=(), incs=(), **kw):
        self.ch(inc)
        self.dma_chans[inc] = eng
        return self.op(
            eng,
            (lambda e, out=out, in_=in_, kw=kw: e.dma_start(out=out, in_=in_, **kw)),
            waits, inc, 16, incs)

    def raw(self, eng, fn, waits=(), bump=()):
        for (c, a) in bump:
            self.ch(c)
            self.cnt[c] += a
        self.pend.append((eng, (lambda e, fn=fn: fn(e, self.sems)), tuple(waits), ()))
        return {c: self.cnt[c] for (c, _a) in bump}

    def noop(self, eng, waits=(), inc=None):
        return self.op(eng, (lambda e: None), waits, inc, 1)

    def barrier(self):
        for c, eng in list(self.dma_chans.items()):
            self.noop(eng, waits=[(c, self.cnt[c])])
        self.ch("bar")
        for eng in ENGINES:
            self.noop(eng, inc="bar")
        tot = self.cnt["bar"]
        for eng in ENGINES:
            self.noop(eng, waits=[("bar", tot)])

    def flush(self, block):
        groups = {}
        order = []
        for item in self.pend:
            if item[0] not in groups:
                groups[item[0]] = []
                order.append(item[0])
            groups[item[0]].append(item)
        for eng in order:
            items = groups[eng]

            def body(engine, items=items, eng=eng):
                for (_, fn, waits, incs) in items:
                    for (c, v) in waits:
                        key = (eng, c)
                        if self._waited.get(key, -1) >= v:
                            continue
                        engine.wait_ge(self.sems[c], v)
                        self._waited[key] = v
                    ins = fn(engine)
                    for (c, a) in incs:
                        if ins is None:
                            engine.sem_inc(self.sems[c], a)
                        else:
                            ins = ins.then_inc(self.sems[c], a)
            getattr(block, eng)(body)
        self.pend = []


def _prep_core_inputs(inputs):
    f = np.float32
    ac = np.ascontiguousarray
    W = inputs
    ip = np.asarray(W["L0_attn_in_proj_w"], f)
    ip_re = np.concatenate([ip[4096:10240], ip[10240:10272], ip[0:4096]], axis=0)
    shared = {
        "w_ip": ac(ip_re.T),
        "w_op": ac(np.asarray(W["L0_attn_out_proj_w"], f).T),
        "convw": ac(np.asarray(W["L0_attn_conv_w"], f)[:, 0, :].reshape(48, 128, 4).transpose(1, 0, 2)),
        "convb": ac(np.asarray(W["L0_attn_conv_b"], f).reshape(48, 128).T),
        "dtbias": ac(np.asarray(W["L0_attn_dt_bias"], f).reshape(32, 1)),
        "alog": ac(np.asarray(W["L0_attn_A_log"], f).reshape(32, 1)),
        "dvec": ac(np.asarray(W["L0_attn_D"], f).reshape(1, 32)),
        "nw": ac(np.asarray(W["L0_attn_norm_w"], f).reshape(32, 128).T),
        "wrt9": ac(np.concatenate([np.asarray(W["L0_ffn_router_weight"], f),
                                   np.asarray(W["L0_ffn_shared_gate_w"], f)], axis=0).T),
        "wgu": ac(np.asarray(W["L0_ffn_gate_up_proj"], f).transpose(0, 2, 1)),
        "wdn": ac(np.asarray(W["L0_ffn_down_proj"], f).transpose(0, 2, 1)),
        "wshg": ac(np.asarray(W["L0_ffn_sh_gate_w"], f).T),
        "wshu": ac(np.asarray(W["L0_ffn_sh_up_w"], f).T),
        "wshd": ac(np.asarray(W["L0_ffn_sh_down_w"], f).T),
    }
    ident = np.eye(128, dtype=f)
    sel9 = np.zeros((9, 9 * 128), f)
    for e in range(9):
        sel9[e, e * 128:(e + 1) * 128] = 1.0
    sel32 = np.zeros((32, 32 * 128), f)
    for h in range(32):
        sel32[h, h * 128:(h + 1) * 128] = 1.0
    maskadd = np.zeros((2, 128, 256), f)
    for k in range(2):
        for p in range(128):
            maskadd[k, p, : 128 * k + p] = NEGC
    shared.update(ident=ident, sel9=sel9, sel32=sel32, maskadd=maskadd)

    x = np.asarray(W["L0_attn_x"], f)
    token = np.int32(np.random.randint(1, 2 ** 30))
    maps = []
    for c in range(8):
        bidx, half = c // 2, c % 2
        m = dict(shared)
        m["x_fm"] = ac(x[bidx, half * T:(half + 1) * T, :].T)
        m["parity"] = np.full((1, 1), half, np.int32)
        m["token"] = np.full((1, 1), token, np.int32)
        m["maskcol"] = np.full((128, 1), float(half), f)
        maps.append(m)
    return maps


class Prog:
    def __init__(self, nc, es, stages=("ffn1", "mamba", "ffn2")):
        self.nc = nc
        self.es = es
        self.stages = stages
        self.b = KB(nc, es)
        self.t = {}
        self.uid = 0
        self.block = None

    def u(self, base):
        self.uid += 1
        return f"{base}_{self.uid}"

    def uc(self, base):
        return base

    def declare_dram(self):
        nc = self.nc
        D = {}

        def din(name, shape, dt=F32):
            D[name] = nc.dram_tensor(name, shape, dt, kind="ExternalInput").ap()
        din("x_fm", [E, T]); din("w_ip", [E, 10272]); din("w_op", [IATT, E])
        din("convw", [128, 48, 4]); din("convb", [128, 48]); din("dtbias", [32, 1])
        din("alog", [32, 1]); din("dvec", [1, 32]); din("nw", [128, 32])
        din("wrt9", [E, 9]); din("wgu", [8, E, 2816]); din("wdn", [8, IMOE, E])
        din("wshg", [E, 5632]); din("wshu", [E, 5632]); din("wshd", [5632, E])
        din("ident", [128, 128]); din("sel9", [9, 9 * 128]); din("sel32", [32, 32 * 128])
        din("maskadd", [2, 128, 256]); din("parity", [1, 1], I32); din("token", [1, 1], I32)
        din("maskcol", [128, 1])
        D["out_fm"] = nc.dram_tensor("out_fm", [E, T], F32, kind="ExternalOutput").ap()
        import os as _os
        _dbg = "ExternalOutput" if _os.environ.get("MK_DBG") else "Internal"
        D["r1_d"] = nc.dram_tensor("r1_d", [E, T], F32, kind=_dbg).ap()
        D["r2_d"] = nc.dram_tensor("r2_d", [E, T], F32, kind="Internal").ap()
        D["hbc_d"] = nc.dram_tensor("hbc_d", [CD, T], BF16, kind=_dbg).ap()
        D["y_d"] = nc.dram_tensor("y_d", [IATT, T], BF16, kind=_dbg).ap()
        D["shst"] = nc.dram_tensor("shst", [2, 128, 32, 128], F32, kind="Internal",
                                   addr_space="Shared").ap()
        D["shr1t"] = nc.dram_tensor("shr1t", [2, 128, 16, 3], F32, kind="Internal",
                                    addr_space="Shared").ap()
        D["shflag"] = nc.dram_tensor("shflag", [2, 2, 1, 16], I32, kind="Internal",
                                     addr_space="Shared").ap()
        self.d = D

    def alloc_global(self):
        nc, es, t = self.nc, self.es, self.t

        def sb(n, s, dt):
            t[n] = es.enter_context(nc.sbuf_tensor(n, s, dt))
        sb("identf", [128, 128], F32)
        sb("sel9b", [9, 9 * 128], BF16)
        sb("wrt9s", [128, 16, 9], BF16)
        sb("onesb", [128, 1], BF16)
        sb("ones1f", [1, 128], F32)
        sb("paritys", [1, 1], I32)
        sb("tokens", [1, 1], I32)
        sb("maskcols", [128, 1], F32)
        sb("flagsb", [1, 16], I32)
        sb("tailacc", [128, 16, 3], F32)
        for i in range(8):
            t[f"pb{i}"] = es.enter_context(nc.psum_tensor(f"pb{i}", [128, 512], F32))

    def flush(self):
        self.b.flush(self.block)

    def emit_preamble(self):
        b, d, t = self.b, self.d, self.t
        b.dma("gpsimd", t["sel9b"][:, :], d["sel9"], "cl")
        b.dma("gpsimd", t["wrt9s"][:, :, :], d["wrt9"].rearrange("(c p) n -> p c n", p=128), "cl")
        b.dma("sync", t["identf"][:, :], d["ident"], "cf")
        b.dma("sync", t["paritys"][:, :], d["parity"], "cf")
        b.dma("sync", t["tokens"][:, :], d["token"], "cf")
        b.dma("sync", t["maskcols"][:, :], d["maskcol"], "cf")
        wf = b.val("cf")
        b.op("vector", lambda e: e.memset(t["onesb"][:, :], 1.0), waits=[("cf", wf), ("cl", b.val("cl"))], inc="pre")
        b.op("vector", lambda e: e.memset(t["ones1f"][:, :], 1.0), inc="pre")
        b.op("vector", lambda e: e.memset(t["flagsb"][:, :], -1), inc="pre")
        b.barrier()
        self.flush()

    def chain(self, eng, fn, ch, extra=(), incs=()):
        b = self.b
        return b.op(eng, fn, waits=[(ch, b.val(ch))] + list(extra), inc=ch, incs=incs)

    def poll_flag(self, stage):
        b, t, d = self.b, self.t, self.d
        nc = self.nc
        pol = b.ch(f"pol{stage}")
        pdn = b.ch("poldone")
        fsb = t["flagsb"]

        def fn(e, sems):
            with e.register(self.u("pr")) as pr, e.register(self.u("df")) as df, \
                    e.register(self.u("cn")) as cn, e.register(self.u("tk")) as tk:
                e.reg_load(pr, t["paritys"][0:1, 0:1])
                e.reg_load(tk, t["tokens"][0:1, 0:1])
                e.reg_mov(df, 1)
                e.reg_mov(cn, 0)
                with e.While(df):
                    e.reg_add(cn, cn, 16)
                    for i in nc.Switch(engines=[e], index=[e.compute_val(pr)], n=2):
                        e.dma_start(out=fsb[:, :], in_=d["shflag"][1 - i, stage]).then_inc(sems[pol], 16)
                    e.wait_ge(sems[pol], cn)
                    e.reg_load(df, fsb[0:1, 0:1])
                    e.reg_sub(df, df, tk)
            e.sem_inc(sems[pdn], 1)
        self.b.raw("sync", fn, waits=(), bump=((pdn, 1),))
        return pdn, b.val(pdn)

    def slot_write(self, dst_slot_fn, src_ap, inc, waits):
        b, t = self.b, self.t
        nc = self.nc
        b.ch(inc)
        b.dma_chans[inc] = "sync"

        def fn(e, sems):
            with e.register(self.u("pr")) as pr:
                e.reg_load(pr, t["paritys"][0:1, 0:1])
                for i in nc.Switch(engines=[e], index=[e.compute_val(pr)], n=2):
                    e.dma_start(out=dst_slot_fn(i), in_=src_ap).then_inc(sems[inc], 16)
        self.b.raw("sync", fn, waits=waits, bump=((inc, 16),))
        return b.val(inc)

    def slot_read(self, eng, dst_ap, src_slot_fn, inc, waits):
        b, t = self.b, self.t
        nc = self.nc
        b.ch(inc)
        b.dma_chans[inc] = eng

        def fn(e, sems):
            with e.register(self.u("pr")) as pr:
                e.reg_load(pr, t["paritys"][0:1, 0:1])
                for i in nc.Switch(engines=[e], index=[e.compute_val(pr)], n=2):
                    e.dma_start(out=dst_ap, in_=src_slot_fn(1 - i)).then_inc(sems[inc], 16)
        self.b.raw(eng, fn, waits=waits, bump=((inc, 16),))
        return b.val(inc)

    def emit_ln(self, src_d, half, hbuf, ph, stats_save=None):
        b, t = self.b, self.t
        rt = ph["rtile"]
        sq = ph["sqt"]
        c_r = self.uc("lnr"); c_rf = self.uc("lnrf")
        c_c = self.uc("lnc"); c_sq = self.uc("lnq"); c_qf = self.uc("lnqf")
        c_mm = self.uc("lnm"); c_row = self.uc("lnw"); c_bc = self.uc("lnb"); c_be = self.uc("lne")
        c_h = self.uc("lnh")
        t0 = half * TH
        lastcast = 0
        for ec in range(NEC):
            w = [(c_rf, b.val(c_rf) - 1)] if ec >= 2 else []
            lv = b.dma("sync", rt[ec % 2][:, :], src_d[ec * 128:(ec + 1) * 128, t0:t0 + TH], c_r, waits=w)
            cv = b.op("vector", lambda e, ec=ec: e.tensor_copy(hbuf[:, ec, :], rt[ec % 2][:, :]),
                      waits=[(c_r, lv)], inc=c_c, incs=((c_rf, 1),))
            lastcast = cv
            wq = [(c_qf, b.val(c_qf) - 1)] if ec >= 2 else []
            sv = b.op("scalar", lambda e, ec=ec: e.activation(sq[ec % 2][:, :], hbuf[:, ec, :], AF.Square),
                      waits=[(c_c, cv)] + wq, inc=c_sq)
            for tc in range(2):
                b.op("tensor",
                     lambda e, ec=ec, tc=tc: e.matmul(t[f"pb{tc}"][0:1, 0:512], t["onesb"][:, :],
                                                      hbuf[:, ec, tc * 512:(tc + 1) * 512],
                                                      start=(ec == 0), stop=(ec == NEC - 1)),
                     waits=[(c_c, cv)], inc=(c_mm if ec == NEC - 1 else None))
                b.op("tensor",
                     lambda e, ec=ec, tc=tc: e.matmul(t[f"pb{2 + tc}"][0:1, 0:512], t["onesb"][:, :],
                                                      sq[ec % 2][:, tc * 512:(tc + 1) * 512],
                                                      start=(ec == 0), stop=(ec == NEC - 1)),
                     waits=[(c_sq, sv)], inc=(c_mm if ec == NEC - 1 else None),
                     incs=(((c_qf, 1),) if tc == 1 else ()))
        mmv = b.val(c_mm)
        mu, msq, var = ph["rowa"], ph["rowb"], ph["rowc"]
        for tc in range(2):
            s = slice(tc * 512, (tc + 1) * 512)
            b.op("vector", lambda e, tc=tc, s=s: e.tensor_scalar(mu[0:1, s], t[f"pb{tc}"][0:1, 0:512],
                                                                 1.0 / E, None, ALU.mult),
                 waits=[(c_mm, mmv)], inc=c_row)
            b.op("vector", lambda e, tc=tc, s=s: e.tensor_scalar(msq[0:1, s], t[f"pb{2 + tc}"][0:1, 0:512],
                                                                 1.0 / E, None, ALU.mult), inc=c_row)
        self.chain("vector", lambda e: e.tensor_mul(var[0:1, :], mu[0:1, :], mu[0:1, :]), c_row)
        self.chain("vector", lambda e: e.tensor_sub(var[0:1, :], msq[0:1, :], var[0:1, :]), c_row)
        self.chain("vector", lambda e: e.tensor_scalar(var[0:1, :], var[0:1, :], LNEPS, None, ALU.add), c_row)
        vv = b.val(c_row)
        b.op("scalar", lambda e: e.activation(var[0:1, :], var[0:1, :], AF.Sqrt),
             waits=[(c_row, vv)], inc=c_row)
        self.chain("vector", lambda e: e.reciprocal(var[0:1, :], var[0:1, :]), c_row)
        self.chain("vector", lambda e: e.tensor_mul(mu[0:1, :], mu[0:1, :], var[0:1, :]), c_row)
        if stats_save is not None:
            mus_s, rstd_s = stats_save
            self.chain("vector", lambda e: e.tensor_copy(mus_s, mu[0:1, :]), c_row)
            self.chain("vector", lambda e: e.tensor_copy(rstd_s, var[0:1, :]), c_row)
        rowv = b.val(c_row)
        for tc in range(2):
            s = slice(tc * 512, (tc + 1) * 512)
            b.op("tensor", lambda e, tc=tc, s=s: e.matmul(t[f"pb{4 + tc}"][:, 0:512], t["ones1f"][:, :],
                                                          var[0:1, s], start=True, stop=True),
                 waits=[(c_row, rowv)], inc=c_bc)
            b.op("tensor", lambda e, tc=tc, s=s: e.matmul(t[f"pb{6 + tc}"][:, 0:512], t["ones1f"][:, :],
                                                          mu[0:1, s], start=True, stop=True),
                 waits=[(c_row, rowv)], inc=c_bc)
        bcv = b.val(c_bc)
        rbc, mbc = ph["rstdbc"], ph["musbc"]
        for tc in range(2):
            s = slice(tc * 512, (tc + 1) * 512)
            b.op("vector", lambda e, tc=tc, s=s: e.tensor_copy(rbc[:, s], t[f"pb{4 + tc}"][:, 0:512]),
                 waits=[(c_bc, bcv)], inc=c_be)
            b.op("vector", lambda e, tc=tc, s=s: e.tensor_copy(mbc[:, s], t[f"pb{6 + tc}"][:, 0:512]), inc=c_be)
        bev = b.val(c_be)
        tt = ph["lntmp"]
        for ec in range(NEC):
            b.op("vector", lambda e, ec=ec: e.tensor_mul(tt[:, :], hbuf[:, ec, :], rbc[:, :]),
                 waits=[(c_be, bev), (c_c, lastcast)], inc=c_h)
            b.op("vector", lambda e, ec=ec: e.tensor_sub(hbuf[:, ec, :], tt[:, :], mbc[:, :]), inc=c_h)
        return c_h, b.val(c_h)

    def emit_ffn(self, src_d, dst_d, capture_tail=False):
        nc, b, t, d = self.nc, self.b, self.t, self.d
        pes = ExitStack()

        def sb(n, s, dt):
            return pes.enter_context(nc.sbuf_tensor(self.u(n), s, dt))
        ph = {
            "rtile": [sb("fr", [128, TH], F32) for _ in range(2)],
            "sqt": [sb("fq", [128, TH], BF16) for _ in range(2)],
            "rowa": sb("fra", [1, TH], F32),
            "rowb": sb("frb", [1, TH], F32),
            "rowc": sb("frc", [1, TH], F32),
            "rstdbc": sb("frs", [128, TH], BF16),
            "musbc": sb("fmu", [128, TH], BF16),
            "lntmp": sb("flt", [128, TH], BF16),
        }
        hbuf = sb("fh", [128, NEC, TH], BF16)
        acc = sb("fac", [128, 16, TH], F32)
        gu2 = sb("fg2", [128, NKD, TH], BF16)
        w1 = [sb("fw1", [128, 16, 256], BF16) for _ in range(2)]
        w2 = [sb("fw2", [128, NKD, 128], BF16) for _ in range(2)]
        wbc = sb("fwb", [128, TH], BF16)
        gtmp = [sb("fgt", [128, 512], BF16) for _ in range(2)]
        gut = [sb("fgu", [128, 512], BF16) for _ in range(2)]
        wrowb = sb("fwo", [9, TH], BF16)
        ltok = sb("flk", [128, 9], F32)
        m1c = sb("fm1", [128, 1], F32)
        m2c = sb("fm2", [128, 1], F32)
        dcol = sb("fdc", [128, 1], F32)
        c2c = sb("fc2", [128, 1], F32)
        eq1 = sb("fe1", [128, 9], F32)
        eq2 = sb("fe2", [128, 9], F32)
        mskt = sb("fmk", [128, 9], F32)
        wtok = sb("fwt", [128, 9], F32)

        c_w1 = self.uc("w1"); c_w1f = self.uc("w1f")
        c_w2 = self.uc("w2"); c_w2f = self.uc("w2f")
        c_mm = self.uc("fmm")
        c_ep = self.uc("fep")
        c_ev = self.uc("fev")
        c_gu2 = self.uc("fgu2")
        c_fin = self.uc("ffin")
        c_sto = self.uc("fsto")
        c_rt = self.uc("frt")
        c_tp = self.uc("ftp")
        nw1 = 0
        nw2 = 0
        base1 = b.val(c_w1f)
        base2 = b.val(c_w2f)
        for half in range(2):
            t0 = half * TH
            c_h, hv = self.emit_ln(src_d, half, hbuf, ph)
            zv = b.op("vector", lambda e: e.memset(acc[:, :, :], 0.0), waits=[(c_h, hv)], inc=c_ev)

            tpv = 0
            for tc8 in range(8):
                for k in range(NEC):
                    b.op("tensor",
                         lambda e, tc8=tc8, k=k: e.matmul(
                             t["pb6"][:, 0:9], hbuf[:, k, tc8 * 128:(tc8 + 1) * 128],
                             t["wrt9s"][:, k, :], start=(k == 0), stop=(k == NEC - 1)),
                         waits=([(c_h, hv)] if (tc8 == 0 and k == 0) else []) +
                               ([(c_rt, b.val(c_rt))] if k == 0 else []),
                         inc=(c_mm if k == NEC - 1 else None))
                mv = b.val(c_mm)
                st = self.uc("t2")
                b.op("vector", lambda e: e.tensor_copy(ltok[:, :], t["pb6"][:, 0:9]),
                     waits=[(c_mm, mv)], inc=st)
                b.noop("vector", waits=[(st, b.val(st))], inc=c_rt)
                self.chain("vector", lambda e: e.reduce_max(m1c[:, :], ltok[:, 0:8], axis=AX.X), st)
                self.chain("vector", lambda e: e.tensor_scalar(eq1[:, 0:8], ltok[:, 0:8], m1c[:, 0:1], None, ALU.is_equal), st)
                self.chain("vector", lambda e: e.scalar_tensor_tensor(mskt[:, 0:8], eq1[:, 0:8], NEGC, ltok[:, 0:8], ALU.mult, ALU.add), st)
                self.chain("vector", lambda e: e.reduce_max(m2c[:, :], mskt[:, 0:8], axis=AX.X), st)
                self.chain("vector", lambda e: e.tensor_sub(dcol[:, :], m1c[:, :], m2c[:, :]), st)
                sv = b.op("scalar", lambda e: e.activation(dcol[:, :], dcol[:, :], AF.Sigmoid),
                          waits=[(st, b.val(st))], inc=st)
                b.op("scalar", lambda e: e.activation(wtok[:, 8:9], ltok[:, 8:9], AF.Sigmoid), inc=st)
                self.chain("vector", lambda e: e.tensor_scalar(c2c[:, :], dcol[:, :], -1.0, 1.0, ALU.mult, ALU.add), st)
                self.chain("vector", lambda e: e.tensor_scalar(eq1[:, 0:8], eq1[:, 0:8], dcol[:, 0:1], None, ALU.mult), st)
                self.chain("vector", lambda e: e.tensor_scalar(eq2[:, 0:8], mskt[:, 0:8], m2c[:, 0:1], None, ALU.is_equal), st)
                self.chain("vector", lambda e: e.tensor_scalar(eq2[:, 0:8], eq2[:, 0:8], c2c[:, 0:1], None, ALU.mult), st)
                self.chain("vector", lambda e: e.tensor_add(wtok[:, 0:8], eq1[:, 0:8], eq2[:, 0:8]), st)
                wv = b.val(st)
                tv = b.op("tensor", lambda e: e.transpose(t["pb7"][0:9, 0:128], wtok[:, :], t["identf"][:, :]),
                          waits=[(st, wv), (c_tp, tpv)], inc=c_mm)
                tpv = b.op("vector",
                           lambda e, tc8=tc8: e.tensor_copy(wrowb[:, tc8 * 128:(tc8 + 1) * 128], t["pb7"][0:9, 0:128]),
                           waits=[(c_mm, tv)], inc=c_tp)
            wrv = tpv

            dn_rel = [zv, zv]
            g_rel = [0, 0]
            u_rel = [0, 0]
            gt_rel = [0, 0]
            wb_rel = 0
            for ex in range(NEXP):
                e9 = ex if ex < 8 else 8
                wbv = []
                for tc in range(2):
                    v = b.op("tensor",
                             lambda e, tc=tc, e9=e9: e.matmul(
                                 t["pb6"][:, 0:512], t["sel9b"][:, e9 * 128:(e9 + 1) * 128],
                                 wrowb[:, tc * 512:(tc + 1) * 512], start=True, stop=True),
                             waits=[(c_tp, wrv), (c_ev, wb_rel), (c_rt, b.val(c_rt))], inc=c_mm)
                    wb_rel = b.op("vector",
                                  lambda e, tc=tc: e.tensor_copy(wbc[:, tc * 512:(tc + 1) * 512], t["pb6"][:, 0:512]),
                                  waits=[(c_mm, v)], inc=c_ev)
                    wbv.append(wb_rel)

                if ex < 8:
                    gsrc = lambda fb, ex=ex: d["wgu"][ex][:, fb * 128:(fb + 1) * 128]
                    usrc = lambda fb, ex=ex: d["wgu"][ex][:, 1408 + fb * 128:1408 + (fb + 1) * 128]
                else:
                    q = ex - 8
                    gsrc = lambda fb, q=q: d["wshg"][:, q * 1408 + fb * 128:q * 1408 + (fb + 1) * 128]
                    usrc = lambda fb, q=q: d["wshu"][:, q * 1408 + fb * 128:q * 1408 + (fb + 1) * 128]
                guv = 0
                for fb in range(NFB):
                    s = nw1 % 2
                    wfree = [(c_w1f, base1 + nw1 - 1)] if nw1 >= 2 else []
                    lv1 = b.dma("gpsimd", w1[s][:, :, 0:128],
                                gsrc(fb).rearrange("(c p) f -> p c f", p=128), c_w1, waits=wfree)
                    lv2 = b.dma("gpsimd", w1[s][:, :, 128:256],
                                usrc(fb).rearrange("(c p) f -> p c f", p=128), c_w1)
                    nw1 += 1
                    mmg = [0, 0]
                    mmu = [0, 0]
                    for k in range(NEC):
                        for tc in range(2):
                            mmg[tc] = b.op("tensor",
                                 lambda e, s=s, k=k, tc=tc: e.matmul(
                                     t[f"pb{tc}"][:, 0:512], w1[s][:, k, 0:128],
                                     hbuf[:, k, tc * 512:(tc + 1) * 512],
                                     start=(k == 0), stop=(k == NEC - 1)),
                                 waits=([(c_w1, lv2), (c_ep, g_rel[tc]), (c_h, hv)] if k == 0 else []),
                                 inc=(c_mm if k == NEC - 1 else None))
                    for k in range(NEC):
                        for tc in range(2):
                            mmu[tc] = b.op("tensor",
                                 lambda e, s=s, k=k, tc=tc: e.matmul(
                                     t[f"pb{2 + tc}"][:, 0:512], w1[s][:, k, 128:256],
                                     hbuf[:, k, tc * 512:(tc + 1) * 512],
                                     start=(k == 0), stop=(k == NEC - 1)),
                                 waits=([(c_ev, u_rel[tc])] if k == 0 else []),
                                 inc=(c_mm if k == NEC - 1 else None),
                                 incs=(((c_w1f, 1),) if (k == NEC - 1 and tc == 1) else ()))
                    for tc in range(2):
                        g_rel[tc] = b.op("scalar",
                             lambda e, tc=tc: e.activation(gtmp[tc][:, :], t[f"pb{tc}"][:, 0:512], AF.Silu),
                             waits=[(c_mm, mmg[tc]), (c_ev, gt_rel[tc])], inc=c_ep)
                        u_rel[tc] = b.op("vector",
                             lambda e, tc=tc: e.tensor_mul(gut[tc][:, :], gtmp[tc][:, :], t[f"pb{2 + tc}"][:, 0:512]),
                             waits=[(c_ep, g_rel[tc]), (c_mm, mmu[tc])], inc=c_ev)
                        gt_rel[tc] = u_rel[tc]
                        guv = b.op("vector",
                             lambda e, fb=fb, tc=tc: e.tensor_mul(gu2[:, fb, tc * 512:(tc + 1) * 512],
                                                                  gut[tc][:, :], wbc[:, tc * 512:(tc + 1) * 512]),
                             waits=[(c_ev, wbv[tc])], inc=c_gu2)

                if ex < 8:
                    dsrc = lambda hb, ex=ex: d["wdn"][ex][:, hb * 128:(hb + 1) * 128]
                else:
                    q = ex - 8
                    dsrc = lambda hb, q=q: d["wshd"][q * 1408:(q + 1) * 1408, hb * 128:(hb + 1) * 128]
                for hb in range(16):
                    s = nw2 % 2
                    wfree = [(c_w2f, base2 + nw2 - 1)] if nw2 >= 2 else []
                    lv = b.dma("gpsimd", w2[s][:, :, :],
                               dsrc(hb).rearrange("(c p) f -> p c f", p=128), c_w2, waits=wfree)
                    nw2 += 1
                    for tc in range(2):
                        mmd = 0
                        for k in range(NKD):
                            mmd = b.op("tensor",
                                 lambda e, s=s, k=k, tc=tc: e.matmul(
                                     t[f"pb{4 + tc}"][:, 0:512], w2[s][:, k, :],
                                     gu2[:, k, tc * 512:(tc + 1) * 512],
                                     start=(k == 0), stop=(k == NKD - 1)),
                                 waits=([(c_w2, lv), (c_gu2, guv), (c_ev, dn_rel[tc])] if k == 0 else []),
                                 inc=(c_mm if k == NKD - 1 else None),
                                 incs=(((c_w2f, 1),) if (k == NKD - 1 and tc == 1) else ()))
                        dn_rel[tc] = b.op("vector",
                             lambda e, hb=hb, tc=tc: e.tensor_add(acc[:, hb, tc * 512:(tc + 1) * 512],
                                                                  acc[:, hb, tc * 512:(tc + 1) * 512],
                                                                  t[f"pb{4 + tc}"][:, 0:512]),
                             waits=[(c_mm, mmd)], inc=c_ev)

            c_fr = self.uc("ffr")
            c_frf = self.uc("ffrf")
            for hb in range(16):
                w = [(c_frf, b.val(c_frf) - 1)] if hb >= 2 else []
                w += [(c_sto, b.val(c_sto) - 16)] if hb >= 2 else []
                lv = b.dma("sync", ph["rtile"][hb % 2][:, :],
                           src_d[hb * 128:(hb + 1) * 128, t0:t0 + TH], c_fr, waits=w)
                fv = b.op("vector",
                          lambda e, hb=hb: e.scalar_tensor_tensor(ph["rtile"][hb % 2][:, :], acc[:, hb, :], 0.5,
                                                                  ph["rtile"][hb % 2][:, :], ALU.mult, ALU.add),
                          waits=[(c_fr, lv), (c_ev, dn_rel[1])], inc=c_fin,
                          incs=((c_frf, 1),))
                if capture_tail and half == 1:
                    fv = b.op("vector",
                              lambda e, hb=hb: e.tensor_copy(t["tailacc"][:, hb, :], ph["rtile"][hb % 2][:, 1021:1024]),
                              waits=[(c_fin, fv)], inc=c_fin)
                b.dma("sync", dst_d[hb * 128:(hb + 1) * 128, t0:t0 + TH], ph["rtile"][hb % 2][:, :], c_sto,
                      waits=[(c_fin, fv)])
        if capture_tail:
            stv = self.slot_write(lambda i: d["shr1t"][i], t["tailacc"][:, :, :], "sht",
                                  waits=[(c_fin, b.val(c_fin))])
            self.slot_write(lambda i: d["shflag"][i, 0][0:1, 0:1], t["tokens"][0:1, 0:1], "shf0",
                            waits=[("sht", stv)])
        b.noop("sync", waits=[(c_sto, b.val(c_sto))])
        self.b.barrier()
        self.flush()
        pes.close()



    def emit_mamba(self, src_d, dst_d):
        nc, b, t, d = self.nc, self.b, self.t, self.d
        pm = ExitStack()

        def sm(n, s_, dt):
            return pm.enter_context(nc.sbuf_tensor(self.u(n), s_, dt))
        rep32b = sm("rep32b", [32, 32 * 128], BF16)
        convws = sm("convws", [128, 48, 4], F32)
        convbs = sm("convbs", [128, 48], F32)
        dtbiass = sm("dtbiass", [32, 1], F32)
        acol = sm("acol", [32, 1], F32)
        dbc = sm("dbc", [128, 32], F32)
        nws = sm("nws", [128, 32], F32)
        h3s = sm("h3s", [128, 16, 3], BF16)
        r1ts = sm("r1ts", [128, 16, 3], F32)
        sq3 = sm("sq3", [128, 3], BF16)
        dts = sm("dts", [32, 8, 256], F32)
        eacumb = sm("eacumb", [32, 8, 256], BF16)
        lambc = sm("lambc", [128, 32, 8], F32)
        sinb = sm("sinb", [128, 32, 128], BF16)
        preK = sm("preK", [128, 48, 3], BF16)
        row3 = sm("row3", [1, 12], F32)

        b.dma("gpsimd", rep32b[:, :], d["sel32"], "mcl")
        b.dma("sync", convws[:, :, :], d["convw"], "mcf")
        b.dma("sync", convbs[:, :], d["convb"], "mcf")
        b.dma("sync", dtbiass[:, :], d["dtbias"], "mcf")
        b.dma("sync", acol[:, :], d["alog"], "mcf")
        b.dma("sync", nws[:, :], d["nw"], "mcf")
        b.dma("sync", dbc[0:1, :], d["dvec"], "mcf")
        wf = b.val("mcf")
        b.op("scalar", lambda e: e.activation(acol[:, :], acol[:, :], AF.Exp),
             waits=[("mcf", wf)], inc="mpre")
        b.op("vector", lambda e: e.tensor_scalar(acol[:, :], acol[:, :], -1.0, None, ALU.mult),
             waits=[("mpre", b.val("mpre"))], inc="mpre")
        b.op("tensor",
             lambda e: e.matmul(t["pb7"][:, 0:32], t["ones1f"][:, :], dbc[0:1, :], start=True, stop=True),
             waits=[("mcf", wf), ("mpre", b.val("mpre"))], inc="mpre")
        b.op("vector", lambda e: e.tensor_copy(dbc[:, :], t["pb7"][:, 0:32]),
             waits=[("mpre", b.val("mpre"))], inc="mpre")
        b.barrier()

        if MK_TRUNC == "consts":
            self.flush(); pm.close(); return

        pa = ExitStack()

        def sa(n, s_, dt):
            return pa.enter_context(nc.sbuf_tensor(self.u(n), s_, dt))
        ph = {
            "rtile": [sa("mr", [128, TH], F32) for _ in range(2)],
            "sqt": [sa("mq", [128, TH], BF16) for _ in range(2)],
            "rowa": sa("mra", [1, TH], F32),
            "rowb": sa("mrb", [1, TH], F32),
            "rowc": sa("mrc", [1, TH], F32),
            "rstdbc": sa("mrs", [128, TH], BF16),
            "musbc": sa("mmu", [128, TH], BF16),
            "lntmp": sa("mlt", [128, TH], BF16),
        }
        hbuf = sa("mh", [128, NEC, TH], BF16)
        w1 = [sa("mw1", [128, 16, 256], BF16) for _ in range(2)]
        wdt = sa("mwdt", [128, 16, 32], BF16)
        pre = [sa("mpre", [128, 3 + TH], BF16) for _ in range(2)]
        cacc = [sa("mca", [128, TH], F32) for _ in range(2)]
        post = [sa("mpo", [128, TH], BF16) for _ in range(2)]
        tailb = sa("mtb", [128, 48, 3], BF16)
        dtv1 = sa("mdv", [32, 512], F32)
        dta = sa("mda", [32, 512], F32)

        c_w1 = self.uc("mi1"); c_w1f = self.uc("mif")
        c_ev = self.uc("mie")
        c_mm = self.uc("mim")
        c_sil = self.uc("mis")
        c_cv = self.uc("micv")
        c_st = self.uc("mist")
        c_tev = self.uc("mtev")
        c_dt = self.uc("mdt")
        ldv = b.dma("gpsimd", wdt[:, :, :], d["w_ip"][:, 6144:6176].rearrange("(c p) f -> p c f", p=128), c_w1)
        nip = 0
        prel = {0: 0, 1: 0, 2: 0, 3: 0}
        t4rel = 0
        silrel = [0, 0]
        strel = [0, 0]
        for half in range(2):
            t0 = half * TH
            c_h, hv = self.emit_ln(src_d, half, hbuf, ph)
            for fb in range(24):
                s = nip % 2
                wfree = [(c_w1f, nip - 1)] if nip >= 2 else []
                lv = b.dma("gpsimd", w1[s][:, :, :],
                           d["w_ip"][:, fb * 256:(fb + 1) * 256].rearrange("(c p) f -> p c f", p=128),
                           c_w1, waits=wfree)
                nip += 1
                for fc2 in range(2):
                    fc = fb * 2 + fc2
                    mmv = [0, 0]
                    for k in range(NEC):
                        for tc in range(2):
                            bank = fc2 * 2 + tc
                            mmv[tc] = b.op("tensor",
                                 lambda e, s=s, k=k, tc=tc, fc2=fc2, bank=bank: e.matmul(
                                     t[f"pb{bank}"][:, 0:512], w1[s][:, k, fc2 * 128:(fc2 + 1) * 128],
                                     hbuf[:, k, tc * 512:(tc + 1) * 512],
                                     start=(k == 0), stop=(k == NEC - 1)),
                                 waits=([(c_w1, lv), (c_h, hv), (c_ev, prel[bank])] if k == 0 else []),
                                 inc=(c_mm if k == NEC - 1 else None),
                                 incs=(((c_w1f, 1),) if (k == NEC - 1 and tc == 1 and fc2 == 1) else ()))
                    if half == 0:
                        b.op("vector", lambda e, s=s: e.memset(pre[s][:, 0:3], 0.0), inc=c_tev)
                    else:
                        b.op("vector", lambda e, s=s, fc=fc: e.tensor_copy(pre[s][:, 0:3], tailb[:, fc, :]),
                             inc=c_tev)
                    for tc in range(2):
                        bank = fc2 * 2 + tc
                        prel[bank] = b.op("vector",
                             lambda e, s=s, tc=tc, bank=bank: e.tensor_copy(
                                 pre[s][:, 3 + tc * 512: 3 + (tc + 1) * 512], t[f"pb{bank}"][:, 0:512]),
                             waits=[(c_mm, mmv[tc])], inc=c_ev)
                    ca = fc % 2
                    if "conv" in MK_P2A:
                        b.op("vector", lambda e, s=s, ca=ca, fc=fc: e.tensor_scalar(
                                 cacc[ca][:, :], pre[s][:, 0:TH], convws[:, fc, 0:1], None, ALU.mult),
                             waits=[(c_sil, silrel[ca])], inc=c_cv)
                        for j in (1, 2, 3):
                            b.op("vector", lambda e, s=s, ca=ca, fc=fc, j=j: e.scalar_tensor_tensor(
                                     cacc[ca][:, :], pre[s][:, j:j + TH], convws[:, fc, j:j + 1],
                                     cacc[ca][:, :], ALU.mult, ALU.add), inc=c_cv)
                        if half == 0:
                            b.op("vector", lambda e, s=s, fc=fc: e.tensor_copy(tailb[:, fc, :], pre[s][:, TH:TH + 3]),
                                 inc=c_cv)
                            b.op("vector", lambda e, s=s, fc=fc: e.tensor_copy(preK[:, fc, :], pre[s][:, 3:6]),
                                 inc=c_cv)
                    cvv = b.val(c_cv)
                    if "silu" in MK_P2A:
                        silrel[ca] = b.op("scalar",
                             lambda e, ca=ca, fc=fc: e.activation(post[ca][:, :], cacc[ca][:, :], AF.Silu,
                                                                  bias=convbs[:, fc:fc + 1]),
                             waits=[(c_cv, cvv), (c_st, strel[ca])], inc=c_sil)
                    if "store" in MK_P2A:
                        strel[ca] = b.dma("sync", d["hbc_d"][fc * 128:(fc + 1) * 128, t0:t0 + TH], post[ca][:, :],
                                          c_st, waits=[(c_sil, silrel[ca])])
            dtrel = 0
            for tc in (range(2) if ("dt" in MK_P2A or "dtmm" in MK_P2A) else []):
                mmv = 0
                for k in range(NEC):
                    mmv = b.op("tensor",
                         lambda e, k=k, tc=tc: e.matmul(t["pb5"][0:32, 0:512], wdt[:, k, :],
                                                        hbuf[:, k, tc * 512:(tc + 1) * 512],
                                                        start=(k == 0), stop=(k == NEC - 1)),
                         waits=([(c_w1, ldv), (c_h, hv), (c_dt, dtrel)] if k == 0 else []),
                         inc=(c_mm if k == NEC - 1 else None))
                if "dt" not in MK_P2A:
                    dtrel = b.op("vector", lambda e: e.tensor_copy(dtv1[:, :], t["pb5"][0:32, 0:512]),
                                 waits=[(c_mm, mmv)], inc=c_dt)
                    continue
                import os as _os
                lvl = int(_os.environ.get("MK_DTLVL", "9"))
                tsv = b.op("vector", lambda e: e.tensor_scalar(dtv1[:, :], t["pb5"][0:32, 0:512],
                                                               dtbiass[:, 0:1], 0.0, ALU.add, ALU.max),
                           waits=[(c_mm, mmv)], inc=c_dt)
                av = b.op("scalar", lambda e: e.activation(dta[:, :], t["pb5"][0:32, 0:512], AF.Abs,
                                                           bias=dtbiass[:, 0:1]),
                          waits=[(c_dt, tsv)], inc=c_dt)
                av = b.op("scalar", lambda e: e.activation(dta[:, :], dta[:, :], AF.Exp, scale=-1.0),
                          waits=[(c_dt, av)], inc=c_dt)
                av = b.op("scalar", lambda e: e.activation(dta[:, :], dta[:, :], AF.Ln, bias=1.0),
                          waits=[(c_dt, av)], inc=c_dt)
                for j in range(2):
                    dtrel = b.op("vector",
                         lambda e, half=half, tc=tc, j=j: e.tensor_add(
                             dts[:, half * 4 + tc * 2 + j, :],
                             dtv1[:, j * 256:(j + 1) * 256], dta[:, j * 256:(j + 1) * 256]),
                         waits=[(c_dt, b.val(c_dt))], inc=c_dt)
        b.barrier()
        self.flush()
        pa.close()
        if MK_TRUNC == "p2a":
            pm.close(); return

        pdn, pdv = self.poll_flag(0)
        rv = self.slot_read("sync", r1ts[:, :, :], lambda i: d["shr1t"][i], "mr1t",
                            waits=[(pdn, pdv)])
        c3 = self.uc("ln3")
        cast3 = b.op("vector", lambda e: e.tensor_copy(h3s[:, :, :], r1ts[:, :, :]),
                     waits=[("mr1t", rv)], inc=c3)
        for k in range(NEC):
            sv = b.op("scalar", lambda e, k=k: e.activation(sq3[:, :], h3s[:, k, :], AF.Square),
                      waits=[(c3, b.val(c3))], inc=c3)
            b.op("tensor", lambda e, k=k: e.matmul(t["pb0"][0:1, 0:3], t["onesb"][:, :], h3s[:, k, :],
                                                   start=(k == 0), stop=(k == NEC - 1)),
                 waits=[(c3, cast3)], inc=c3)
            b.op("tensor", lambda e, k=k: e.matmul(t["pb1"][0:1, 0:3], t["onesb"][:, :], sq3[:, :],
                                                   start=(k == 0), stop=(k == NEC - 1)),
                 waits=[(c3, sv)], inc=c3)
        self.chain("vector", lambda e: e.tensor_scalar(row3[0:1, 0:3], t["pb0"][0:1, 0:3], 1.0 / E, None, ALU.mult), c3)
        self.chain("vector", lambda e: e.tensor_scalar(row3[0:1, 4:7], t["pb1"][0:1, 0:3], 1.0 / E, None, ALU.mult), c3)
        self.chain("vector", lambda e: e.tensor_mul(row3[0:1, 8:11], row3[0:1, 0:3], row3[0:1, 0:3]), c3)
        self.chain("vector", lambda e: e.tensor_sub(row3[0:1, 8:11], row3[0:1, 4:7], row3[0:1, 8:11]), c3)
        self.chain("vector", lambda e: e.tensor_scalar(row3[0:1, 8:11], row3[0:1, 8:11], LNEPS, None, ALU.add), c3)
        sv = b.op("scalar", lambda e: e.activation(row3[0:1, 8:11], row3[0:1, 8:11], AF.Sqrt),
                  waits=[(c3, b.val(c3))], inc=c3)
        self.chain("vector", lambda e: e.reciprocal(row3[0:1, 8:11], row3[0:1, 8:11]), c3)
        self.chain("vector", lambda e: e.tensor_mul(row3[0:1, 0:3], row3[0:1, 0:3], row3[0:1, 8:11]), c3)
        rw3 = b.val(c3)
        b.op("tensor", lambda e: e.matmul(t["pb2"][:, 0:3], t["ones1f"][:, :], row3[0:1, 8:11],
                                          start=True, stop=True), waits=[(c3, rw3)], inc=c3)
        b.op("tensor", lambda e: e.matmul(t["pb3"][:, 0:3], t["ones1f"][:, :], row3[0:1, 0:3],
                                          start=True, stop=True), waits=[(c3, rw3)], inc=c3)
        bc3 = b.val(c3)
        for k in range(NEC):
            b.op("vector", lambda e, k=k: e.tensor_mul(h3s[:, k, :], h3s[:, k, :], t["pb2"][:, 0:3]),
                 waits=[(c3, bc3)], inc=c3)
            b.op("vector", lambda e, k=k: e.tensor_sub(h3s[:, k, :], h3s[:, k, :], t["pb3"][:, 0:3]),
                 inc=c3)
        h3v = b.val(c3)
        pf = ExitStack()

        def sf(n, s_, dt):
            return pf.enter_context(nc.sbuf_tensor(self.u(n), s_, dt))
        w1t = [sf("tw1", [128, 16, 256], BF16) for _ in range(2)]
        tail6 = sf("t6", [128, 48, 6], BF16)
        facc = sf("tfa", [128, 48], F32)
        fm2 = sf("tfm", [128, 48], F32)
        ff = sf("tff", [128, 48, 3], BF16)
        c_tw = self.uc("tw"); c_twf = self.uc("twf")
        c_tmm = self.uc("tmm"); c_tdv = self.uc("tdv")
        c_pat = self.uc("tpat")
        b.op("vector", lambda e: e.tensor_copy(tail6[:, :, 3:6], preK[:, :, :]), inc=c_tdv)
        ntw = 0
        t4r = 0
        for fb in range(24):
            sl = ntw % 2
            wfree = [(c_twf, ntw - 1)] if ntw >= 2 else []
            lv = b.dma("gpsimd", w1t[sl][:, :, :],
                       d["w_ip"][:, fb * 256:(fb + 1) * 256].rearrange("(c p) f -> p c f", p=128),
                       c_tw, waits=wfree)
            ntw += 1
            for fc2 in range(2):
                fc = fb * 2 + fc2
                tmv = 0
                for k in range(NEC):
                    tmv = b.op("tensor",
                         lambda e, sl=sl, k=k, fc2=fc2: e.matmul(
                             t["pb4"][:, 0:3], w1t[sl][:, k, fc2 * 128:(fc2 + 1) * 128],
                             h3s[:, k, :], start=(k == 0), stop=(k == NEC - 1)),
                         waits=([(c3, h3v), (c_tw, lv), (c_tdv, t4r)] if k == 0 else []),
                         inc=(c_tmm if k == NEC - 1 else None),
                         incs=(((c_twf, 1),) if (k == NEC - 1 and fc2 == 1) else ()))
                t4r = b.op("vector",
                     lambda e, fc=fc: e.tensor_scalar(tail6[:, fc, 0:3], t["pb4"][:, 0:3],
                                                      t["maskcols"][:, 0:1], None, ALU.mult),
                     waits=[(c_tmm, tmv)], inc=c_tdv)
        pv = b.val(c_tdv)
        for tcol in range(3):
            fv = b.op("vector", lambda e, tcol=tcol: e.tensor_mul(facc[:, :], tail6[:, :, tcol], convws[:, :, 0]),
                      waits=[(c_tdv, pv)], inc=c_tdv)
            for j in (1, 2, 3):
                self.chain("vector", lambda e, tcol=tcol, j=j: e.tensor_mul(fm2[:, :], tail6[:, :, tcol + j], convws[:, :, j]), c_tdv)
                self.chain("vector", lambda e: e.tensor_add(facc[:, :], facc[:, :], fm2[:, :]), c_tdv)
            self.chain("vector", lambda e: e.tensor_add(facc[:, :], facc[:, :], convbs[:, :]), c_tdv)
            fv = b.op("scalar", lambda e, tcol=tcol: e.activation(ff[:, :, tcol], facc[:, :], AF.Silu),
                      waits=[(c_tdv, b.val(c_tdv))], inc=c_tdv)
            b.noop("vector", waits=[(c_tdv, fv)], inc=c_tdv)
        fixv = b.val(c_tdv)
        for fc in range(48):
            b.dma("sync", d["hbc_d"][fc * 128:(fc + 1) * 128, 0:3], ff[:, fc, :], c_pat,
                  waits=[(c_tdv, fixv)])
        patv = b.val(c_pat)
        self.flush()
        pf.close()

        pc = ExitStack()

        def sc(n, s_, dt):
            return pc.enter_context(nc.sbuf_tensor(self.u(n), s_, dt))
        sel32f = sc("sel32f", [32, 32 * 128], F32)
        maskadds = sc("maskadds", [128, 2, 256], F32)
        arow = sc("arow", [32, 8, 256], F32)
        acum = sc("acum", [32, 8, 256], F32)
        nacum = sc("nacum", [32, 8, 256], F32)
        dtb = sc("dtb", [32, 8, 256], BF16)
        alast = sc("alast", [32, 8], F32)
        ealast = sc("ealast", [32, 8], F32)
        lamx = sc("lamx", [32, 8], F32)
        acumT = sc("acumT", [128, 16, 32], F32)
        nacumT = sc("nacumT", [128, 16, 32], F32)
        ealastbc = sc("ealastbc", [128, 32, 8], F32)
        b.dma("sync", sel32f[:, :], d["sel32"], "mcf")
        b.dma("sync", maskadds[:, :, :], d["maskadd"].rearrange("k p l -> p k l"), "mcf")
        cB = self.uc("p2b")
        wcf = b.val("mcf")
        b.op("vector", lambda e: e.tensor_scalar(arow[:, :, :], dts[:, :, :], acol[:, 0:1], None, ALU.mult),
             waits=[(cB, 0)], inc=cB)
        b.op("vector", lambda e: e.tensor_copy(dtb[:, :, :], dts[:, :, :]), inc=cB)
        for c in range(8):
            self.chain("vector", lambda e, c=c: e.tensor_tensor_scan(
                acum[:, c, :], arow[:, c, :], arow[:, c, :], 0.0, ALU.add, ALU.bypass), cB)
        self.chain("vector", lambda e: e.tensor_scalar(nacum[:, :, :], acum[:, :, :], -1.0, None, ALU.mult), cB)
        eav = b.op("scalar", lambda e: e.activation(eacumb[:, :, :], acum[:, :, :], AF.Exp),
                   waits=[(cB, b.val(cB))], inc=cB)
        self.chain("vector", lambda e: e.tensor_copy(alast[:, :], acum[:, :, 255]), cB)
        alv = b.op("scalar", lambda e: e.activation(ealast[:, :], alast[:, :], AF.Exp),
                   waits=[(cB, b.val(cB))], inc=cB)
        self.chain("vector", lambda e: e.tensor_tensor_scan(lamx[:, :], alast[:, :], alast[:, :],
                                                            0.0, ALU.add, ALU.bypass), cB)
        self.chain("vector", lambda e: e.tensor_sub(lamx[:, :], lamx[:, :], alast[:, :]), cB)
        lmv = b.op("scalar", lambda e: e.activation(lamx[:, :], lamx[:, :], AF.Exp),
                   waits=[(cB, b.val(cB))], inc=cB)
        for i in range(16):
            c = i // 2
            s0 = (i % 2) * 128
            tv = b.op("tensor", lambda e, c=c, s0=s0: e.transpose(
                     t["pb6"][:, 0:32], acum[:, c, s0:s0 + 128],
                     t["identf"][0:32, 0:32]),
                 waits=[(cB, b.val(cB))], inc=cB)
            self.chain("vector", lambda e, i=i: e.tensor_copy(acumT[:, i, :], t["pb6"][:, 0:32]), cB)
        self.chain("vector", lambda e: e.tensor_scalar(nacumT[:, :, :], acumT[:, :, :], -1.0, None, ALU.mult), cB)
        for h in range(NH):
            tv = b.op("tensor", lambda e, h=h: e.matmul(t["pb0"][:, 0:8], sel32f[:, h * 128:(h + 1) * 128],
                                                        ealast[:, :], start=True, stop=True),
                      waits=[(cB, b.val(cB)), ("mcf", wcf), (cB, alv)], inc=cB)
            self.chain("vector", lambda e, h=h: e.tensor_copy(ealastbc[:, h, :], t["pb0"][:, 0:8]), cB)
            tv = b.op("tensor", lambda e, h=h: e.matmul(t["pb1"][:, 0:8], sel32f[:, h * 128:(h + 1) * 128],
                                                        lamx[:, :], start=True, stop=True),
                      waits=[(cB, b.val(cB)), (cB, lmv)], inc=cB)
            self.chain("vector", lambda e, h=h: e.tensor_copy(lambc[:, h, :], t["pb1"][:, 0:8]), cB)
        b.barrier()
        if MK_TRUNC == "p2b":
            self.flush(); pc.close(); pm.close(); return

        XT = [sc("XT", [128, 2, 4096], BF16) for _ in range(2)]
        XDD = sc("XDD", [128, 2, 4096], BF16)
        BT = [sc("BT", [128, 2, 1024], BF16) for _ in range(2)]
        BF_ = [sc("BFm", [128, 8, 256], BF16) for _ in range(2)]
        CF = [sc("CFm", [128, 8, 256], BF16) for _ in range(2)]
        Ssb = sc("Ssb", [128, 32, 128], F32)
        SPb = [sc("SPb", [128, 128], BF16) for _ in range(2)]
        LTE = [sc("LTE", [128, 2, 256], F32) for _ in range(2)]
        MTb = [sc("MTb", [128, 2, 256], BF16) for _ in range(2)]
        Cscb = [sc("Cscb", [128, 256], BF16) for _ in range(2)]
        decf = sc("decf", [32, 256], F32)
        decb = sc("decb", [32, 256], BF16)
        yev = [sc("yev", [128, 256], BF16) for _ in range(2)]

        c_xl = self.uc("sxl")
        c_smm = self.uc("smm")
        c_sdv = self.uc("sdv")
        c_sac = self.uc("sac")
        c_sg = self.uc("ssg")
        c_yst = self.uc("syst")
        zmv = b.op("vector", lambda e: e.memset(Ssb[:, :, :], 0.0), inc=c_sdv)
        xf = [0, 0]
        bcf = [0, 0]
        relb = {f"pb{i}": None for i in range(8)}
        supd = {}
        lastym = [0, 0]
        lastex = [0, 0]
        loads = {}
        for c in range(9):
            if c < 8:
                sl = c % 2
                cs = c * CH
                w = [(c_smm, xf[sl]), (c_sdv, bcf[sl])] + ([(c_pat, patv)] if c == 0 else [])
                b.dma("scalar", XT[sl][:, 0, :], d["hbc_d"][0:IATT, cs:cs + 128], c_xl,
                      waits=w, transpose=True)
                b.dma("scalar", XT[sl][:, 1, :], d["hbc_d"][0:IATT, cs + 128:cs + 256], c_xl,
                      transpose=True)
                b.dma("scalar", BT[sl][:, 0, :], d["hbc_d"][IATT:IATT + 1024, cs:cs + 128], c_xl,
                      transpose=True)
                b.dma("scalar", BT[sl][:, 1, :], d["hbc_d"][IATT:IATT + 1024, cs + 128:cs + 256], c_xl,
                      transpose=True)
                b.dma("scalar", BF_[sl][:, :, :],
                      d["hbc_d"][IATT:IATT + 1024, cs:cs + 256].rearrange("(g p) s -> p g s", p=128), c_xl)
                b.dma("scalar", CF[sl][:, :, :],
                      d["hbc_d"][IATT + 1024:CD, cs:cs + 256].rearrange("(g p) s -> p g s", p=128), c_xl)
                loads[c] = b.val(c_xl)
            if c == 0:
                continue
            cc = c - 1
            sl2 = cc % 2
            ldv2 = loads[cc]
            dxv = 0
            for s2 in range(2):
                for n8 in range(8):
                    bank = "pb4" if (n8 % 2 == 0) else "pb5"
                    wb = [relb[bank]] if relb[bank] else []
                    mv = b.op("tensor", lambda e, cc=cc, s2=s2, n8=n8, bank=bank: e.matmul(
                             t[bank][:, 0:512], dtb[:, cc, s2 * 128:(s2 + 1) * 128],
                             rep32b[:, n8 * 512:(n8 + 1) * 512], start=True, stop=True),
                         waits=[(c_xl, ldv2)] + wb, inc=c_smm)
                    dxv = b.op("vector", lambda e, sl2=sl2, s2=s2, n8=n8, bank=bank: e.tensor_mul(
                             XT[sl2][:, s2, n8 * 512:(n8 + 1) * 512],
                             XT[sl2][:, s2, n8 * 512:(n8 + 1) * 512], t[bank][:, 0:512]),
                         waits=[(c_smm, mv)], inc=c_sdv)
                    relb[bank] = (c_sdv, dxv)
            dv = self.chain("vector", lambda e, cc=cc: e.tensor_scalar(
                     decf[:, :], nacum[:, cc, :], alast[:, cc:cc + 1], None, ALU.add), c_sdv)
            dv = b.op("scalar", lambda e: e.activation(decb[:, :], decf[:, :], AF.Exp),
                      waits=[(c_sdv, dv), (c_smm, b.val(c_smm))], inc=c_sac)
            ddv = 0
            xddgate = b.val(c_smm)
            for s2 in range(2):
                for n8 in range(8):
                    bank = "pb4" if (n8 % 2 == 0) else "pb5"
                    wb = [relb[bank]] if relb[bank] else []
                    mv = b.op("tensor", lambda e, s2=s2, n8=n8, bank=bank: e.matmul(
                             t[bank][:, 0:512], decb[:, s2 * 128:(s2 + 1) * 128],
                             rep32b[:, n8 * 512:(n8 + 1) * 512], start=True, stop=True),
                         waits=[(c_sac, dv)] + wb, inc=c_smm)
                    ddv = b.op("vector", lambda e, sl2=sl2, s2=s2, n8=n8, bank=bank: e.tensor_mul(
                             XDD[:, s2, n8 * 512:(n8 + 1) * 512],
                             XT[sl2][:, s2, n8 * 512:(n8 + 1) * 512], t[bank][:, 0:512]),
                         waits=[(c_smm, mv), (c_smm, xddgate)], inc=c_sdv)
                    relb[bank] = (c_sdv, ddv)
            for h in range(NH):
                g = h % 8
                par = h % 2
                bM = f"pb{par}"
                bY = f"pb{4 + par}"
                bC = f"pb{6 + par}"
                wb = [relb["pb2"]] if relb["pb2"] else []
                rb = b.op("tensor", lambda e, h=h, cc=cc: e.matmul(
                         t["pb2"][:, 0:256], sel32f[:, h * 128:(h + 1) * 128], acum[:, cc, :],
                         start=True, stop=True),
                     waits=wb, inc=c_smm)
                wb = [relb[bM]] if relb[bM] else []
                m1 = 0
                for s2 in range(2):
                    m1 = b.op("tensor", lambda e, h=h, g=g, s2=s2, sl2=sl2, bM=bM: e.matmul(
                             t[bM][:, s2 * 256:(s2 + 1) * 256], BF_[sl2][:, g, s2 * 128:(s2 + 1) * 128],
                             CF[sl2][:, g, :], start=(s2 == 0), stop=(s2 == 1)),
                         waits=(wb + [(c_xl, ldv2)] if s2 == 0 else []),
                         inc=(c_smm if s2 == 1 else None))
                wb = [relb[bC]] if relb[bC] else []
                cb = b.op("tensor", lambda e, h=h, cc=cc, bC=bC: e.matmul(
                         t[bC][:, 0:256], rep32b[:, h * 128:(h + 1) * 128], eacumb[:, cc, :],
                         start=True, stop=True),
                     waits=wb, inc=c_smm)
                wb = [relb["pb3"]] if relb["pb3"] else []
                sm = 0
                for s2 in range(2):
                    sm = b.op("tensor", lambda e, h=h, g=g, s2=s2, sl2=sl2: e.matmul(
                             t["pb3"][:, 0:128], BT[sl2][:, s2, g * 128:(g + 1) * 128],
                             XDD[:, s2, h * 128:(h + 1) * 128],
                             start=(s2 == 0), stop=(s2 == 1)),
                         waits=(wb + [(c_sdv, ddv)] if s2 == 0 else []), inc=(c_smm if s2 == 1 else None))
                mt = 0
                for s2 in range(2):
                    lt = b.op("vector", lambda e, h=h, cc=cc, s2=s2, par=par: e.scalar_tensor_tensor(
                             LTE[par][:, s2, :], t["pb2"][:, 0:256], nacumT[:, cc * 2 + s2, h:h + 1],
                             maskadds[:, s2, :], ALU.add, ALU.add),
                         waits=[(c_smm, rb), (c_sac, lastex[par])], inc=c_sdv)
                    if s2 == 1:
                        relb["pb2"] = (c_sdv, lt)
                    ex = b.op("scalar", lambda e, s2=s2, par=par: e.activation(
                              LTE[par][:, s2, :], LTE[par][:, s2, :], AF.Exp),
                              waits=[(c_sdv, lt)], inc=c_sac)
                    lastex[par] = ex
                    mt = b.op("vector", lambda e, s2=s2, par=par, bM=bM: e.tensor_mul(
                             MTb[par][:, s2, :], LTE[par][:, s2, :], t[bM][:, s2 * 256:(s2 + 1) * 256]),
                         waits=[(c_sac, ex), (c_smm, m1), (c_smm, lastym[par])], inc=c_sdv)
                relb[bM] = (c_sdv, mt)
                csc = b.op("vector", lambda e, g=g, sl2=sl2, par=par, bC=bC: e.tensor_mul(
                         Cscb[par][:, :], CF[sl2][:, g, :], t[bC][:, 0:256]),
                     waits=[(c_smm, cb), (c_smm, lastym[par])], inc=c_sdv)
                relb[bC] = (c_sdv, csc)
                spv = b.op("gpsimd", lambda e, h=h, par=par: e.tensor_copy(SPb[par][:, :], Ssb[:, h, :]),
                     waits=[(c_sdv, supd.get(h, zmv)), (c_smm, lastym[par])], inc=c_sg)
                wb = [relb[bY]] if relb[bY] else []
                ym = 0
                for s2 in range(2):
                    ym = b.op("tensor", lambda e, h=h, s2=s2, sl2=sl2, par=par, bY=bY: e.matmul(
                             t[bY][:, 0:256], XT[sl2][:, s2, h * 128:(h + 1) * 128], MTb[par][:, s2, :],
                             start=(s2 == 0), stop=False),
                         waits=(wb + [(c_sdv, mt), (c_sdv, dxv)] if s2 == 0 else []),
                         inc=None)
                ym = b.op("tensor", lambda e, h=h, par=par, bY=bY: e.matmul(
                         t[bY][:, 0:256], SPb[par][:, :], Cscb[par][:, :], start=False, stop=True),
                     waits=[(c_sdv, csc), (c_sg, spv)], inc=c_smm)
                lastym[par] = ym
                if h == NH - 1:
                    xf[sl2] = ym
                sup = b.op("vector", lambda e, h=h, cc=cc: e.scalar_tensor_tensor(
                         Ssb[:, h, :], Ssb[:, h, :], ealastbc[:, h, cc:cc + 1], t["pb3"][:, 0:128],
                         ALU.mult, ALU.add),
                     waits=[(c_smm, sm), (c_sg, spv)], inc=c_sdv)
                supd[h] = sup
                relb["pb3"] = (c_sdv, sup)
                if h == NH - 1:
                    bcf[sl2] = sup
                ya = b.op("scalar", lambda e, h=h, par=par, bY=bY: e.activation(
                         yev[par][:, :], t[bY][:, 0:256], AF.Copy),
                     waits=[(c_smm, ym), (c_yst, b.val(c_yst) - 16 if h >= 2 else 0)], inc=c_sac)
                relb[bY] = (c_sac, ya)
                b.dma("sync", d["y_d"][h * 128:(h + 1) * 128, cc * CH:(cc + 1) * CH], yev[par][:, :],
                      c_yst, waits=[(c_sac, ya)])
        sv = self.slot_write(lambda i: d["shst"][i], Ssb[:, :, :], "mshs",
                             waits=[(c_sdv, b.val(c_sdv))])
        self.slot_write(lambda i: d["shflag"][i, 1][0:1, 0:1], t["tokens"][0:1, 0:1], "mshf",
                        waits=[("mshs", sv)])
        if MK_TRUNC == "p2c":
            b.barrier(); self.flush(); pc.close(); pm.close(); return
        pdn2, pdv2 = self.poll_flag(1)
        rv2 = self.slot_read("gpsimd", sinb[:, :, :], lambda i: d["shst"][i], "msin",
                             waits=[(pdn2, pdv2)])
        c_si = self.uc("msk")
        b.op("vector", lambda e: e.tensor_scalar(sinb[:, :, :], sinb[:, :, :], t["maskcols"][:, 0:1],
                                                 None, ALU.mult),
             waits=[("msin", rv2)], inc=c_si)
        b.barrier()
        self.flush()
        pc.close()
        if MK_TRUNC == "p2d":
            pm.close(); return

        p3 = ExitStack()

        def s3(n, s_, dt):
            return p3.enter_context(nc.sbuf_tensor(self.u(n), s_, dt))
        ph3 = {
            "rtile": [s3("3r", [128, TH], F32) for _ in range(2)],
            "sqt": [s3("3q", [128, TH], BF16) for _ in range(2)],
            "rowa": s3("3ra", [1, TH], F32),
            "rowb": s3("3rb", [1, TH], F32),
            "rowc": s3("3rc", [1, TH], F32),
            "rstdbc": s3("3rs", [128, TH], BF16),
            "musbc": s3("3mu", [128, TH], BF16),
            "lntmp": s3("3lt", [128, TH], BF16),
        }
        hbuf3 = s3("3h", [128, NEC, TH], BF16)
        gbuf = s3("3g", [128, 32, TH], BF16)
        zw = [s3("3zw", [128, 16, 128], BF16) for _ in range(2)]
        wo = [s3("3wo", [128, 32, 128], BF16) for _ in range(2)]
        hs2 = [s3("3hs", [128, TH], BF16) for _ in range(2)]
        ytl = [s3("3yt", [128, TH], BF16) for _ in range(2)]
        ctl = [s3("3ca", [128, TH], BF16) for _ in range(2)]
        silz = s3("3sz", [128, 256], BF16)
        sqg = [s3("3sq", [128, 512], BF16) for _ in range(2)]
        yt2 = s3("3y2", [128, 256], F32)
        SPc = s3("3sp", [128, 128], BF16)
        Cs3 = s3("3cs", [128, 256], BF16)
        rrow = s3("3rr", [1, 512], F32)
        rbc3 = [s3("3rb2", [128, 512], F32) for _ in range(2)]

        c_zl = self.uc("z3l"); c_zlf = self.uc("z3f")
        c_hl = self.uc("h3l")
        c_yl = self.uc("y3l")
        c_3m = self.uc("m3m")
        c_3v = self.uc("m3v")
        c_3a = self.uc("m3a")
        c_gn = self.uc("m3g")
        c_ol = self.uc("o3l"); c_olf = self.uc("o3f")
        c_or = self.uc("o3r")
        c_ost = self.uc("o3s")
        nzl = 0
        nol = 0
        for half in range(2):
            t0 = half * TH
            c_h3, h3vv = self.emit_ln(src_d, half, hbuf3, ph3)

            relz = 0
            relc = 0
            rel1 = 0
            rstat = 0
            rbcr = [0, 0]
            for grp in range(8):
                for hh in range(4):
                    h = grp * 4 + hh
                    g = h % 8
                    s = nzl % 2
                    wz = [(c_zlf, nzl - 1)] if nzl >= 2 else []
                    zlv = b.dma("gpsimd", zw[s][:, :, :],
                                d["w_ip"][:, 6176 + h * 128:6176 + (h + 1) * 128].rearrange("(c p) f -> p c f", p=128),
                                c_zl, waits=wz)
                    nzl += 1
                    hlv = b.dma("sync", hs2[s][:, :], d["hbc_d"][h * 128:(h + 1) * 128, t0:t0 + TH], c_hl,
                                waits=([(c_3v, b.val(c_3v) - 2)] if nzl >= 3 else []))
                    calv = b.dma("sync", ctl[s][:, :],
                                 d["hbc_d"][IATT + 1024 + g * 128:IATT + 1024 + (g + 1) * 128, t0:t0 + TH],
                                 c_hl, waits=([(c_3v, b.val(c_3v) - 2)] if nzl >= 3 else []))
                    ylv = b.dma("sync", ytl[s][:, :], d["y_d"][h * 128:(h + 1) * 128, t0:t0 + TH], c_yl,
                                waits=([(c_3v, b.val(c_3v) - 2)] if nzl >= 3 else []))
                    for c2 in range(4):
                        cglob = half * 4 + c2
                        zm = 0
                        for k in range(NEC):
                            zm = b.op("tensor", lambda e, s=s, k=k, c2=c2: e.matmul(
                                     t["pb0"][:, 0:256], zw[s][:, k, :],
                                     hbuf3[:, k, c2 * 256:(c2 + 1) * 256],
                                     start=(k == 0), stop=(k == NEC - 1)),
                                 waits=([(c_zl, zlv), (c_h3, h3vv), (c_3a, relz)] if k == 0 else []),
                                 inc=(c_3m if k == NEC - 1 else None),
                                 incs=(((c_zlf, 1),) if (k == NEC - 1 and c2 == 3) else ()))
                        ebm = b.op("tensor", lambda e, h=h, cglob=cglob: e.matmul(
                                 t["pb6"][:, 0:256], rep32b[:, h * 128:(h + 1) * 128], eacumb[:, cglob, :],
                                 start=True, stop=True),
                             waits=[(c_3v, relc)], inc=c_3m)
                        cs3 = b.op("vector", lambda e, s=s, c2=c2: e.tensor_mul(
                                 Cs3[:, :], ctl[s][:, c2 * 256:(c2 + 1) * 256], t["pb6"][:, 0:256]),
                             waits=[(c_3m, ebm), (c_hl, calv)], inc=c_3v)
                        relc = cs3
                        spv = b.op("vector", lambda e, h=h, cglob=cglob: e.tensor_scalar(
                                 SPc[:, :], sinb[:, h, :], lambc[:, h, cglob:cglob + 1], None, ALU.mult),
                             waits=[(c_si, 1)], inc=c_3v)
                        ym = b.op("tensor", lambda e: e.matmul(t["pb1"][:, 0:256], SPc[:, :], Cs3[:, :],
                                                               start=True, stop=True),
                             waits=[(c_3v, spv), (c_3v, cs3), (c_3v, rel1)],
                             inc=c_3m)
                        sz = b.op("scalar", lambda e: e.activation(silz[:, :], t["pb0"][:, 0:256], AF.Silu),
                             waits=[(c_3m, zm)], inc=c_3a)
                        relz = sz
                        rel1 = b.op("vector", lambda e, c2=c2, s=s: e.tensor_add(yt2[:, :],
                                                                          ytl[s][:, c2 * 256:(c2 + 1) * 256],
                                                                          t["pb1"][:, 0:256]),
                             waits=[(c_3m, ym), (c_yl, ylv)], inc=c_3v)
                        y2 = b.op("vector", lambda e, h=h, c2=c2, s=s: e.scalar_tensor_tensor(
                                 yt2[:, :], hs2[s][:, c2 * 256:(c2 + 1) * 256], dbc[:, h:h + 1],
                                 yt2[:, :], ALU.mult, ALU.add),
                             waits=[(c_hl, hlv)], inc=c_3v)
                        gv = b.op("vector", lambda e, h=h, c2=c2: e.tensor_mul(
                                 gbuf[:, h, c2 * 256:(c2 + 1) * 256], yt2[:, :], silz[:, :]),
                             waits=[(c_3a, sz)], inc=c_3v)
                for tc in range(2):
                    smv = 0
                    for hh in range(4):
                        h = grp * 4 + hh
                        qv = b.op("scalar", lambda e, h=h, tc=tc, hh=hh: e.activation(
                                 sqg[hh % 2][:, :], gbuf[:, h, tc * 512:(tc + 1) * 512], AF.Square),
                             waits=[(c_3v, b.val(c_3v))], inc=c_3a)
                        smv = b.op("tensor", lambda e, hh=hh: e.matmul(
                                 t["pb2"][0:1, 0:512], t["onesb"][:, :], sqg[hh % 2][:, :],
                                 start=(hh == 0), stop=(hh == 3)),
                             waits=[(c_3a, qv), (c_3v, rstat)], inc=(c_3m if hh == 3 else None))
                    rv3 = b.op("vector", lambda e: e.tensor_scalar(rrow[0:1, :], t["pb2"][0:1, 0:512],
                                                                   1.0 / 512, RMSEPS, ALU.mult, ALU.add),
                         waits=[(c_3m, smv)], inc=c_3v)
                    rstat = rv3
                    rv3 = b.op("scalar", lambda e: e.activation(rrow[0:1, :], rrow[0:1, :], AF.Sqrt),
                         waits=[(c_3v, rv3)], inc=c_3a)
                    rv3 = b.op("vector", lambda e: e.reciprocal(rrow[0:1, :], rrow[0:1, :]),
                         waits=[(c_3a, rv3)], inc=c_3v)
                    bm = b.op("tensor", lambda e: e.matmul(t["pb3"][:, 0:512], t["ones1f"][:, :], rrow[0:1, :],
                                                           start=True, stop=True),
                         waits=[(c_3v, rv3), (c_3v, rbcr[tc])], inc=c_3m)
                    rbv = b.op("vector", lambda e, tc=tc: e.tensor_copy(rbc3[tc][:, :], t["pb3"][:, 0:512]),
                         waits=[(c_3m, bm)], inc=c_3v)
                    for hh in range(4):
                        h = grp * 4 + hh
                        rbcr[tc] = b.op("vector", lambda e, h=h, tc=tc: e.scalar_tensor_tensor(
                                 gbuf[:, h, tc * 512:(tc + 1) * 512], gbuf[:, h, tc * 512:(tc + 1) * 512],
                                 nws[:, h:h + 1], rbc3[tc][:, :], ALU.mult, ALU.mult),
                             waits=[(c_3v, rbv)], inc=c_3v, incs=(((c_gn, 1),) if hh == 3 and tc == 1 else ()))
            gnv = b.val(c_gn)
            for fb in range(16):
                s = nol % 2
                wz = [(c_olf, nol - 1)] if nol >= 2 else []
                olv = b.dma("gpsimd", wo[s][:, :, :],
                            d["w_op"][:, fb * 128:(fb + 1) * 128].rearrange("(c p) f -> p c f", p=128),
                            c_ol, waits=wz)
                nol += 1
                rlv = b.dma("sync", ph3["rtile"][s][:, :], src_d[fb * 128:(fb + 1) * 128, t0:t0 + TH],
                            c_hl, waits=([(c_ost, b.val(c_ost) - 16)] if fb >= 2 else []))
                adv = 0
                for tc in range(2):
                    om = 0
                    for k in range(32):
                        om = b.op("tensor", lambda e, s=s, k=k, tc=tc: e.matmul(
                                 t[f"pb{4 + tc}"][:, 0:512], wo[s][:, k, :],
                                 gbuf[:, k, tc * 512:(tc + 1) * 512],
                                 start=(k == 0), stop=(k == 31)),
                             waits=([(c_ol, olv), (c_gn, gnv), (c_or, b.val(c_or) - 1 if fb >= 1 else 0)] if k == 0 else []),
                             inc=(c_3m if k == 31 else None),
                             incs=(((c_olf, 1),) if (k == 31 and tc == 1) else ()))
                    adv = b.op("vector", lambda e, s=s, tc=tc: e.tensor_add(
                             ph3["rtile"][s][:, tc * 512:(tc + 1) * 512],
                             ph3["rtile"][s][:, tc * 512:(tc + 1) * 512], t[f"pb{4 + tc}"][:, 0:512]),
                         waits=[(c_3m, om), (c_hl, rlv)], inc=c_or)
                b.dma("sync", dst_d[fb * 128:(fb + 1) * 128, t0:t0 + TH], ph3["rtile"][s][:, :], c_ost,
                      waits=[(c_or, adv)])
        b.noop("sync", waits=[(c_ost, b.val(c_ost))])
        b.barrier()
        self.flush()
        p3.close()
        pm.close()



def build_nc(stages=("ffn1", "mamba", "ffn2")):
    nc = bass.Bass("TRN2", target_bir_lowering=False, debug=False)
    with ExitStack() as es:
        p = Prog(nc, es, stages)
        p.declare_dram()
        p.alloc_global()
        p.block = es.enter_context(nc.Block())
        p.emit_preamble()
        if "ffn1" in stages:
            dst = p.d["r1_d"] if ("mamba" in stages or "ffn2" in stages) else p.d["out_fm"]
            p.emit_ffn(p.d["x_fm"], dst, capture_tail=("mamba" in stages))
        if "mamba" in stages:
            dst = p.d["r2_d"] if "ffn2" in stages else p.d["out_fm"]
            p.emit_mamba(p.d["r1_d"], dst)
        if "ffn2" in stages:
            src = p.d["r2_d"] if "mamba" in stages else (p.d["r1_d"] if "ffn1" in stages else p.d["x_fm"])
            p.emit_ffn(src, p.d["out_fm"], capture_tail=False)
    return nc


def kernel(**inputs):
    maps = _prep_core_inputs(inputs)
    nc = build_nc()
    res = run_bass_kernel_spmd(nc, maps, core_ids=list(range(8)))
    out = np.empty((4, 4096, 2048), np.float32)
    for c in range(8):
        bidx, half = c // 2, c % 2
        out[bidx, half * T:(half + 1) * T, :] = res.results[c]["out_fm"].T
    return out
